# version 01
# baseline (speedup 1.0000x reference)
import numpy as np
import ml_dtypes
from contextlib import ExitStack

import concourse.bass as bass
import concourse.mybir as mybir
from concourse.bass_utils import run_bass_kernel_spmd

F32 = mybir.dt.float32
BF16 = mybir.dt.bfloat16
I32 = mybir.dt.int32
AF = mybir.ActivationFunctionType
ALU = mybir.AluOpType
AXL = mybir.AxisListType

B = 2
T = 8192
D = 1024
NH = 16
HD = 64
HPC = 4
TPC = 2048
CHUNK = 128
NCHUNK = T // CHUNK


class Sem:
    def __init__(self, h):
        self.h = h
        self.c = 0


class Sched:
    def __init__(self, nc, es):
        self.nc = nc
        self.q = {e: [] for e in ("pe", "act", "dve", "pool", "sync")}
        self.sem = {n: Sem(es.enter_context(nc.semaphore(f"s_{n}")))
                    for n in ("ld", "ld2", "lx0", "lx1", "lst", "st1", "pe", "act", "dve", "pool", "sync", "st")}

    def op(self, eng, fn, waits=(), inc=None, dma=False):
        s = self.sem[inc] if inc else None
        n = 16 if dma else 1
        val = None
        if s is not None:
            s.c += n
            val = s.c
        waits = tuple((self.sem[w], v) for (w, v) in waits)

        def g(e, fn=fn, waits=waits, s=s, n=n):
            for (ss, v) in waits:
                e.wait_ge(ss.h, v)
            ins = fn(e)
            if s is not None:
                ins.then_inc(s.h, n)
        self.q[eng].append(g)
        return val

    def c(self, name):
        return self.sem[name].c

    def emit(self, block):
        @block.tensor
        def _(eng):
            for f in self.q["pe"]:
                f(eng)

        @block.scalar
        def _(eng):
            for f in self.q["act"]:
                f(eng)

        @block.vector
        def _(eng):
            for f in self.q["dve"]:
                f(eng)

        @block.gpsimd
        def _(eng):
            for f in self.q["pool"]:
                f(eng)

        @block.sync
        def _(eng):
            for f in self.q["sync"]:
                f(eng)


def make_ident(sc, ident):
    v = sc.op("pool", lambda g: g.memset(ident[:], 1.0), inc="pool")
    return sc.op("pool", lambda g: g.affine_select(
        out=ident[:], in_=ident[:], pattern=[[-1, 128]],
        compare_op=ALU.is_equal, fill=0.0, base=0, channel_multiplier=1),
        waits=[("pool", v)], inc="pool")


def make_triu(sc, triu):
    v = sc.op("pool", lambda g: g.memset(triu[:], 1.0), inc="pool")
    return sc.op("pool", lambda g: g.affine_select(
        out=triu[:], in_=triu[:], pattern=[[1, 128]],
        compare_op=ALU.is_ge, fill=0.0, base=0, channel_multiplier=-1),
        waits=[("pool", v)], inc="pool")


def build_A(debug=False):
    nc = bass.Bass("TRN2", target_bir_lowering=False)
    xfm = nc.dram_tensor("xfm", [D, T], F32, kind="ExternalInput").ap()
    wT = {w: nc.dram_tensor(f"w{w}T", [D, HPC * HD], F32, kind="ExternalInput").ap()
          for w in ("q", "k", "v")}
    bqkv = nc.dram_tensor("bqkv", [3, HPC * HD], F32, kind="ExternalInput").ap()
    yfm_d = nc.dram_tensor("yfm", [HPC * HD, T], BF16, kind="ExternalOutput").ap()
    if debug:
        dbg = {n: nc.dram_tensor(f"dbg_{n}", sh, BF16, kind="ExternalOutput").ap()
               for n, sh in (("q", [128, 2 * T]), ("k", [128, 2 * T]),
                             ("kt", [128, HPC * NCHUNK * HD]), ("vt", [128, HPC * NCHUNK * 65]))}

    TT = 512
    NTT = T // TT
    es = ExitStack()
    with es:
        def sb(name, shape, dt=F32):
            return es.enter_context(nc.sbuf_tensor(name, shape, dt))

        def psum(name, shape, dt=F32):
            return es.enter_context(nc.psum_tensor(name, shape, dt))

        w_sb = {w: sb(f"w{w}", [128, 8 * 256], BF16) for w in ("q", "k", "v")}
        bias_sb = sb("bias", [128, 6], F32)
        x_t = [sb(f"x_t{i}", [128, 8 * TT], BF16) for i in range(2)]
        q_fm = sb("q_fm", [128, 2 * T], BF16)
        k_fm = sb("k_fm", [128, 2 * T], BF16)
        kt = sb("kt", [128, HPC * NCHUNK * HD], BF16)
        vt = sb("vt", [128, HPC * NCHUNK * 65], BF16)
        v_stage = sb("v_stage", [128, TT], BF16)
        phi_m = sb("phi_m", [128, TT], F32)
        phi_e = sb("phi_e", [128, TT], F32)
        phi_r = sb("phi_r", [128, TT], F32)
        masked = [sb(f"masked{i}", [128, 128], BF16) for i in range(2)]
        triu = sb("triu", [128, 128], BF16)
        ident = sb("ident", [128, 128], BF16)
        S_arena = sb("S_arena", [128, 2 * 65], BF16)
        ones1 = sb("ones1", [1, 64], F32)
        rden = sb("rden", [1, 128], F32)
        bcast_sb = sb("bcast_sb", [64, 128], F32)
        y_fm = sb("y_fm", [128, 2 * T], BF16)

        pp = [psum(f"pp{i}", [128, TT]) for i in range(6)]
        ptr = psum("ptr", [128, 128], BF16)
        psc_bank = psum("psc", [128, 512])
        psc = [psc_bank[:, 0:128], psc_bank[:, 128:256]]
        pnum = [pp[0][0:65, 0:128], pp[0][0:65, 128:256]]
        pS_h = [pp[1], pp[3], pp[4], pp[5]]
        pbc = pp[2][0:64, 0:128]

        sc = Sched(nc, es)
        block = es.enter_context(nc.Block())

        make_ident(sc, ident)
        make_triu(sc, triu)
        sc.op("pool", lambda g: g.memset(ones1[:], 1.0), inc="pool")
        sc.op("pool", lambda g: g.memset(vt[:], 1.0), inc="pool")
        sc.op("pool", lambda g: g.memset(S_arena[:], 0.0), inc="pool")
        init_done = sc.c("pool")

        for w in ("q", "k", "v"):
            for j in range(8):
                sc.op("pool", lambda g, w=w, j=j: g.dma_start(
                    out=w_sb[w][:, j * 256:(j + 1) * 256],
                    in_=wT[w][j * 128:(j + 1) * 128, :]), inc="ld", dma=True)
        for wi in range(3):
            for p in range(2):
                sc.op("pool", lambda g, wi=wi, p=p: g.dma_start(
                    out=bias_sb[:, wi * 2 + p: wi * 2 + p + 1],
                    in_=bqkv[wi:wi + 1, p * 128:(p + 1) * 128].rearrange("a e -> e a")),
                    inc="ld", dma=True)
        ld_w = sc.c("ld")

        pe_tile_done = {}
        dve_tile_done = {}
        for ti in range(NTT):
            buf = ti % 2
            waits = [("pe", pe_tile_done[ti - 2])] if ti >= 2 else []
            xv = sc.op("pool", lambda g, buf=buf, ti=ti: g.dma_start(
                out=x_t[buf][:].rearrange("p (j t) -> p j t", j=8),
                in_=xfm[:, ti * TT:(ti + 1) * TT].rearrange("(j p) t -> p j t", p=128)),
                waits=waits, inc=f"lx{buf}", dma=True)

            for wi, w in enumerate(("q", "k", "v")):
                for p in range(2):
                    bank = pp[wi * 2 + p]
                    bcol = bias_sb[:, wi * 2 + p: wi * 2 + p + 1]
                    for j in range(8):
                        wts = []
                        if j == 0:
                            wts = [("ld", ld_w), (f"lx{buf}", xv)]
                            if ti > 0:
                                wts.append(("dve", dve_tile_done[ti - 1]))
                        mm = sc.op("pe", lambda t, bank=bank, w=w, j=j, p=p, buf=buf: t.matmul(
                            bank[:],
                            w_sb[w][:, j * 256 + p * 128: j * 256 + (p + 1) * 128],
                            x_t[buf][:, j * TT:(j + 1) * TT],
                            start=(j == 0), stop=(j == 7)),
                            waits=wts, inc="pe" if j == 7 else None)
                    if w in ("q", "k"):
                        dst = q_fm if w == "q" else k_fm
                        dslc = dst[:, p * T + ti * TT: p * T + (ti + 1) * TT]
                        v1 = sc.op("dve", lambda v, bank=bank, bcol=bcol: v.tensor_scalar(
                            out=phi_m[:], in0=bank[:], scalar1=bcol, scalar2=0.0,
                            op0=ALU.add, op1=ALU.min),
                            waits=[("pe", mm), ("ld", ld_w)], inc="dve")
                        a1 = sc.op("act", lambda a: a.activation(
                            out=phi_e[:], in_=phi_m[:], func=AF.Exp),
                            waits=[("dve", v1)], inc="act")
                        sc.op("dve", lambda v, bank=bank, bcol=bcol: v.tensor_scalar(
                            out=phi_r[:], in0=bank[:], scalar1=bcol, scalar2=0.0,
                            op0=ALU.add, op1=ALU.max), inc="dve")
                        v3 = sc.op("dve", lambda v, dslc=dslc: v.tensor_add(
                            out=dslc, in0=phi_e[:], in1=phi_r[:]),
                            waits=[("act", a1)], inc="dve")
                        if w == "k":
                            for sti in range(4):
                                c = ti * 4 + sti
                                src = dst[:, p * T + c * 128:p * T + (c + 1) * 128]
                                tv = sc.op("pe", lambda t, src=src: t.transpose(
                                    ptr[:], src, ident[:]),
                                    waits=[("dve", sc.c("dve")), ("pool", init_done)],
                                    inc="pe")
                                for hh in range(2):
                                    h = 2 * p + hh
                                    sc.op("dve", lambda v, hh=hh, h=h, c=c: v.tensor_copy(
                                        out=kt[:, (h * NCHUNK + c) * 64:(h * NCHUNK + c) * 64 + 64],
                                        in_=ptr[:, hh * 64:(hh + 1) * 64]),
                                        waits=[("pe", tv)], inc="dve")
                    else:
                        vv = sc.op("dve", lambda v, bank=bank, bcol=bcol: v.tensor_scalar_add(
                            out=v_stage[:], in0=bank[:], scalar1=bcol),
                            waits=[("pe", mm), ("ld", ld_w)], inc="dve")
                        for sti in range(4):
                            c = ti * 4 + sti
                            tv = sc.op("pe", lambda t, sti=sti: t.transpose(
                                ptr[:], v_stage[:, sti * 128:(sti + 1) * 128], ident[:]),
                                waits=[("dve", sc.c("dve")), ("pool", init_done)],
                                inc="pe")
                            for hh in range(2):
                                h = 2 * p + hh
                                sc.op("dve", lambda v, hh=hh, h=h, c=c: v.tensor_copy(
                                    out=vt[:, (h * NCHUNK + c) * 65:(h * NCHUNK + c) * 65 + 64],
                                    in_=ptr[:, hh * 64:(hh + 1) * 64]),
                                    waits=[("pe", tv)], inc="dve")
            pe_tile_done[ti] = sc.c("pe")
            dve_tile_done[ti] = sc.c("dve")
        phase1_dve = sc.c("dve")
        phase1_pe = sc.c("pe")

        prev = {}
        NJ = NCHUNK * HPC
        mask_done = [0] * NJ
        num1_done = [0] * NJ
        supd_done = [0] * NJ
        ymult_done = [0] * NJ
        recip_done = [0] * NJ
        sevac_done = [0] * NJ
        bc_done = [0] * NJ
        bcevac_done = [0] * NJ
        for i in range(NCHUNK):
            for h in range(HPC):
                j = i * HPC + h
                par = h % 2
                pair = h // 2
                buf = j % 2
                rowoff = par * 64
                qk_off = pair * T + i * 128
                Sap = S_arena[rowoff:rowoff + 64, pair * 65: pair * 65 + 65]
                pSap = pS_h[h][rowoff:rowoff + 64, 0:65]
                vslc = vt[:, (h * NCHUNK + i) * 65:(h * NCHUNK + i) * 65 + 65]
                kslc = kt[:, (h * NCHUNK + i) * 64:(h * NCHUNK + i) * 64 + 64]

                if j > 0:
                    bcw = [("dve", recip_done[j - 1])]
                    if j >= 2:
                        bcw.append(("act", bcevac_done[j - 2]))
                    bc_done[j - 1] = sc.op("pe", lambda t: t.matmul(
                        pbc, ones1[:], rden[:], start=True, stop=True),
                        waits=bcw, inc="pe")
                wts = [("dve", phase1_dve)]
                if j >= 2:
                    wts.append(("dve", mask_done[j - 2]))
                scd = sc.op("pe", lambda t, rowoff=rowoff, qk_off=qk_off, buf=buf: t.matmul(
                    psc[buf][:],
                    k_fm[rowoff:rowoff + 64, qk_off:qk_off + 128],
                    q_fm[rowoff:rowoff + 64, qk_off:qk_off + 128],
                    start=True, stop=True, tile_position=(rowoff, 0)),
                    waits=wts, inc="pe")
                mwts = [("pe", scd)]
                if j >= 2:
                    mwts.append(("pe", num1_done[j - 2]))
                mask_done[j] = sc.op("dve", lambda v, buf=buf: v.tensor_mul(
                    out=masked[buf][:], in0=psc[buf][:], in1=triu[:]),
                    waits=mwts, inc="dve")
                nwts = [("dve", mask_done[j])]
                if j >= 2:
                    nwts.append(("dve", ymult_done[j - 2]))
                num1_done[j] = sc.op("pe", lambda t, buf=buf, vslc=vslc: t.matmul(
                    pnum[buf], vslc, masked[buf][:], start=True, stop=False),
                    waits=nwts, inc="pe")
                n2wts = []
                if j >= HPC:
                    n2wts = [("act", sevac_done[j - HPC])]
                sc.op("pe", lambda t, buf=buf, Sap=Sap, rowoff=rowoff, qk_off=qk_off: t.matmul(
                    pnum[buf], Sap,
                    q_fm[rowoff:rowoff + 64, qk_off:qk_off + 128],
                    start=False, stop=True, tile_position=(rowoff, 0)),
                    waits=n2wts, inc="pe")
                supd_done[j] = sc.op("pe", lambda t, pSap=pSap, kslc=kslc, vslc=vslc, i=i, rowoff=rowoff: t.matmul(
                    pSap, kslc, vslc, start=(i == 0), stop=(i == NCHUNK - 1),
                    tile_position=(0, rowoff)),
                    inc="pe")
                sevac_done[j] = sc.op("act", lambda a, Sap=Sap, pSap=pSap: a.copy(
                    out=Sap, in_=pSap),
                    waits=[("pe", supd_done[j])], inc="act")
                rwts = [("pe", supd_done[j])]
                if j > 0:
                    rwts.append(("pe", bc_done[j - 1]))
                recip_done[j] = sc.op("dve", lambda v, buf=buf: v.reciprocal(
                    out=rden[:], in_=pnum[buf][64:65, 0:128]),
                    waits=rwts, inc="dve")
                if j > 0:
                    bwts = [("pe", bc_done[j - 1])]
                    if j >= 2:
                        bwts.append(("dve", ymult_done[j - 2]))
                    bcevac_done[j - 1] = sc.op("act", lambda a: a.copy(
                        out=bcast_sb[:], in_=pbc),
                        waits=bwts, inc="act")
                    jj = j - 1
                    pj = (jj // HPC, jj % HPC)
                    ii, hh = pj
                    ydst = y_fm[(hh % 2) * 64:(hh % 2) * 64 + 64,
                                (hh // 2) * T + ii * 128:(hh // 2) * T + (ii + 1) * 128]
                    ymult_done[jj] = sc.op("dve", lambda v, buf2=jj % 2, ydst=ydst: v.tensor_mul(
                        out=ydst, in0=pnum[buf2][0:64, 0:128], in1=bcast_sb[:]),
                        waits=[("act", bcevac_done[jj])], inc="dve")
        j = NJ
        bc_done_last = sc.op("pe", lambda t: t.matmul(
            pbc, ones1[:], rden[:], start=True, stop=True),
            waits=[("dve", recip_done[NJ - 1])], inc="pe")
        bclast = sc.op("act", lambda a: a.copy(out=bcast_sb[:], in_=pbc),
                       waits=[("pe", bc_done_last), ("dve", ymult_done[NJ - 2])], inc="act")
        jj = NJ - 1
        ii, hh = jj // HPC, jj % HPC
        ydst = y_fm[(hh % 2) * 64:(hh % 2) * 64 + 64,
                    (hh // 2) * T + ii * 128:(hh // 2) * T + (ii + 1) * 128]
        yl = sc.op("dve", lambda v: v.tensor_mul(
            out=ydst, in0=pnum[jj % 2][0:64, 0:128], in1=bcast_sb[:]),
            waits=[("act", bclast)], inc="dve")

        for p in range(2):
            sc.op("sync", lambda e, p=p: e.dma_start(
                out=yfm_d[p * 128:(p + 1) * 128, :], in_=y_fm[:, p * T:(p + 1) * T]),
                waits=[("dve", yl)], inc="st", dma=True)
        if debug:
            for n, t in (("q", q_fm), ("k", k_fm), ("kt", kt), ("vt", vt)):
                sc.op("sync", lambda e, n=n, t=t: e.dma_start(
                    out=dbg[n], in_=t[:]), waits=[("dve", yl)], inc="st", dma=True)
        sc.op("sync", lambda e: None, waits=[("st", sc.c("st"))])

        sc.emit(block)
    return nc




def build_B(debug=False):
    nc = bass.Bass("TRN2", target_bir_lowering=False)
    yfm = nc.dram_tensor("yfm_sl", [D, TPC], BF16, kind="ExternalInput").ap()
    x_sl = nc.dram_tensor("x_sl", [TPC, D], F32, kind="ExternalInput").ap()
    woT_d = nc.dram_tensor("woT", [D, D], F32, kind="ExternalInput").ap()
    bo_d = nc.dram_tensor("bo", [1, D], F32, kind="ExternalInput").ap()
    wgT_d = nc.dram_tensor("wgT", [D, 3072], I32, kind="ExternalInput").ap()
    wuT_d = nc.dram_tensor("wuT", [D, 3072], I32, kind="ExternalInput").ap()
    wdT_d = nc.dram_tensor("wdT", [3072, D], I32, kind="ExternalInput").ap()
    x2o = nc.dram_tensor("x2o", [TPC, D], F32, kind="ExternalOutput").ap()
    if debug:
        x1o = nc.dram_tensor("x1o", [TPC, D], F32, kind="ExternalOutput").ap()
    x1_dram = nc.dram_tensor("x1_scr", [TPC, D], F32).ap()
    x1fm_dram = nc.dram_tensor("x1fm_scr", [D, TPC], BF16).ap()

    es = ExitStack()
    with es:
        def sb(name, shape, dt=F32):
            return es.enter_context(nc.sbuf_tensor(name, shape, dt))

        def psum(name, shape, dt=F32):
            return es.enter_context(nc.psum_tensor(name, shape, dt))

        yfm_t = sb("yfm_t", [128, 8 * 128], BF16)
        woT_sb = sb("woT_sb", [128, 8 * D], BF16)
        bo_bc = sb("bo_bc", [128, D], F32)
        xres = sb("xres", [128, D], F32)
        pre = sb("pre", [128, D], F32)
        mean = sb("mean", [128, 1], F32)
        nmean = sb("nmean", [128, 1], F32)
        var = sb("var", [128, 1], F32)
        rstd = sb("rstd", [128, 1], F32)
        xout = sb("xout", [128, D], F32)
        xout_bf = sb("xout_bf", [128, D], BF16)
        ident = sb("ident", [128, 128], BF16)
        ones1 = sb("ones1", [1, 128], F32)
        x1fm_t = sb("x1fm_t", [128, 8 * 128], BF16)
        wg_sb = sb("wg_sb", [128, 8 * 3072], BF16)
        wu_sb = sb("wu_sb", [128, 8 * 3072], BF16)
        wd_sb = sb("wd_sb", [128, 24 * D], BF16)
        stage_i = sb("stage_i", [128, 512], I32)
        h_fm = sb("h_fm", [128, 24 * 256], BF16)
        silu_t = sb("silu_t", [128, 256], F32)
        x1fm_blk = sb("x1fm_blk", [128, 8 * 256], BF16)

        pb = [psum(f"pb{i}", [128, 512]) for i in range(6)]
        ptr = psum("ptr", [128, 128], BF16)

        sc = Sched(nc, es)
        block = es.enter_context(nc.Block())

        make_ident(sc, ident)
        sc.op("pool", lambda g: g.memset(ones1[:], 1.0), inc="pool")
        init_done = sc.c("pool")

        for j in range(8):
            sc.op("pool", lambda g, j=j: g.dma_start(
                out=woT_sb[:, j * D:(j + 1) * D],
                in_=woT_d[j * 128:(j + 1) * 128, :]), inc="ld", dma=True)
        ld_w = sc.c("ld")

        bo_sb = sb("bo_sb", [1, D], F32)
        bv = sc.op("pool", lambda g: g.dma_start(out=bo_sb[:], in_=bo_d), inc="ld", dma=True)
        for oj in range(2):
            mm = sc.op("pe", lambda t, oj=oj: t.matmul(
                pb[4][:, 0:512], ones1[:], bo_sb[:, oj * 512:(oj + 1) * 512],
                start=True, stop=True),
                waits=[("ld", bv), ("pool", init_done)] + ([("act", sc.c("act"))] if oj else []),
                inc="pe")
            sc.op("act", lambda a, oj=oj: a.copy(
                out=bo_bc[:, oj * 512:(oj + 1) * 512], in_=pb[4][:, 0:512]),
                waits=[("pe", mm)], inc="act")
        bo_done = sc.c("act")

        NT1 = TPC // 128
        for tt in range(NT1):
            xv = sc.op("sync", lambda e, tt=tt: e.dma_start(
                out=xres[:], in_=x_sl[tt * 128:(tt + 1) * 128, :]),
                waits=([("dve", sc.c("dve"))] if tt else []), inc="ld2", dma=True)
            yv = sc.op("sync", lambda e, tt=tt: e.dma_start(
                out=yfm_t[:].rearrange("p (j t) -> p j t", j=8),
                in_=yfm[:, tt * 128:(tt + 1) * 128].rearrange("(j p) t -> p j t", p=128)),
                waits=([("pe", sc.c("pe"))] if tt else []), inc="ld2", dma=True)
            mms = []
            for oj in range(2):
                for j in range(8):
                    w = []
                    if j == 0:
                        w = [("ld", ld_w), ("ld2", max(xv, yv))]
                        if tt > 0:
                            w.append(("dve", sc.c("dve")))
                    mm = sc.op("pe", lambda t, oj=oj, j=j: t.matmul(
                        pb[oj][:],
                        yfm_t[:, j * 128:(j + 1) * 128],
                        woT_sb[:, j * D + oj * 512: j * D + (oj + 1) * 512],
                        start=(j == 0), stop=(j == 7)),
                        waits=w, inc="pe" if j == 7 else None)
                mms.append(mm)
            for oj in range(2):
                sc.op("dve", lambda v, oj=oj: v.tensor_add(
                    out=pre[:, oj * 512:(oj + 1) * 512], in0=pb[oj][:],
                    in1=bo_bc[:, oj * 512:(oj + 1) * 512]),
                    waits=[("pe", mms[oj]), ("act", bo_done)], inc="dve")
            v1 = sc.op("dve", lambda v: v.tensor_add(out=pre[:], in0=pre[:], in1=xres[:]),
                       waits=[("ld2", max(xv, yv))], inc="dve")
            v2 = sc.op("dve", lambda v: v.reduce_sum(out=mean[:, 0:1], in_=pre[:], axis=AXL.X),
                       inc="dve")
            v3 = sc.op("dve", lambda v: v.tensor_scalar_mul(
                out=nmean[:, 0:1], in0=mean[:, 0:1], scalar1=-1.0 / D),
                waits=[("dve", v2)], inc="dve")
            a1 = sc.op("act", lambda a: a.activation(
                out=xout[:], in_=pre[:], func=AF.Square, bias=nmean[:, 0:1], scale=1.0,
                accum_out=var[:, 0:1]),
                waits=[("dve", v3), ("st1", sc.c("st1")), ("st", sc.c("st"))], inc="act")
            v4 = sc.op("dve", lambda v: v.tensor_scalar(
                out=var[:, 0:1], in0=var[:, 0:1], scalar1=1.0 / D, scalar2=1e-5,
                op0=ALU.mult, op1=ALU.add),
                waits=[("act", a1)], inc="dve")
            a2 = sc.op("act", lambda a: a.activation(
                out=rstd[:, 0:1], in_=var[:, 0:1], func=AF.Sqrt),
                waits=[("dve", v4)], inc="act")
            v5 = sc.op("dve", lambda v: v.reciprocal(out=rstd[:, 0:1], in_=rstd[:, 0:1]),
                       waits=[("act", a2)], inc="dve")
            v6 = sc.op("dve", lambda v: v.tensor_scalar(
                out=xout[:], in0=pre[:], scalar1=nmean[:, 0:1], scalar2=rstd[:, 0:1],
                op0=ALU.add, op1=ALU.mult),
                waits=[("dve", v5)], inc="dve")
            v7 = sc.op("dve", lambda v: v.tensor_copy(out=xout_bf[:], in_=xout[:]),
                       inc="dve")
            for sj in range(8):
                tv = sc.op("pe", lambda t, sj=sj: t.transpose(
                    ptr[:], xout_bf[:, sj * 128:(sj + 1) * 128], ident[:]),
                    waits=[("dve", v7), ("pool", init_done)] +
                          ([("dve", sc.c("dve"))] if sj else []), inc="pe")
                sc.op("dve", lambda v, sj=sj: v.tensor_copy(
                    out=x1fm_t[:, sj * 128:(sj + 1) * 128], in_=ptr[:]),
                    waits=[("pe", tv)], inc="dve")
            vlast = sc.c("dve")
            sc.op("sync", lambda e, tt=tt: e.dma_start(
                out=x1_dram[tt * 128:(tt + 1) * 128, :], in_=xout[:]),
                waits=[("dve", vlast)], inc="st1", dma=True)
            sc.op("sync", lambda e, tt=tt: e.dma_start(
                out=x1fm_dram[:, tt * 128:(tt + 1) * 128].rearrange("(j p) t -> p j t", p=128),
                in_=x1fm_t[:].rearrange("p (j t) -> p j t", j=8)),
                waits=[("dve", vlast)], inc="st1", dma=True)
            if debug:
                sc.op("sync", lambda e, tt=tt: e.dma_start(
                    out=x1o[tt * 128:(tt + 1) * 128, :], in_=xout[:]),
                    waits=[("dve", vlast)], inc="st", dma=True)
        b1_st = sc.c("st1")

        for (wsb, wdr, nch, wid) in ((wg_sb, wgT_d, 8, 3072), (wu_sb, wuT_d, 8, 3072),
                                     (wd_sb, wdT_d, 24, D)):
            for j in range(nch):
                for cc2 in range(wid // 512):
                    lv = sc.op("pool", lambda g, wdr=wdr, j=j, cc2=cc2: g.dma_start(
                        out=stage_i[:], in_=wdr[j * 128:(j + 1) * 128, cc2 * 512:(cc2 + 1) * 512]),
                        waits=[("dve", sc.c("dve"))], inc="lst", dma=True)
                    sc.op("dve", lambda v, wsb=wsb, j=j, wid=wid, cc2=cc2: v.tensor_scalar_mul(
                        out=wsb[:, j * wid + cc2 * 512: j * wid + (cc2 + 1) * 512], in0=stage_i[:],
                        scalar1=1.0 / 127.0),
                        waits=[("lst", lv)], inc="dve")
        dq_done = sc.c("dve")

        NB = TPC // 256
        for bb in range(NB):
            xv = sc.op("sync", lambda e, bb=bb: e.dma_start(
                out=x1fm_blk[:].rearrange("p (j t) -> p j t", j=8),
                in_=x1fm_dram[:, bb * 256:(bb + 1) * 256].rearrange("(j p) t -> p j t", p=128)),
                waits=[("st1", b1_st)] + ([("dve", sc.c("dve"))] if bb else []),
                inc="ld2", dma=True)
            for hj in range(24):
                gmm = umm = None
                for j in range(8):
                    w = []
                    if j == 0:
                        w = [("ld2", xv), ("dve", dq_done)]
                        if hj >= 2:
                            w.append(("dve", sc.c("dve")))
                    gmm = sc.op("pe", lambda t, hj=hj, j=j: t.matmul(
                        pb[hj % 2][:, 0:256],
                        wg_sb[:, j * 3072 + hj * 128: j * 3072 + (hj + 1) * 128],
                        x1fm_blk[:, j * 256:(j + 1) * 256],
                        start=(j == 0), stop=(j == 7)),
                        waits=w, inc="pe" if j == 7 else None)
                for j in range(8):
                    umm = sc.op("pe", lambda t, hj=hj, j=j: t.matmul(
                        pb[2 + hj % 2][:, 0:256],
                        wu_sb[:, j * 3072 + hj * 128: j * 3072 + (hj + 1) * 128],
                        x1fm_blk[:, j * 256:(j + 1) * 256],
                        start=(j == 0), stop=(j == 7)),
                        inc="pe" if j == 7 else None)
                a1 = sc.op("act", lambda a, hj=hj: a.activation(
                    out=silu_t[:], in_=pb[hj % 2][:, 0:256], func=AF.Silu),
                    waits=[("pe", gmm)] + ([("dve", sc.c("dve"))] if hj >= 1 else []),
                    inc="act")
                sc.op("dve", lambda v, hj=hj: v.tensor_mul(
                    out=h_fm[:, hj * 256:(hj + 1) * 256], in0=pb[2 + hj % 2][:, 0:256],
                    in1=silu_t[:]),
                    waits=[("act", a1), ("pe", umm)], inc="dve")
            hdone = sc.c("dve")
            for st_i in range(2):
                tt = bb * 2 + st_i
                xv2 = sc.op("sync", lambda e, tt=tt: e.dma_start(
                    out=xres[:], in_=x1_dram[tt * 128:(tt + 1) * 128, :]),
                    waits=[("st1", b1_st), ("dve", sc.c("dve"))], inc="ld2", dma=True)
                dmm = []
                for oj in range(2):
                    for hj in range(24):
                        w = []
                        if hj == 0:
                            w = [("dve", hdone)]
                            w.append(("dve", sc.c("dve")))
                        mm = sc.op("pe", lambda t, oj=oj, hj=hj, st_i=st_i: t.matmul(
                            pb[4 + oj][:],
                            h_fm[:, hj * 256 + st_i * 128: hj * 256 + (st_i + 1) * 128],
                            wd_sb[:, hj * D + oj * 512: hj * D + (oj + 1) * 512],
                            start=(hj == 0), stop=(hj == 23)),
                            waits=w, inc="pe" if hj == 23 else None)
                    dmm.append(mm)
                for oj in range(2):
                    sc.op("dve", lambda v, oj=oj: v.tensor_add(
                        out=pre[:, oj * 512:(oj + 1) * 512], in0=pb[4 + oj][:],
                        in1=xres[:, oj * 512:(oj + 1) * 512]),
                        waits=[("pe", dmm[oj]), ("ld2", xv2)], inc="dve")
                v2 = sc.op("dve", lambda v: v.reduce_sum(out=mean[:, 0:1], in_=pre[:], axis=AXL.X),
                           inc="dve")
                v3 = sc.op("dve", lambda v: v.tensor_scalar_mul(
                    out=nmean[:, 0:1], in0=mean[:, 0:1], scalar1=-1.0 / D),
                    waits=[("dve", v2)], inc="dve")
                a1 = sc.op("act", lambda a: a.activation(
                    out=xout[:], in_=pre[:], func=AF.Square, bias=nmean[:, 0:1], scale=1.0,
                    accum_out=var[:, 0:1]),
                    waits=[("dve", v3), ("st", sc.c("st"))], inc="act")
                v4 = sc.op("dve", lambda v: v.tensor_scalar(
                    out=var[:, 0:1], in0=var[:, 0:1], scalar1=1.0 / D, scalar2=1e-5,
                    op0=ALU.mult, op1=ALU.add),
                    waits=[("act", a1)], inc="dve")
                a2 = sc.op("act", lambda a: a.activation(
                    out=rstd[:, 0:1], in_=var[:, 0:1], func=AF.Sqrt),
                    waits=[("dve", v4)], inc="act")
                v5 = sc.op("dve", lambda v: v.reciprocal(out=rstd[:, 0:1], in_=rstd[:, 0:1]),
                           waits=[("act", a2)], inc="dve")
                v6 = sc.op("dve", lambda v: v.tensor_scalar(
                    out=xout[:], in0=pre[:], scalar1=nmean[:, 0:1], scalar2=rstd[:, 0:1],
                    op0=ALU.add, op1=ALU.mult),
                    waits=[("dve", v5)], inc="dve")
                sc.op("sync", lambda e, tt=tt: e.dma_start(
                    out=x2o[tt * 128:(tt + 1) * 128, :], in_=xout[:]),
                    waits=[("dve", v6)], inc="st", dma=True)
        sc.op("sync", lambda e: None, waits=[("st", sc.c("st"))])
        sc.emit(block)
    return nc




def build_C(sign=1.0):
    nc = bass.Bass("TRN2", target_bir_lowering=False)
    xfm = nc.dram_tensor("x2fm", [D, T], F32, kind="ExternalInput").ap()
    wC = {w: nc.dram_tensor(f"w{w}C", [D, HPC * HD], F32, kind="ExternalInput").ap()
          for w in ("q", "k", "v")}
    bqkv = nc.dram_tensor("bqkvC", [3, HPC * HD], F32, kind="ExternalInput").ap()
    gain_d = nc.dram_tensor("gain", [1, 1], F32, kind="ExternalInput").ap()
    yfm_d = nc.dram_tensor("yfm", [HPC * HD, T], BF16, kind="ExternalOutput").ap()

    TT = 512
    NTT = T // TT
    NQS = T // 128
    NQT = T // 512
    es = ExitStack()
    with es:
        def sb(name, shape, dt=F32):
            return es.enter_context(nc.sbuf_tensor(name, shape, dt))

        w_sb = {w: sb(f"w{w}", [128, 8 * 256], BF16) for w in ("q", "k", "v")}
        bias_sb = sb("bias", [128, 6], F32)
        x_t = [sb(f"x_t{i}", [128, 8 * TT], BF16) for i in range(2)]
        q_fm = sb("q_fm", [128, 2 * T], BF16)
        k_fm = sb("k_fm", [128, 2 * T], BF16)
        vt = sb("vt", [128, HPC * NCHUNK * 65], BF16)
        v_stage = sb("v_stage", [128, TT], F32)
        ident = sb("ident", [128, 128], BF16)
        identf = sb("identf", [128, 128], F32)
        onesc = sb("onesc", [65, 128], BF16)
        onesf = sb("onesf", [1, 128], F32)
        ksum = sb("ksum", [128, 2], F32)
        ksumb = sb("ksumb", [128, 2], BF16)
        negmu = [sb(f"negmu{h}", [128, NQS]) for h in range(HPC)]
        absum = [sb(f"absum{h}", [128, NQS]) for h in range(HPC)]
        epsc = [sb(f"epsc{h}", [128, NQS]) for h in range(HPC)]
        acc8 = [sb(f"acc8_{i}", [128, 8], F32) for i in range(2)]
        uabs = sb("uabs", [128, 1024], BF16)
        gaincol = sb("gaincol", [128, 1], F32)
        g4inv = sb("g4inv", [128, 1], F32)
        scr = sb("scr", [128, 2], F32)
        madt = sb("madt", [128, NQS], F32)
        m2t = sb("m2t", [128, NQS], F32)
        u_t = [sb(f"u_t{i}", [128, 512], BF16) for i in range(4)]
        w_t = [sb(f"w_t{i}", [128, 512], BF16) for i in range(4)]
        r_t = [sb(f"r_t{i}", [128, 512], BF16) for i in range(4)]
        dmask = [sb(f"dmask{i}", [128, 512], BF16) for i in range(4)]
        negmurow = [sb(f"negmurow{i}", [1, 512], BF16) for i in range(2)]
        epsrow = [sb(f"epsrow{i}", [1, 512], F32) for i in range(2)]
        denrow = [sb(f"denrow{i}", [1, 512], F32) for i in range(2)]
        rden = [sb(f"rdenC{i}", [1, 512], F32) for i in range(2)]
        bcast_sb = [sb(f"bcastC{i}", [64, 512], F32) for i in range(2)]
        y_fm = sb("y_fmC", [128, 2 * T], BF16)

        pa = es.enter_context(nc.psum_tensor("pa", [128, 2048], F32))
        pa2 = es.enter_context(nc.psum_tensor("pa2", [128, 2048], F32))
        ptrf = pa2[:, 1536:1664]
        pproj = [pa[:, i * 512:(i + 1) * 512] for i in range(4)] + \
                [pa2[:, i * 512:(i + 1) * 512] for i in range(2)]
        psF = {0: [pa[:, 0:1024], pa[:, 1024:2048]],
               1: [pa2[:, 0:1024], pa2[:, 1024:2048]]}
        psc = {0: [pa[:, 0:512], pa[:, 512:1024]],
               1: [pa[:, 1024:1536], pa[:, 1536:2048]]}
        py = {0: pa2[0:65, 0:512], 1: pa2[0:65, 512:1024]}
        prow = {0: pa2[:, 1024:1536], 1: pa2[:, 1536:2048]}
        pmu = prow[0]

        sc = Sched(nc, es)
        block = es.enter_context(nc.Block())

        make_ident(sc, ident)
        v = sc.op("pool", lambda g: g.memset(identf[:], 1.0), inc="pool")
        sc.op("pool", lambda g: g.affine_select(
            out=identf[:], in_=identf[:], pattern=[[-1, 128]],
            compare_op=ALU.is_equal, fill=0.0, base=0, channel_multiplier=1),
            waits=[("pool", v)], inc="pool")
        sc.op("pool", lambda g: g.memset(onesc[:], 1.0), inc="pool")
        sc.op("pool", lambda g: g.memset(onesf[:], 1.0), inc="pool")
        sc.op("pool", lambda g: g.memset(vt[:], 1.0), inc="pool")
        for j in range(4):
            v = sc.op("pool", lambda g, j=j: g.memset(dmask[j][:], 1.0), inc="pool")
            sc.op("pool", lambda g, j=j: g.affine_select(
                out=dmask[j][:], in_=dmask[j][:], pattern=[[1, 512]],
                compare_op=ALU.is_ge, fill=0.0, base=-128 * j, channel_multiplier=-1),
                waits=[("pool", v)], inc="pool")
        init_done = sc.c("pool")

        for w in ("q", "k", "v"):
            for j in range(8):
                sc.op("pool", lambda g, w=w, j=j: g.dma_start(
                    out=w_sb[w][:, j * 256:(j + 1) * 256],
                    in_=wC[w][j * 128:(j + 1) * 128, :]), inc="ld", dma=True)
        for wi in range(3):
            for p in range(2):
                sc.op("pool", lambda g, wi=wi, p=p: g.dma_start(
                    out=bias_sb[:, wi * 2 + p: wi * 2 + p + 1],
                    in_=bqkv[wi:wi + 1, p * 128:(p + 1) * 128].rearrange("a e -> e a")),
                    inc="ld", dma=True)
        gv = sc.op("pool", lambda g: g.dma_start(out=scr[0:1, 0:1], in_=gain_d),
                   inc="ld", dma=True)
        ld_w = sc.c("ld")

        gmm = sc.op("pe", lambda t: t.matmul(
            pmu[:, 0:1], onesf[:], scr[0:1, 0:1], start=True, stop=True),
            waits=[("ld", gv), ("pool", init_done)], inc="pe")
        a_g = sc.op("act", lambda a: a.copy(out=gaincol[:, 0:1], in_=pmu[:, 0:1]),
                    waits=[("pe", gmm)], inc="act")
        v1 = sc.op("dve", lambda v: v.tensor_mul(out=scr[:, 0:1], in0=gaincol[:, 0:1],
                                                 in1=gaincol[:, 0:1]),
                   waits=[("act", a_g)], inc="dve")
        v2 = sc.op("dve", lambda v: v.tensor_mul(out=scr[:, 1:2], in0=scr[:, 0:1],
                                                 in1=scr[:, 0:1]),
                   waits=[("dve", v1)], inc="dve")
        g4_done = sc.op("dve", lambda v: v.reciprocal(out=g4inv[:, 0:1], in_=scr[:, 1:2]),
                        waits=[("dve", v2)], inc="dve")

        pe_tile_done = {}
        dve_tile_done = {}
        for ti in range(NTT):
            buf = ti % 2
            waits = [("pe", pe_tile_done[ti - 2])] if ti >= 2 else []
            xv = sc.op("pool", lambda g, buf=buf, ti=ti: g.dma_start(
                out=x_t[buf][:].rearrange("p (j t) -> p j t", j=8),
                in_=xfm[:, ti * TT:(ti + 1) * TT].rearrange("(j p) t -> p j t", p=128)),
                waits=waits, inc=f"lx{buf}", dma=True)
            for wi, w in enumerate(("q", "k", "v")):
                for p in range(2):
                    bank = pproj[wi * 2 + p]
                    bcol = bias_sb[:, wi * 2 + p: wi * 2 + p + 1]
                    for j in range(8):
                        wts = []
                        if j == 0:
                            wts = [("ld", ld_w), (f"lx{buf}", xv)]
                            if ti > 0:
                                wts.append(("dve", dve_tile_done[ti - 1]))
                            if ti == 0 and wi == 1 and p == 0:
                                pass
                        mm = sc.op("pe", lambda t, bank=bank, w=w, j=j, p=p, buf=buf: t.matmul(
                            bank, w_sb[w][:, j * 256 + p * 128: j * 256 + (p + 1) * 128],
                            x_t[buf][:, j * TT:(j + 1) * TT],
                            start=(j == 0), stop=(j == 7)),
                            waits=wts, inc="pe" if j == 7 else None)
                    if w == "q":
                        dst = q_fm[:, p * T + ti * TT: p * T + (ti + 1) * TT]
                        sc.op("dve", lambda v, bank=bank, bcol=bcol, dst=dst: v.tensor_scalar(
                            out=dst, in0=bank, scalar1=bcol, scalar2=float(sign),
                            op0=ALU.add, op1=ALU.mult),
                            waits=[("pe", mm), ("ld", ld_w)], inc="dve")
                    elif w == "k":
                        dst = k_fm[:, p * T + ti * TT: p * T + (ti + 1) * TT]
                        sc.op("dve", lambda v, bank=bank, bcol=bcol, dst=dst: v.tensor_scalar_add(
                            out=dst, in0=bank, scalar1=bcol),
                            waits=[("pe", mm), ("ld", ld_w)], inc="dve")
                    else:
                        sc.op("dve", lambda v, bank=bank, bcol=bcol: v.tensor_scalar_add(
                            out=v_stage[:], in0=bank, scalar1=bcol),
                            waits=[("pe", mm), ("ld", ld_w)], inc="dve")
                        for sti in range(4):
                            c = ti * 4 + sti
                            tv = sc.op("pe", lambda t, sti=sti: t.transpose(
                                ptrf, v_stage[:, sti * 128:(sti + 1) * 128], identf[:]),
                                waits=[("dve", sc.c("dve")), ("pool", init_done)], inc="pe")
                            for hh in range(2):
                                h = 2 * p + hh
                                sc.op("dve", lambda v, hh=hh, h=h, c=c: v.tensor_copy(
                                    out=vt[:, (h * NCHUNK + c) * 65:(h * NCHUNK + c) * 65 + 64],
                                    in_=ptrf[:, hh * 64:(hh + 1) * 64]),
                                    waits=[("pe", tv)], inc="dve")
            pe_tile_done[ti] = sc.c("pe")
            dve_tile_done[ti] = sc.c("dve")
        proj_dve = sc.c("dve")

        kv = {}
        for p in range(2):
            r0 = sc.op("dve", lambda v, p=p: v.reduce_sum(
                out=ksum[:, p:p + 1], in_=k_fm[:, p * T:(p + 1) * T], axis=AXL.X),
                waits=[("dve", proj_dve)], inc="dve")
            kv[p] = sc.op("dve", lambda v, p=p: v.tensor_copy(
                out=ksumb[:, p:p + 1], in_=ksum[:, p:p + 1]),
                waits=[("dve", r0)], inc="dve")

        mu_done = {}
        for h in range(HPC):
            par, pair = h % 2, h // 2
            ro = par * 64
            for qs in range(NQS):
                w = []
                if qs == 0:
                    w = [("dve", kv[pair]), ("act", a_g)]
                    if h > 0:
                        w.append(("dve", mu_done[h - 1]))
                mm = sc.op("pe", lambda t, ro=ro, pair=pair, qs=qs: t.matmul(
                    pmu[:, qs:qs + 1],
                    q_fm[ro:ro + 64, pair * T + qs * 128: pair * T + (qs + 1) * 128],
                    ksumb[ro:ro + 64, pair:pair + 1],
                    start=True, stop=True, tile_position=(ro, 0)),
                    waits=w, inc="pe" if qs == NQS - 1 else None)
            sc.op("dve", lambda v, h=h: v.tensor_scalar_mul(
                out=negmu[h][:], in0=pmu[:, 0:NQS], scalar1=-1.0 / T),
                waits=[("pe", mm)], inc="dve")
            mu_done[h] = sc.c("dve")
        mu_all = sc.c("dve")

        eps_done = {}
        bank_last_act = {}
        for pair in range(2):
            hA, hB = 2 * pair, 2 * pair + 1
            prev_reduce = None
            for qs in range(NQS):
                acts = {}
                for kq in range(8):
                    half = kq % 2
                    for a_i, (h, ro) in enumerate(((hA, 0), (hB, 64))):
                        for kk in range(2):
                            kt0 = (kq * 2 + kk) * 512
                            w = []
                            if kk == 0:
                                bk = (a_i, half)
                                if bk in bank_last_act:
                                    w.append(("act", bank_last_act[bk]))
                                if pair == 0 and qs == 0 and kq < 2:
                                    w.append(("dve", mu_all))
                            mm = sc.op("pe", lambda t, ro=ro, pair=pair, qs=qs, kt0=kt0,
                                       a_i=a_i, half=half, kk=kk: t.matmul(
                                psF[a_i][half][:, kk * 512:(kk + 1) * 512],
                                q_fm[ro:ro + 64, pair * T + qs * 128: pair * T + (qs + 1) * 128],
                                k_fm[ro:ro + 64, pair * T + kt0: pair * T + kt0 + 512],
                                start=True, stop=True, tile_position=(ro, 0)),
                                waits=w, inc="pe" if kk == 1 else None)
                        h_ = hA if a_i == 0 else hB
                        aw = [("pe", mm)]
                        if prev_reduce is not None and kq == 0:
                            aw.append(("dve", prev_reduce))
                        bank_last_act[(a_i, half)] = sc.op(
                            "act", lambda a, a_i=a_i, half=half, h_=h_, qs=qs, kq=kq: a.activation(
                                out=uabs[:], in_=psF[a_i][half],
                                func=AF.Abs, bias=negmu[h_][:, qs:qs + 1], scale=1.0,
                                accum_out=acc8[a_i][:, kq:kq + 1]),
                            waits=aw, inc="act")
                rs = None
                for a_i, h_ in ((0, hA), (1, hB)):
                    rs = sc.op("dve", lambda v, a_i=a_i, h_=h_, qs=qs: v.reduce_sum(
                        out=absum[h_][:, qs:qs + 1], in_=acc8[a_i][:], axis=AXL.X),
                        waits=[("act", max(bank_last_act[(a_i, 0)], bank_last_act[(a_i, 1)]))],
                        inc="dve")
                prev_reduce = rs
            for h_ in (hA, hB):
                w1 = sc.op("dve", lambda v, h_=h_: v.tensor_scalar(
                    out=madt[:], in0=absum[h_][:], scalar1=1.0 / T, scalar2=1e-6,
                    op0=ALU.mult, op1=ALU.add), waits=[("dve", prev_reduce)], inc="dve")
                w2 = sc.op("dve", lambda v: v.tensor_mul(out=m2t[:], in0=madt[:], in1=madt[:]),
                           inc="dve")
                w3 = sc.op("dve", lambda v: v.tensor_mul(out=madt[:], in0=m2t[:], in1=m2t[:]),
                           inc="dve")
                eps_done[h_] = sc.op("dve", lambda v, h_=h_: v.tensor_scalar(
                    out=epsc[h_][:], in0=madt[:], scalar1=g4inv[:, 0:1], scalar2=1e-6,
                    op0=ALU.mult, op1=ALU.mult),
                    waits=[("dve", g4_done)], inc="dve")
        stats_done = sc.c("dve")
        stats_act = sc.c("act")

        ymult_done = 0
        prev_pair_last = {}
        relu_done = {}
        u_free = {}
        r_free = {}
        bc_free = {}
        prev_avmm = {}
        for pair in range(2):
            hA, hB = 2 * pair, 2 * pair + 1
            for qt in range(NQT):
                nrows = {}
                for a_i, h_ in ((0, hA), (1, hB)):
                    ro = (h_ % 2) * 64
                    for jj in range(4):
                        qs = qt * 4 + jj
                        w = [("dve", stats_done)] if jj == 0 else []
                        key = ("rowbank", a_i)
                        if jj == 0 and key in prev_pair_last:
                            w.append(("act", prev_pair_last[key]))
                        if jj == 0 and a_i in prev_avmm:
                            w.append(("pe", prev_avmm[a_i]))
                        tv = sc.op("pe", lambda t, a_i=a_i, h_=h_, qs=qs, jj=jj: t.transpose(
                            prow[a_i][0:1, jj * 128:(jj + 1) * 128],
                            negmu[h_][:, qs:qs + 1], identf[:]),
                            waits=w, inc="pe" if jj == 3 else None)
                    nrows[a_i] = sc.op("act", lambda a, a_i=a_i: a.copy(
                        out=negmurow[a_i][:], in_=prow[a_i][0:1, 0:512]),
                        waits=[("pe", tv)], inc="act")
                    prev_pair_last[("rowbank", a_i)] = nrows[a_i]
                kcmax = 4 * qt + 4
                avmm = {}
                for kc in range(kcmax):
                    for a_i, h_ in ((0, hA), (1, hB)):
                        ro = (h_ % 2) * 64
                        sbuf_i = a_i * 2 + kc % 2
                        bank = psc[a_i][kc % 2]
                        w = []
                        if kc >= 2:
                            w.append(("act", relu_done[(a_i, kc % 2)]))
                        else:
                            w.append(("act", nrows[a_i]))
                        smm = sc.op("pe", lambda t, bank=bank, ro=ro, pair=pair, kc=kc, qt=qt: t.matmul(
                            bank, k_fm[ro:ro + 64, pair * T + kc * 128: pair * T + (kc + 1) * 128],
                            q_fm[ro:ro + 64, pair * T + qt * 512: pair * T + (qt + 1) * 512],
                            start=True, stop=False, tile_position=(ro, 0)),
                            waits=w, inc="pe")
                        bmm = sc.op("pe", lambda t, bank=bank, a_i=a_i: t.matmul(
                            bank, onesc[0:1, :], negmurow[a_i][:],
                            start=False, stop=True, tile_position=(0, 0)),
                            inc="pe")
                        rv = sc.op("act", lambda a, bank=bank, sbuf_i=sbuf_i: a.activation(
                            out=u_t[sbuf_i][:], in_=bank, func=AF.Relu),
                            waits=[("pe", bmm)] +
                                  ([("dve", u_free[(a_i, kc % 2)])] if (a_i, kc % 2) in u_free else []),
                            inc="act")
                        relu_done[(a_i, kc % 2)] = rv
                        d1 = sc.op("dve", lambda v, sbuf_i=sbuf_i: v.tensor_mul(
                            out=w_t[sbuf_i][:], in0=u_t[sbuf_i][:], in1=u_t[sbuf_i][:]),
                            waits=[("act", rv)], inc="dve")
                        u_free[(a_i, kc % 2)] = d1
                        d2 = sc.op("dve", lambda v, sbuf_i=sbuf_i: v.tensor_mul(
                            out=r_t[sbuf_i][:], in0=w_t[sbuf_i][:], in1=w_t[sbuf_i][:]),
                            waits=([("pe", r_free[sbuf_i])] if sbuf_i in r_free else []),
                            inc="dve")
                        if kc >= 4 * qt:
                            d2 = sc.op("dve", lambda v, sbuf_i=sbuf_i, kc=kc, qt=qt: v.tensor_mul(
                                out=r_t[sbuf_i][:], in0=r_t[sbuf_i][:],
                                in1=dmask[kc - 4 * qt][:]),
                                inc="dve")
                        w = [("dve", d2)]
                        if kc == 0:
                            w.append(("dve", ymult_done))
                        avmm[a_i] = sc.op("pe", lambda t, a_i=a_i, h_=h_, kc=kc, sbuf_i=sbuf_i, kcmax=kcmax: t.matmul(
                            py[a_i], vt[:, (h_ * NCHUNK + kc) * 65:(h_ * NCHUNK + kc) * 65 + 65],
                            r_t[sbuf_i][:],
                            start=(kc == 0), stop=(kc == kcmax - 1)),
                            waits=w, inc="pe")
                        r_free[sbuf_i] = avmm[a_i]
                for a_i, h_ in ((0, hA), (1, hB)):
                    ro = (h_ % 2) * 64
                    for jj in range(4):
                        qs = qt * 4 + jj
                        w = [("pe", avmm[a_i])] if jj == 0 else []
                        tv = sc.op("pe", lambda t, a_i=a_i, h_=h_, qs=qs, jj=jj: t.transpose(
                            prow[a_i][0:1, jj * 128:(jj + 1) * 128],
                            epsc[h_][:, qs:qs + 1], identf[:]),
                            waits=w, inc="pe" if jj == 3 else None)
                    ev = sc.op("act", lambda a, a_i=a_i: a.copy(
                        out=epsrow[a_i][:], in_=prow[a_i][0:1, 0:512]),
                        waits=[("pe", tv)], inc="act")
                    d1 = sc.op("dve", lambda v, a_i=a_i: v.tensor_add(
                        out=denrow[a_i][:], in0=py[a_i][64:65, 0:512], in1=epsrow[a_i][:]),
                        waits=[("act", ev), ("pe", avmm[a_i])], inc="dve")
                    d2 = sc.op("dve", lambda v, a_i=a_i: v.reciprocal(
                        out=rden[a_i][:], in_=denrow[a_i][:]),
                        waits=[("dve", d1)], inc="dve")
                    bmm = sc.op("pe", lambda t, a_i=a_i: t.matmul(
                        prow[a_i][0:64, 0:512], onesf[0:1, 0:64], rden[a_i][:],
                        start=True, stop=True),
                        waits=[("dve", d2), ("act", ev)], inc="pe")
                    bev = sc.op("act", lambda a, a_i=a_i: a.copy(
                        out=bcast_sb[a_i][:], in_=prow[a_i][0:64, 0:512]),
                        waits=[("pe", bmm)] +
                              ([("dve", bc_free[a_i])] if a_i in bc_free else []),
                        inc="act")
                    prev_pair_last[("rowbank", a_i)] = bev
                    ydst = y_fm[ro:ro + 64, (h_ // 2) * T + qt * 512:(h_ // 2) * T + (qt + 1) * 512]
                    ym = sc.op("dve", lambda v, a_i=a_i, ydst=ydst: v.tensor_mul(
                        out=ydst, in0=py[a_i][0:64, 0:512], in1=bcast_sb[a_i][:]),
                        waits=[("act", bev)], inc="dve")
                    bc_free[a_i] = ym
                    ymult_done = ym
                prev_avmm = dict(avmm)

        yl = sc.c("dve")
        for p in range(2):
            sc.op("sync", lambda e, p=p: e.dma_start(
                out=yfm_d[p * 128:(p + 1) * 128, :], in_=y_fm[:, p * T:(p + 1) * T]),
                waits=[("dve", yl)], inc="st", dma=True)
        sc.op("sync", lambda e: None, waits=[("st", sc.c("st"))])

        sc.emit(block)
    return nc




def build_D(stage=4):
    nc = bass.Bass("TRN2", target_bir_lowering=False)
    yfm = nc.dram_tensor("y1fm_sl", [D, TPC], BF16, kind="ExternalInput").ap()
    x_sl = nc.dram_tensor("x2_sl", [TPC, D], F32, kind="ExternalInput").ap()
    wp_d = nc.dram_tensor("wproj", [D, D], F32, kind="ExternalInput").ap()
    bp_d = nc.dram_tensor("bproj", [1, D], F32, kind="ExternalInput").ap()
    wl_d = nc.dram_tensor("wLT", [D, D], F32, kind="ExternalInput").ap()
    bl_d = nc.dram_tensor("bL", [1, D], F32, kind="ExternalInput").ap()
    ldT_d = nc.dram_tensor("ldT", [100, 4], F32, kind="ExternalInput").ap()
    da_d = nc.dram_tensor("da", [100, D], F32, kind="ExternalInput").ap()
    luT_d = nc.dram_tensor("luT", [4, 50], F32, kind="ExternalInput").ap()
    uaT_d = nc.dram_tensor("uaT", [50, D], F32, kind="ExternalInput").ap()
    x4o = nc.dram_tensor("x4o", [TPC, D], F32, kind="ExternalOutput").ap()
    x3_dram = nc.dram_tensor("x3_scr", [TPC, D], F32).ap()
    x3fm_dram = nc.dram_tensor("x3fm_scr", [D, TPC], BF16).ap()

    es = ExitStack()
    with es:
        def sb(name, shape, dt=F32):
            return es.enter_context(nc.sbuf_tensor(name, shape, dt))

        yfm_t = sb("yfm_t", [128, 8 * 128], BF16)
        wp_sb = sb("wp_sb", [128, 8 * D], BF16)
        weff = sb("weff", [128, 8 * D], BF16)
        bp_bc = sb("bp_bc", [128, D], F32)
        bl_bc = sb("bl_bc", [128, D], F32)
        bp_sb = sb("bp_sb", [1, D], F32)
        bl_sb = sb("bl_sb", [1, D], F32)
        xres = sb("xres", [128, D], F32)
        pre = sb("pre", [128, D], F32)
        mean = sb("mean", [128, 1], F32)
        nmean = sb("nmean", [128, 1], F32)
        var = sb("var", [128, 1], F32)
        rstd = sb("rstd", [128, 1], F32)
        xout = sb("xout", [128, D], F32)
        xout_bf = sb("xout_bf", [128, D], BF16)
        ident = sb("ident", [128, 128], BF16)
        onesf = sb("onesf", [1, 128], F32)
        x3fm_t = sb("x3fm_t", [128, 8 * 128], BF16)
        ldT_sb = sb("ldT_sb", [100, 4], BF16)
        da_sb = sb("da_sb", [100, D], BF16)
        luT_sb = sb("luT_sb", [4, 50], BF16)
        uaT_sb = sb("uaT_sb", [50, D], BF16)
        a1_sb = sb("a1_sb", [4, D], BF16)
        a2_sb = sb("a2_sb", [50, D], BF16)

        pb = [es.enter_context(nc.psum_tensor(f"pbD{i}", [128, 512], F32)) for i in range(4)]
        ptr = es.enter_context(nc.psum_tensor("ptrD", [128, 128], BF16))

        sc = Sched(nc, es)
        block = es.enter_context(nc.Block())

        make_ident(sc, ident)
        sc.op("pool", lambda g: g.memset(onesf[:], 1.0), inc="pool")
        init_done = sc.c("pool")

        for j in range(8):
            sc.op("pool", lambda g, j=j: g.dma_start(
                out=wp_sb[:, j * D:(j + 1) * D], in_=wp_d[j * 128:(j + 1) * 128, :]),
                inc="ld", dma=True)
        for nm, dst, srcd in (("ldT", ldT_sb, ldT_d), ("da", da_sb, da_d),
                              ("luT", luT_sb, luT_d), ("uaT", uaT_sb, uaT_d)):
            sc.op("pool", lambda g, dst=dst, srcd=srcd: g.dma_start(out=dst[:], in_=srcd),
                  inc="ld", dma=True)
        bv = sc.op("pool", lambda g: g.dma_start(out=bp_sb[:], in_=bp_d), inc="ld", dma=True)
        bv = sc.op("pool", lambda g: g.dma_start(out=bl_sb[:], in_=bl_d), inc="ld", dma=True)
        ld_w = sc.c("ld")

        evs = []
        for (row, bc) in ((bp_sb, bp_bc), (bl_sb, bl_bc)):
            for oj in range(2):
                mm = sc.op("pe", lambda t, row=row, oj=oj: t.matmul(
                    pb[3][:], onesf[:], row[:, oj * 512:(oj + 1) * 512],
                    start=True, stop=True),
                    waits=[("ld", ld_w), ("pool", init_done)] +
                          ([("act", evs[-1])] if evs else []),
                    inc="pe")
                evs.append(sc.op("act", lambda a, bc=bc, oj=oj: a.copy(
                    out=bc[:, oj * 512:(oj + 1) * 512], in_=pb[3][:]),
                    waits=[("pe", mm)], inc="act"))
        bias_done = evs[-1]

        prev = bias_done
        for oj in range(2 if stage >= 0 else 0):
            mm = sc.op("pe", lambda t, oj=oj: t.matmul(
                pb[3][0:4, :], ldT_sb[:], da_sb[:, oj * 512:(oj + 1) * 512],
                start=True, stop=True),
                waits=[("act", prev)], inc="pe")
            prev = sc.op("act", lambda a, oj=oj: a.copy(
                out=a1_sb[:, oj * 512:(oj + 1) * 512], in_=pb[3][0:4, :]),
                waits=[("pe", mm)], inc="act")
        for oj in range(2 if stage >= 1 else 0):
            mm = sc.op("pe", lambda t, oj=oj: t.matmul(
                pb[3][0:50, :], luT_sb[:], a1_sb[:, oj * 512:(oj + 1) * 512],
                start=True, stop=True),
                waits=[("act", prev)], inc="pe")
            prev = sc.op("act", lambda a, oj=oj: a.copy(
                out=a2_sb[:, oj * 512:(oj + 1) * 512], in_=pb[3][0:50, :]),
                waits=[("pe", mm)], inc="act")
        wl_stage = sb("wl_stage", [128, D], BF16)
        wdone = prev
        dv = None
        for j in range(8 if stage >= 2 else 0):
            lv = sc.op("pool", lambda g, j=j: g.dma_start(
                out=wl_stage[:], in_=wl_d[j * 128:(j + 1) * 128, :]),
                waits=[("dve", sc.c("dve"))], inc="lst", dma=True)
            for oj in range(2):
                mm = sc.op("pe", lambda t, j=j, oj=oj: t.matmul(
                    pb[2][:], a2_sb[:, j * 128:(j + 1) * 128],
                    uaT_sb[:, oj * 512:(oj + 1) * 512],
                    start=True, stop=True),
                    waits=[("act", wdone)] + ([("dve", dv)] if dv else []), inc="pe")
                dv = sc.op("dve", lambda v, j=j, oj=oj: v.tensor_add(
                    out=weff[:, j * D + oj * 512: j * D + (oj + 1) * 512],
                    in0=pb[2][:], in1=wl_stage[:, oj * 512:(oj + 1) * 512]),
                    waits=[("pe", mm), ("lst", lv)], inc="dve")
        weff_done = sc.c("dve")

        NT1 = TPC // 128
        def ln_tile(tt, src_pre, out_dram, fm_dram, fm_tile, extra_store=None):
            v2 = sc.op("dve", lambda v: v.reduce_sum(out=mean[:, 0:1], in_=src_pre, axis=AXL.X),
                       inc="dve")
            v3 = sc.op("dve", lambda v: v.tensor_scalar_mul(
                out=nmean[:, 0:1], in0=mean[:, 0:1], scalar1=-1.0 / D),
                waits=[("dve", v2)], inc="dve")
            a1 = sc.op("act", lambda a: a.activation(
                out=xout[:], in_=src_pre, func=AF.Square, bias=nmean[:, 0:1], scale=1.0,
                accum_out=var[:, 0:1]),
                waits=[("dve", v3), ("st", sc.c("st")), ("st1", sc.c("st1"))], inc="act")
            v4 = sc.op("dve", lambda v: v.tensor_scalar(
                out=var[:, 0:1], in0=var[:, 0:1], scalar1=1.0 / D, scalar2=1e-5,
                op0=ALU.mult, op1=ALU.add),
                waits=[("act", a1)], inc="dve")
            a2 = sc.op("act", lambda a: a.activation(
                out=rstd[:, 0:1], in_=var[:, 0:1], func=AF.Sqrt),
                waits=[("dve", v4)], inc="act")
            v5 = sc.op("dve", lambda v: v.reciprocal(out=rstd[:, 0:1], in_=rstd[:, 0:1]),
                       waits=[("act", a2)], inc="dve")
            v6 = sc.op("dve", lambda v: v.tensor_scalar(
                out=xout[:], in0=src_pre, scalar1=nmean[:, 0:1], scalar2=rstd[:, 0:1],
                op0=ALU.add, op1=ALU.mult),
                waits=[("dve", v5)], inc="dve")
            stsem = "st1" if fm_dram is not None else "st"
            if fm_dram is not None:
                v7 = sc.op("dve", lambda v: v.tensor_copy(out=xout_bf[:], in_=xout[:]),
                           inc="dve")
                for sj in range(8):
                    tv = sc.op("pe", lambda t, sj=sj: t.transpose(
                        ptr[:], xout_bf[:, sj * 128:(sj + 1) * 128], ident[:]),
                        waits=[("dve", sc.c("dve")), ("pool", init_done)], inc="pe")
                    sc.op("dve", lambda v, sj=sj: v.tensor_copy(
                        out=fm_tile[:, sj * 128:(sj + 1) * 128], in_=ptr[:]),
                        waits=[("pe", tv)], inc="dve")
                vlast = sc.c("dve")
                sc.op("sync", lambda e, tt=tt: e.dma_start(
                    out=fm_dram[:, tt * 128:(tt + 1) * 128].rearrange("(j p) t -> p j t", p=128),
                    in_=fm_tile[:].rearrange("p (j t) -> p j t", j=8)),
                    waits=[("dve", vlast)], inc=stsem, dma=True)
            else:
                vlast = v6
            sc.op("sync", lambda e, tt=tt: e.dma_start(
                out=out_dram[tt * 128:(tt + 1) * 128, :], in_=xout[:]),
                waits=[("dve", vlast)], inc=stsem, dma=True)

        for tt in range(NT1 if stage >= 3 else 0):
            xv = sc.op("sync", lambda e, tt=tt: e.dma_start(
                out=xres[:], in_=x_sl[tt * 128:(tt + 1) * 128, :]),
                waits=([("dve", sc.c("dve"))] if tt else []), inc="ld2", dma=True)
            yv = sc.op("sync", lambda e, tt=tt: e.dma_start(
                out=yfm_t[:].rearrange("p (j t) -> p j t", j=8),
                in_=yfm[:, tt * 128:(tt + 1) * 128].rearrange("(j p) t -> p j t", p=128)),
                waits=([("pe", sc.c("pe"))] if tt else []), inc="ld2", dma=True)
            mms = []
            for oj in range(2):
                for j in range(8):
                    w = []
                    if j == 0:
                        w = [("ld", ld_w), ("ld2", max(xv, yv))]
                        if tt > 0:
                            w.append(("dve", sc.c("dve")))
                    mm = sc.op("pe", lambda t, oj=oj, j=j: t.matmul(
                        pb[oj][:], yfm_t[:, j * 128:(j + 1) * 128],
                        wp_sb[:, j * D + oj * 512: j * D + (oj + 1) * 512],
                        start=(j == 0), stop=(j == 7)),
                        waits=w, inc="pe" if j == 7 else None)
                mms.append(mm)
            for oj in range(2):
                sc.op("dve", lambda v, oj=oj: v.tensor_add(
                    out=pre[:, oj * 512:(oj + 1) * 512], in0=pb[oj][:],
                    in1=bp_bc[:, oj * 512:(oj + 1) * 512]),
                    waits=[("pe", mms[oj]), ("act", bias_done)], inc="dve")
            sc.op("dve", lambda v: v.tensor_add(out=pre[:], in0=pre[:], in1=xres[:]),
                  waits=[("ld2", max(xv, yv))], inc="dve")
            ln_tile(tt, pre[:], x3_dram, x3fm_dram, x3fm_t)
        d1_st = sc.c("st1")

        for tt in range(NT1 if stage >= 4 else 0):
            xv = sc.op("sync", lambda e, tt=tt: e.dma_start(
                out=xres[:], in_=x3_dram[tt * 128:(tt + 1) * 128, :]),
                waits=[("st1", d1_st), ("dve", sc.c("dve"))], inc="ld2", dma=True)
            yv = sc.op("sync", lambda e, tt=tt: e.dma_start(
                out=yfm_t[:].rearrange("p (j t) -> p j t", j=8),
                in_=x3fm_dram[:, tt * 128:(tt + 1) * 128].rearrange("(j p) t -> p j t", p=128)),
                waits=[("st1", d1_st), ("pe", sc.c("pe"))], inc="ld2", dma=True)
            mms = []
            for oj in range(2):
                for j in range(8):
                    w = []
                    if j == 0:
                        w = [("dve", weff_done), ("ld2", max(xv, yv))]
                        if tt > 0:
                            w.append(("dve", sc.c("dve")))
                    mm = sc.op("pe", lambda t, oj=oj, j=j: t.matmul(
                        pb[oj][:], yfm_t[:, j * 128:(j + 1) * 128],
                        weff[:, j * D + oj * 512: j * D + (oj + 1) * 512],
                        start=(j == 0), stop=(j == 7)),
                        waits=w, inc="pe" if j == 7 else None)
                mms.append(mm)
            for oj in range(2):
                sc.op("dve", lambda v, oj=oj: v.tensor_add(
                    out=pre[:, oj * 512:(oj + 1) * 512], in0=pb[oj][:],
                    in1=bl_bc[:, oj * 512:(oj + 1) * 512]),
                    waits=[("pe", mms[oj]), ("act", bias_done)], inc="dve")
            sc.op("dve", lambda v: v.tensor_add(out=pre[:], in0=pre[:], in1=xres[:]),
                  waits=[("ld2", max(xv, yv))], inc="dve")
            ln_tile(tt, pre[:], x4o, None, None)
        sc.op("sync", lambda e: None, waits=[("st", sc.c("st"))])
        sc.emit(block)
    return nc


HEAD_OF_CORE = [(c // 4, 4 * (c % 4)) for c in range(8)]


def kernel(**inp):
    x = np.asarray(inp["L0_attn_x"])
    Wq = np.asarray(inp["L0_attn_Wq"])
    Wk = np.asarray(inp["L0_attn_Wk"])
    Wv = np.asarray(inp["L0_attn_Wv"])
    bq = np.asarray(inp["L0_attn_bq"])
    bk = np.asarray(inp["L0_attn_bk"])
    bv = np.asarray(inp["L0_attn_bv"])

    nc = build_A()
    in_maps = []
    for c in range(8):
        b, h0 = HEAD_OF_CORE[c]
        sl = slice(h0 * HD, (h0 + HPC) * HD)
        in_maps.append({
            "xfm": np.ascontiguousarray(x[b].T),
            "wqT": np.ascontiguousarray(Wq[sl, :].T),
            "wkT": np.ascontiguousarray(Wk[sl, :].T),
            "wvT": np.ascontiguousarray(Wv[sl, :].T),
            "bqkv": np.ascontiguousarray(np.stack([bq[sl], bk[sl], bv[sl]])),
        })
    res = run_bass_kernel_spmd(nc, in_maps, core_ids=list(range(8)))
    y_fm = [np.concatenate([np.asarray(res.results[4 * b_ + i]["yfm"]) for i in range(4)], axis=0)
            for b_ in range(B)]

    nc = build_B(debug=False)
    in_maps = []
    for c in range(8):
        b, q = c // 4, c % 4
        ts = slice(q * TPC, (q + 1) * TPC)
        in_maps.append({
            "yfm_sl": np.ascontiguousarray(y_fm[b][:, ts]),
            "x_sl": np.ascontiguousarray(x[b][ts, :]),
            "woT": np.ascontiguousarray(np.asarray(inp["L0_attn_Wo"]).T),
            "bo": np.ascontiguousarray(np.asarray(inp["L0_attn_bo"])[None, :]),
            "wgT": np.ascontiguousarray(np.asarray(inp["L0_ffn_w_gate"]).T),
            "wuT": np.ascontiguousarray(np.asarray(inp["L0_ffn_w_up"]).T),
            "wdT": np.ascontiguousarray(np.asarray(inp["L0_ffn_w_down"]).T),
        })
    res = run_bass_kernel_spmd(nc, in_maps, core_ids=list(range(8)))
    x2 = np.zeros((B, T, D), np.float32)
    for c in range(8):
        b, q = c // 4, c % 4
        ts = slice(q * TPC, (q + 1) * TPC)
        x2[b, ts] = np.asarray(res.results[c]["x2o"])

    W_attn = np.asarray(inp["L1_attn_W_attn"])
    b_attn = np.asarray(inp["L1_attn_b_attn"])
    gain = np.asarray(inp["L1_attn_score_gain"]).reshape(1, 1)
    sign = 1.0 if float(gain[0, 0]) >= 0 else -1.0
    nc = build_C(sign=sign)
    in_maps = []
    for c in range(8):
        b, h0 = HEAD_OF_CORE[c]
        cs = slice(h0 * HD, (h0 + HPC) * HD)
        in_maps.append({
            "x2fm": np.ascontiguousarray(x2[b].T),
            "wqC": np.ascontiguousarray(W_attn[:, cs]),
            "wkC": np.ascontiguousarray(W_attn[:, 1024 + h0 * HD: 1024 + (h0 + HPC) * HD]),
            "wvC": np.ascontiguousarray(W_attn[:, 2048 + h0 * HD: 2048 + (h0 + HPC) * HD]),
            "bqkvC": np.ascontiguousarray(np.stack([
                b_attn[cs], b_attn[1024 + h0 * HD: 1024 + (h0 + HPC) * HD],
                b_attn[2048 + h0 * HD: 2048 + (h0 + HPC) * HD]])),
            "gain": np.ascontiguousarray(gain.astype(np.float32)),
        })
    res = run_bass_kernel_spmd(nc, in_maps, core_ids=list(range(8)))
    y1_fm = [np.concatenate([np.asarray(res.results[4 * b_ + i]["yfm"]) for i in range(4)], axis=0)
             for b_ in range(B)]

    nc = build_D()
    in_maps = []
    for c in range(8):
        b, q = c // 4, c % 4
        ts = slice(q * TPC, (q + 1) * TPC)
        in_maps.append({
            "y1fm_sl": np.ascontiguousarray(y1_fm[b][:, ts]),
            "x2_sl": np.ascontiguousarray(x2[b][ts, :]),
            "wproj": np.ascontiguousarray(np.asarray(inp["L1_attn_W_proj"])),
            "bproj": np.ascontiguousarray(np.asarray(inp["L1_attn_b_proj"])[None, :]),
            "wLT": np.ascontiguousarray(np.asarray(inp["L1_ffn_W"]).T),
            "bL": np.ascontiguousarray(np.asarray(inp["L1_ffn_b"])[None, :]),
            "ldT": np.ascontiguousarray(np.asarray(inp["L1_ffn_lora_down"]).T),
            "da": np.ascontiguousarray(np.asarray(inp["L1_ffn_down_aux"])),
            "luT": np.ascontiguousarray(np.asarray(inp["L1_ffn_lora_up"]).T),
            "uaT": np.ascontiguousarray(np.asarray(inp["L1_ffn_up_aux"]).T),
        })
    res = run_bass_kernel_spmd(nc, in_maps, core_ids=list(range(8)))
    out = np.zeros((B, T, D), np.float32)
    for c in range(8):
        b, q = c // 4, c % 4
        ts = slice(q * TPC, (q + 1) * TPC)
        out[b, ts] = np.asarray(res.results[c]["x4o"])
    return out


# version 03
# speedup vs baseline: 1.3465x; 1.3465x over previous
import numpy as np
import ml_dtypes
from contextlib import ExitStack

import concourse.bass as bass
import concourse.mybir as mybir
from concourse.bass_utils import run_bass_kernel_spmd
from concourse.aot_env import allow_ici_all_to_all

F32 = mybir.dt.float32
BF16 = mybir.dt.bfloat16
I32 = mybir.dt.int32
AF = mybir.ActivationFunctionType
ALU = mybir.AluOpType
AXL = mybir.AxisListType

B = 2
T = 8192
D = 1024
NH = 16
HD = 64
HPC = 4
TPC = 2048
CHUNK = 128
NCHUNK = T // CHUNK


class Sem:
    def __init__(self, h):
        self.h = h
        self.c = 0


class Sched:
    ENGS = ("pe", "act", "dve", "pool", "sync")

    def __init__(self, nc, es):
        self.nc = nc
        self.q = {e: [] for e in ("pe", "act", "dve", "pool", "sync")}
        self.sem = {n: Sem(es.enter_context(nc.semaphore(f"s_{n}")))
                    for n in ("ld", "ld2", "lx0", "lx1", "lst", "st1", "pe", "act",
                              "dve", "pool", "sync", "st", "cc")}
        self.pending = {}

    def op(self, eng, fn, waits=(), inc=None, dma=False):
        s = self.sem[inc] if inc else None
        n = 16 if dma else 1
        val = None
        if s is not None:
            s.c += n
            val = s.c
        if self.pending.get(eng):
            waits = tuple(self.pending.pop(eng)) + tuple(waits)
        waits = tuple((self.sem[w], v) for (w, v) in waits)

        def g(e, fn=fn, waits=waits, s=s, n=n):
            for (ss, v) in waits:
                e.wait_ge(ss.h, v)
            ins = fn(e)
            if s is not None:
                ins.then_inc(s.h, n)
        self.q[eng].append(g)
        return val

    def c(self, name):
        return self.sem[name].c

    def barrier(self):
        snap = [(n, s.c) for n, s in self.sem.items() if s.c > 0]
        for e in self.ENGS:
            self.pending[e] = list(snap)

    def flush(self, block):
        self.emit(block)
        self.q = {e: [] for e in self.ENGS}

    def emit(self, block):
        @block.tensor
        def _(eng):
            for f in self.q["pe"]:
                f(eng)

        @block.scalar
        def _(eng):
            for f in self.q["act"]:
                f(eng)

        @block.vector
        def _(eng):
            for f in self.q["dve"]:
                f(eng)

        @block.gpsimd
        def _(eng):
            for f in self.q["pool"]:
                f(eng)

        @block.sync
        def _(eng):
            for f in self.q["sync"]:
                f(eng)


def make_ident(sc, ident):
    v = sc.op("pool", lambda g: g.memset(ident[:], 1.0), inc="pool")
    return sc.op("pool", lambda g: g.affine_select(
        out=ident[:], in_=ident[:], pattern=[[-1, 128]],
        compare_op=ALU.is_equal, fill=0.0, base=0, channel_multiplier=1),
        waits=[("pool", v)], inc="pool")


def make_triu(sc, triu):
    v = sc.op("pool", lambda g: g.memset(triu[:], 1.0), inc="pool")
    return sc.op("pool", lambda g: g.affine_select(
        out=triu[:], in_=triu[:], pattern=[[1, 128]],
        compare_op=ALU.is_ge, fill=0.0, base=0, channel_multiplier=-1),
        waits=[("pool", v)], inc="pool")


def build_A(debug=False):
    nc = bass.Bass("TRN2", target_bir_lowering=False)
    dram = {}
    dram["xfm"] = nc.dram_tensor("xfm", [D, T], F32, kind="ExternalInput").ap()
    for w in ("q", "k", "v"):
        dram[f"w{w}T"] = nc.dram_tensor(f"w{w}T", [D, HPC * HD], F32, kind="ExternalInput").ap()
    dram["bqkv"] = nc.dram_tensor("bqkv", [3, HPC * HD], F32, kind="ExternalInput").ap()
    dram["yfm"] = nc.dram_tensor("yfm", [HPC * HD, T], BF16, kind="ExternalOutput").ap()
    es0 = ExitStack()
    with es0:
        sc = Sched(nc, es0)
        block = es0.enter_context(nc.Block())
        phase_A(nc, sc, block, dram, fused=False)
    return nc


def phase_A(nc, sc, block, dram, fused):
    xfm = dram["xfm"]
    wT = {w: dram[f"w{w}T"] for w in ("q", "k", "v")}
    bqkv = dram["bqkv"]
    TT = 512
    NTT = T // TT
    es = ExitStack()
    with es:
        def sb(name, shape, dt=F32):
            return es.enter_context(nc.sbuf_tensor("A_" + name, shape, dt))

        def psum(name, shape, dt=F32):
            return es.enter_context(nc.psum_tensor("A_" + name, shape, dt))

        w_sb = {w: sb(f"w{w}", [128, 8 * 256], BF16) for w in ("q", "k", "v")}
        bias_sb = sb("bias", [128, 6], F32)
        x_t = [sb(f"x_t{i}", [128, 8 * TT], BF16) for i in range(2)]
        q_fm = sb("q_fm", [128, 2 * T], BF16)
        k_fm = sb("k_fm", [128, 2 * T], BF16)
        kt = sb("kt", [128, HPC * NCHUNK * HD], BF16)
        vt = sb("vt", [128, HPC * NCHUNK * 65], BF16)
        v_stage = sb("v_stage", [128, TT], BF16)
        phi_m = sb("phi_m", [128, TT], F32)
        phi_e = sb("phi_e", [128, TT], F32)
        phi_r = sb("phi_r", [128, TT], F32)
        masked = [sb(f"masked{i}", [128, 128], BF16) for i in range(2)]
        triu = sb("triu", [128, 128], BF16)
        ident = sb("ident", [128, 128], BF16)
        S_arena = sb("S_arena", [128, 2 * 65], BF16)
        ones1 = sb("ones1", [1, 64], F32)
        rden = sb("rden", [1, 128], F32)
        bcast_sb = sb("bcast_sb", [64, 128], F32)
        y_fm = sb("y_fm", [128, 2 * T], BF16)

        pp = [psum(f"pp{i}", [128, TT]) for i in range(6)]
        ptr = psum("ptr", [128, 128], BF16)
        psc_bank = psum("psc", [128, 512])
        psc = [psc_bank[:, 0:128], psc_bank[:, 128:256]]
        pnum = [pp[0][0:65, 0:128], pp[0][0:65, 128:256]]
        pS_h = [pp[1], pp[3], pp[4], pp[5]]
        pbc = pp[2][0:64, 0:128]

        make_ident(sc, ident)
        make_triu(sc, triu)
        sc.op("pool", lambda g: g.memset(ones1[:], 1.0), inc="pool")
        sc.op("pool", lambda g: g.memset(vt[:], 1.0), inc="pool")
        sc.op("pool", lambda g: g.memset(S_arena[:], 0.0), inc="pool")
        init_done = sc.c("pool")

        for w in ("q", "k", "v"):
            for j in range(8):
                sc.op("pool", lambda g, w=w, j=j: g.dma_start(
                    out=w_sb[w][:, j * 256:(j + 1) * 256],
                    in_=wT[w][j * 128:(j + 1) * 128, :]), inc="ld", dma=True)
        for wi in range(3):
            for p in range(2):
                sc.op("pool", lambda g, wi=wi, p=p: g.dma_start(
                    out=bias_sb[:, wi * 2 + p: wi * 2 + p + 1],
                    in_=bqkv[wi:wi + 1, p * 128:(p + 1) * 128].rearrange("a e -> e a")),
                    inc="ld", dma=True)
        ld_w = sc.c("ld")

        pe_tile_done = {}
        dve_tile_done = {}
        for ti in range(NTT):
            buf = ti % 2
            waits = [("pe", pe_tile_done[ti - 2])] if ti >= 2 else []
            xv = sc.op("pool", lambda g, buf=buf, ti=ti: g.dma_start(
                out=x_t[buf][:].rearrange("p (j t) -> p j t", j=8),
                in_=xfm[:, ti * TT:(ti + 1) * TT].rearrange("(j p) t -> p j t", p=128)),
                waits=waits, inc=f"lx{buf}", dma=True)

            for wi, w in enumerate(("q", "k", "v")):
                for p in range(2):
                    bank = pp[wi * 2 + p]
                    bcol = bias_sb[:, wi * 2 + p: wi * 2 + p + 1]
                    for j in range(8):
                        wts = []
                        if j == 0:
                            wts = [("ld", ld_w), (f"lx{buf}", xv)]
                            if ti > 0:
                                wts.append(("dve", dve_tile_done[ti - 1]))
                        mm = sc.op("pe", lambda t, bank=bank, w=w, j=j, p=p, buf=buf: t.matmul(
                            bank[:],
                            w_sb[w][:, j * 256 + p * 128: j * 256 + (p + 1) * 128],
                            x_t[buf][:, j * TT:(j + 1) * TT],
                            start=(j == 0), stop=(j == 7)),
                            waits=wts, inc="pe" if j == 7 else None)
                    if w in ("q", "k"):
                        dst = q_fm if w == "q" else k_fm
                        dslc = dst[:, p * T + ti * TT: p * T + (ti + 1) * TT]
                        v1 = sc.op("dve", lambda v, bank=bank, bcol=bcol: v.tensor_scalar(
                            out=phi_m[:], in0=bank[:], scalar1=bcol, scalar2=0.0,
                            op0=ALU.add, op1=ALU.min),
                            waits=[("pe", mm), ("ld", ld_w)], inc="dve")
                        a1 = sc.op("act", lambda a: a.activation(
                            out=phi_e[:], in_=phi_m[:], func=AF.Exp),
                            waits=[("dve", v1)], inc="act")
                        sc.op("dve", lambda v, bank=bank, bcol=bcol: v.tensor_scalar(
                            out=phi_r[:], in0=bank[:], scalar1=bcol, scalar2=0.0,
                            op0=ALU.add, op1=ALU.max), inc="dve")
                        v3 = sc.op("dve", lambda v, dslc=dslc: v.tensor_add(
                            out=dslc, in0=phi_e[:], in1=phi_r[:]),
                            waits=[("act", a1)], inc="dve")
                        if w == "k":
                            for sti in range(4):
                                c = ti * 4 + sti
                                src = dst[:, p * T + c * 128:p * T + (c + 1) * 128]
                                tv = sc.op("pe", lambda t, src=src: t.transpose(
                                    ptr[:], src, ident[:]),
                                    waits=[("dve", sc.c("dve")), ("pool", init_done)],
                                    inc="pe")
                                for hh in range(2):
                                    h = 2 * p + hh
                                    sc.op("dve", lambda v, hh=hh, h=h, c=c: v.tensor_copy(
                                        out=kt[:, (h * NCHUNK + c) * 64:(h * NCHUNK + c) * 64 + 64],
                                        in_=ptr[:, hh * 64:(hh + 1) * 64]),
                                        waits=[("pe", tv)], inc="dve")
                    else:
                        vv = sc.op("dve", lambda v, bank=bank, bcol=bcol: v.tensor_scalar_add(
                            out=v_stage[:], in0=bank[:], scalar1=bcol),
                            waits=[("pe", mm), ("ld", ld_w)], inc="dve")
                        for sti in range(4):
                            c = ti * 4 + sti
                            tv = sc.op("pe", lambda t, sti=sti: t.transpose(
                                ptr[:], v_stage[:, sti * 128:(sti + 1) * 128], ident[:]),
                                waits=[("dve", sc.c("dve")), ("pool", init_done)],
                                inc="pe")
                            for hh in range(2):
                                h = 2 * p + hh
                                sc.op("dve", lambda v, hh=hh, h=h, c=c: v.tensor_copy(
                                    out=vt[:, (h * NCHUNK + c) * 65:(h * NCHUNK + c) * 65 + 64],
                                    in_=ptr[:, hh * 64:(hh + 1) * 64]),
                                    waits=[("pe", tv)], inc="dve")
            pe_tile_done[ti] = sc.c("pe")
            dve_tile_done[ti] = sc.c("dve")
        phase1_dve = sc.c("dve")
        phase1_pe = sc.c("pe")

        prev = {}
        NJ = NCHUNK * HPC
        mask_done = [0] * NJ
        num1_done = [0] * NJ
        supd_done = [0] * NJ
        ymult_done = [0] * NJ
        recip_done = [0] * NJ
        sevac_done = [0] * NJ
        bc_done = [0] * NJ
        bcevac_done = [0] * NJ
        for i in range(NCHUNK):
            for h in range(HPC):
                j = i * HPC + h
                par = h % 2
                pair = h // 2
                buf = j % 2
                rowoff = par * 64
                qk_off = pair * T + i * 128
                Sap = S_arena[rowoff:rowoff + 64, pair * 65: pair * 65 + 65]
                pSap = pS_h[h][rowoff:rowoff + 64, 0:65]
                vslc = vt[:, (h * NCHUNK + i) * 65:(h * NCHUNK + i) * 65 + 65]
                kslc = kt[:, (h * NCHUNK + i) * 64:(h * NCHUNK + i) * 64 + 64]

                if j > 0:
                    bcw = [("dve", recip_done[j - 1])]
                    if j >= 2:
                        bcw.append(("act", bcevac_done[j - 2]))
                    bc_done[j - 1] = sc.op("pe", lambda t: t.matmul(
                        pbc, ones1[:], rden[:], start=True, stop=True),
                        waits=bcw, inc="pe")
                wts = [("dve", phase1_dve)]
                if j >= 2:
                    wts.append(("dve", mask_done[j - 2]))
                scd = sc.op("pe", lambda t, rowoff=rowoff, qk_off=qk_off, buf=buf: t.matmul(
                    psc[buf][:],
                    k_fm[rowoff:rowoff + 64, qk_off:qk_off + 128],
                    q_fm[rowoff:rowoff + 64, qk_off:qk_off + 128],
                    start=True, stop=True, tile_position=(rowoff, 0)),
                    waits=wts, inc="pe")
                mwts = [("pe", scd)]
                if j >= 2:
                    mwts.append(("pe", num1_done[j - 2]))
                mask_done[j] = sc.op("dve", lambda v, buf=buf: v.tensor_mul(
                    out=masked[buf][:], in0=psc[buf][:], in1=triu[:]),
                    waits=mwts, inc="dve")
                nwts = [("dve", mask_done[j])]
                if j >= 2:
                    nwts.append(("dve", ymult_done[j - 2]))
                num1_done[j] = sc.op("pe", lambda t, buf=buf, vslc=vslc: t.matmul(
                    pnum[buf], vslc, masked[buf][:], start=True, stop=False),
                    waits=nwts, inc="pe")
                n2wts = []
                if j >= HPC:
                    n2wts = [("act", sevac_done[j - HPC])]
                sc.op("pe", lambda t, buf=buf, Sap=Sap, rowoff=rowoff, qk_off=qk_off: t.matmul(
                    pnum[buf], Sap,
                    q_fm[rowoff:rowoff + 64, qk_off:qk_off + 128],
                    start=False, stop=True, tile_position=(rowoff, 0)),
                    waits=n2wts, inc="pe")
                supd_done[j] = sc.op("pe", lambda t, pSap=pSap, kslc=kslc, vslc=vslc, i=i, rowoff=rowoff: t.matmul(
                    pSap, kslc, vslc, start=(i == 0), stop=(i == NCHUNK - 1),
                    tile_position=(0, rowoff)),
                    inc="pe")
                sevac_done[j] = sc.op("act", lambda a, Sap=Sap, pSap=pSap: a.copy(
                    out=Sap, in_=pSap),
                    waits=[("pe", supd_done[j])], inc="act")
                rwts = [("pe", supd_done[j])]
                if j > 0:
                    rwts.append(("pe", bc_done[j - 1]))
                recip_done[j] = sc.op("dve", lambda v, buf=buf: v.reciprocal(
                    out=rden[:], in_=pnum[buf][64:65, 0:128]),
                    waits=rwts, inc="dve")
                if j > 0:
                    bwts = [("pe", bc_done[j - 1])]
                    if j >= 2:
                        bwts.append(("dve", ymult_done[j - 2]))
                    bcevac_done[j - 1] = sc.op("act", lambda a: a.copy(
                        out=bcast_sb[:], in_=pbc),
                        waits=bwts, inc="act")
                    jj = j - 1
                    pj = (jj // HPC, jj % HPC)
                    ii, hh = pj
                    ydst = y_fm[(hh % 2) * 64:(hh % 2) * 64 + 64,
                                (hh // 2) * T + ii * 128:(hh // 2) * T + (ii + 1) * 128]
                    ymult_done[jj] = sc.op("dve", lambda v, buf2=jj % 2, ydst=ydst: v.tensor_mul(
                        out=ydst, in0=pnum[buf2][0:64, 0:128], in1=bcast_sb[:]),
                        waits=[("act", bcevac_done[jj])], inc="dve")
        j = NJ
        bc_done_last = sc.op("pe", lambda t: t.matmul(
            pbc, ones1[:], rden[:], start=True, stop=True),
            waits=[("dve", recip_done[NJ - 1])], inc="pe")
        bclast = sc.op("act", lambda a: a.copy(out=bcast_sb[:], in_=pbc),
                       waits=[("pe", bc_done_last), ("dve", ymult_done[NJ - 2])], inc="act")
        jj = NJ - 1
        ii, hh = jj // HPC, jj % HPC
        ydst = y_fm[(hh % 2) * 64:(hh % 2) * 64 + 64,
                    (hh // 2) * T + ii * 128:(hh // 2) * T + (ii + 1) * 128]
        yl = sc.op("dve", lambda v: v.tensor_mul(
            out=ydst, in0=pnum[jj % 2][0:64, 0:128], in1=bcast_sb[:]),
            waits=[("act", bclast)], inc="dve")

        if not fused:
            for p in range(2):
                sc.op("sync", lambda e, p=p: e.dma_start(
                    out=dram["yfm"][p * 128:(p + 1) * 128, :], in_=y_fm[:, p * T:(p + 1) * T]),
                    waits=[("dve", yl)], inc="st", dma=True)
            sc.op("sync", lambda e: None, waits=[("st", sc.c("st"))])
        else:
            a2a_in, a2a_out = dram["a2a1_in"], dram["a2a1_out"]
            for jj in range(8):
                for p in range(2):
                    sc.op("sync", lambda e, jj=jj, p=p: e.dma_start(
                        out=a2a_in[jj * 256 + p * 128: jj * 256 + (p + 1) * 128, :],
                        in_=y_fm[:, p * T + (jj % 4) * TPC: p * T + (jj % 4) * TPC + TPC]),
                        waits=[("dve", yl)], inc="st1", dma=True)
            def _a2a(g, a2a_in=a2a_in, a2a_out=a2a_out):
                with allow_ici_all_to_all():
                    return g.collective_compute(
                        "AllToAll", ALU.bypass, replica_groups=[list(range(8))],
                        ins=[a2a_in.opt()], outs=[a2a_out.opt()])
            sc.op("pool", _a2a, waits=[("st1", sc.c("st1"))], inc="cc")
        sc.flush(block)




def build_B(debug=False):
    nc = bass.Bass("TRN2", target_bir_lowering=False)
    dram = {}
    dram["yfm_sl"] = nc.dram_tensor("yfm_sl", [D, TPC], BF16, kind="ExternalInput").ap()
    dram["x_sl"] = nc.dram_tensor("x_sl", [TPC, D], F32, kind="ExternalInput").ap()
    dram["woT"] = nc.dram_tensor("woT", [D, D], F32, kind="ExternalInput").ap()
    dram["bo"] = nc.dram_tensor("bo", [1, D], F32, kind="ExternalInput").ap()
    dram["wgT"] = nc.dram_tensor("wgT", [D, 3072], I32, kind="ExternalInput").ap()
    dram["wuT"] = nc.dram_tensor("wuT", [D, 3072], I32, kind="ExternalInput").ap()
    dram["wdT"] = nc.dram_tensor("wdT", [3072, D], I32, kind="ExternalInput").ap()
    dram["x2o"] = nc.dram_tensor("x2o", [TPC, D], F32, kind="ExternalOutput").ap()
    dram["x1_scr"] = nc.dram_tensor("x1_scr", [TPC, D], F32).ap()
    dram["x1fm_scr"] = nc.dram_tensor("x1fm_scr", [D, TPC], BF16).ap()
    es0 = ExitStack()
    with es0:
        sc = Sched(nc, es0)
        block = es0.enter_context(nc.Block())
        phase_B(nc, sc, block, dram, fused=False)
    return nc


def phase_B(nc, sc, block, dram, fused):
    x_sl = dram["x_sl"]
    woT_d = dram.get("woT")
    bo_d = dram["bo"]
    wgT_d = dram["wgT"]
    wuT_d = dram["wuT"]
    wdT_d = dram["wdT"]
    x1_dram = dram["x1_scr"]
    x1fm_dram = dram["x1fm_scr"]
    if fused:
        yfm = dram["a2a1_out"]
        wo2_d = dram["wo2T"]
        NCJ = 16
        x2_dram = dram["x2_scr"]
        x2fm_dram = dram["ag2_in"]
    else:
        yfm = dram["yfm_sl"]
        wo2_d = dram["woT"]
        NCJ = 8
        x2_dram = None
        x2fm_dram = None
    es = ExitStack()
    with es:
        def sb(name, shape, dt=F32):
            return es.enter_context(nc.sbuf_tensor("B_" + name, shape, dt))

        def psum(name, shape, dt=F32):
            return es.enter_context(nc.psum_tensor("B_" + name, shape, dt))

        bo_bc = sb("bo_bc", [128, D], F32)
        bo_sb = sb("bo_sb", [1, D], F32)
        xres = sb("xres", [128, D], F32)
        pre = sb("pre", [128, D], F32)
        mean = sb("mean", [128, 1], F32)
        nmean = sb("nmean", [128, 1], F32)
        var = sb("var", [128, 1], F32)
        rstd = sb("rstd", [128, 1], F32)
        xout = sb("xout", [128, D], F32)
        xout_bf = sb("xout_bf", [128, D], BF16)
        ident = sb("ident", [128, 128], BF16)
        ones1 = sb("ones1", [1, 128], F32)
        x1fm_t = sb("x1fm_t", [128, 8 * 128], BF16)

        pb = [psum(f"pb{i}", [128, 512]) for i in range(6)]
        ptr = psum("ptr", [128, 128], BF16)

        make_ident(sc, ident)
        sc.op("pool", lambda g: g.memset(ones1[:], 1.0), inc="pool")
        init_done = sc.c("pool")

        es1 = ExitStack()
        with es1:
            yfm_t = es1.enter_context(nc.sbuf_tensor("B_yfm_t", [128, NCJ * 128], BF16))
            woT_sb = es1.enter_context(nc.sbuf_tensor("B_woT_sb", [128, NCJ * D], BF16))
            for j in range(NCJ):
                sc.op("pool", lambda g, j=j: g.dma_start(
                    out=woT_sb[:, j * D:(j + 1) * D],
                    in_=wo2_d[j * 128:(j + 1) * 128, :]), inc="ld", dma=True)
            ld_w = sc.c("ld")

            bv = sc.op("pool", lambda g: g.dma_start(out=bo_sb[:], in_=bo_d), inc="ld", dma=True)
            for oj in range(2):
                mm = sc.op("pe", lambda t, oj=oj: t.matmul(
                    pb[4][:, 0:512], ones1[:], bo_sb[:, oj * 512:(oj + 1) * 512],
                    start=True, stop=True),
                    waits=[("ld", bv), ("pool", init_done)] + ([("act", sc.c("act"))] if oj else []),
                    inc="pe")
                sc.op("act", lambda a, oj=oj: a.copy(
                    out=bo_bc[:, oj * 512:(oj + 1) * 512], in_=pb[4][:, 0:512]),
                    waits=[("pe", mm)], inc="act")
            bo_done = sc.c("act")

            NT1 = TPC // 128
            for tt in range(NT1):
                xv = sc.op("sync", lambda e, tt=tt: e.dma_start(
                    out=xres[:], in_=x_sl[tt * 128:(tt + 1) * 128, :]),
                    waits=([("dve", sc.c("dve"))] if tt else []), inc="ld2", dma=True)
                ccw = [("cc", dram["cc_wait"])] if fused else []
                yv = sc.op("sync", lambda e, tt=tt: e.dma_start(
                    out=yfm_t[:].rearrange("p (j t) -> p j t", j=NCJ),
                    in_=yfm[:, tt * 128:(tt + 1) * 128].rearrange("(j p) t -> p j t", p=128)),
                    waits=([("pe", sc.c("pe"))] if tt else []) + ccw, inc="ld2", dma=True)
                mms = []
                for oj in range(2):
                    for j in range(NCJ):
                        w = []
                        if j == 0:
                            w = [("ld", ld_w), ("ld2", max(xv, yv))]
                            if tt > 0:
                                w.append(("dve", sc.c("dve")))
                        mm = sc.op("pe", lambda t, oj=oj, j=j: t.matmul(
                            pb[oj][:],
                            yfm_t[:, j * 128:(j + 1) * 128],
                            woT_sb[:, j * D + oj * 512: j * D + (oj + 1) * 512],
                            start=(j == 0), stop=(j == NCJ - 1)),
                            waits=w, inc="pe" if j == NCJ - 1 else None)
                    mms.append(mm)
                for oj in range(2):
                    sc.op("dve", lambda v, oj=oj: v.tensor_add(
                        out=pre[:, oj * 512:(oj + 1) * 512], in0=pb[oj][:],
                        in1=bo_bc[:, oj * 512:(oj + 1) * 512]),
                        waits=[("pe", mms[oj]), ("act", bo_done)], inc="dve")
                v1 = sc.op("dve", lambda v: v.tensor_add(out=pre[:], in0=pre[:], in1=xres[:]),
                           waits=[("ld2", max(xv, yv))], inc="dve")
                v2 = sc.op("dve", lambda v: v.reduce_sum(out=mean[:, 0:1], in_=pre[:], axis=AXL.X),
                           inc="dve")
                v3 = sc.op("dve", lambda v: v.tensor_scalar_mul(
                    out=nmean[:, 0:1], in0=mean[:, 0:1], scalar1=-1.0 / D),
                    waits=[("dve", v2)], inc="dve")
                a1 = sc.op("act", lambda a: a.activation(
                    out=xout[:], in_=pre[:], func=AF.Square, bias=nmean[:, 0:1], scale=1.0,
                    accum_out=var[:, 0:1]),
                    waits=[("dve", v3), ("st1", sc.c("st1")), ("st", sc.c("st"))], inc="act")
                v4 = sc.op("dve", lambda v: v.tensor_scalar(
                    out=var[:, 0:1], in0=var[:, 0:1], scalar1=1.0 / D, scalar2=1e-5,
                    op0=ALU.mult, op1=ALU.add),
                    waits=[("act", a1)], inc="dve")
                a2 = sc.op("act", lambda a: a.activation(
                    out=rstd[:, 0:1], in_=var[:, 0:1], func=AF.Sqrt),
                    waits=[("dve", v4)], inc="act")
                v5 = sc.op("dve", lambda v: v.reciprocal(out=rstd[:, 0:1], in_=rstd[:, 0:1]),
                           waits=[("act", a2)], inc="dve")
                v6 = sc.op("dve", lambda v: v.tensor_scalar(
                    out=xout[:], in0=pre[:], scalar1=nmean[:, 0:1], scalar2=rstd[:, 0:1],
                    op0=ALU.add, op1=ALU.mult),
                    waits=[("dve", v5)], inc="dve")
                v7 = sc.op("dve", lambda v: v.tensor_copy(out=xout_bf[:], in_=xout[:]),
                           inc="dve")
                for sj in range(8):
                    tv = sc.op("pe", lambda t, sj=sj: t.transpose(
                        ptr[:], xout_bf[:, sj * 128:(sj + 1) * 128], ident[:]),
                        waits=[("dve", v7), ("pool", init_done)] +
                              ([("dve", sc.c("dve"))] if sj else []), inc="pe")
                    sc.op("dve", lambda v, sj=sj: v.tensor_copy(
                        out=x1fm_t[:, sj * 128:(sj + 1) * 128], in_=ptr[:]),
                        waits=[("pe", tv)], inc="dve")
                vlast = sc.c("dve")
                sc.op("sync", lambda e, tt=tt: e.dma_start(
                    out=x1_dram[tt * 128:(tt + 1) * 128, :], in_=xout[:]),
                    waits=[("dve", vlast)], inc="st1", dma=True)
                sc.op("sync", lambda e, tt=tt: e.dma_start(
                    out=x1fm_dram[:, tt * 128:(tt + 1) * 128].rearrange("(j p) t -> p j t", p=128),
                    in_=x1fm_t[:].rearrange("p (j t) -> p j t", j=8)),
                    waits=[("dve", vlast)], inc="st1", dma=True)

            b1_st = sc.c("st1")
            sc.flush(block)
        es2 = ExitStack()
        es2.__enter__()
        wg_sb = es2.enter_context(nc.sbuf_tensor("B_wg_sb", [128, 8 * 3072], BF16))
        wu_sb = es2.enter_context(nc.sbuf_tensor("B_wu_sb", [128, 8 * 3072], BF16))
        wd_sb = es2.enter_context(nc.sbuf_tensor("B_wd_sb", [128, 24 * D], BF16))
        stage_i = es2.enter_context(nc.sbuf_tensor("B_stage_i", [128, 512], I32))
        h_fm = es2.enter_context(nc.sbuf_tensor("B_h_fm", [128, 24 * 256], BF16))
        silu_t = es2.enter_context(nc.sbuf_tensor("B_silu_t", [128, 256], F32))
        x1fm_blk = es2.enter_context(nc.sbuf_tensor("B_x1fm_blk", [128, 8 * 256], BF16))
        for (wsb, wdr, nch, wid) in ((wg_sb, wgT_d, 8, 3072), (wu_sb, wuT_d, 8, 3072),
                                     (wd_sb, wdT_d, 24, D)):
            for j in range(nch):
                for cc2 in range(wid // 512):
                    lv = sc.op("pool", lambda g, wdr=wdr, j=j, cc2=cc2: g.dma_start(
                        out=stage_i[:], in_=wdr[j * 128:(j + 1) * 128, cc2 * 512:(cc2 + 1) * 512]),
                        waits=[("dve", sc.c("dve"))], inc="lst", dma=True)
                    sc.op("dve", lambda v, wsb=wsb, j=j, wid=wid, cc2=cc2: v.tensor_scalar_mul(
                        out=wsb[:, j * wid + cc2 * 512: j * wid + (cc2 + 1) * 512], in0=stage_i[:],
                        scalar1=1.0 / 127.0),
                        waits=[("lst", lv)], inc="dve")
        dq_done = sc.c("dve")

        NB = TPC // 256
        for bb in range(NB):
            xv = sc.op("sync", lambda e, bb=bb: e.dma_start(
                out=x1fm_blk[:].rearrange("p (j t) -> p j t", j=8),
                in_=x1fm_dram[:, bb * 256:(bb + 1) * 256].rearrange("(j p) t -> p j t", p=128)),
                waits=[("st1", b1_st)] + ([("dve", sc.c("dve"))] if bb else []),
                inc="ld2", dma=True)
            for hj in range(24):
                gmm = umm = None
                for j in range(8):
                    w = []
                    if j == 0:
                        w = [("ld2", xv), ("dve", dq_done)]
                        if hj >= 2:
                            w.append(("dve", sc.c("dve")))
                    gmm = sc.op("pe", lambda t, hj=hj, j=j: t.matmul(
                        pb[hj % 2][:, 0:256],
                        wg_sb[:, j * 3072 + hj * 128: j * 3072 + (hj + 1) * 128],
                        x1fm_blk[:, j * 256:(j + 1) * 256],
                        start=(j == 0), stop=(j == 7)),
                        waits=w, inc="pe" if j == 7 else None)
                for j in range(8):
                    umm = sc.op("pe", lambda t, hj=hj, j=j: t.matmul(
                        pb[2 + hj % 2][:, 0:256],
                        wu_sb[:, j * 3072 + hj * 128: j * 3072 + (hj + 1) * 128],
                        x1fm_blk[:, j * 256:(j + 1) * 256],
                        start=(j == 0), stop=(j == 7)),
                        inc="pe" if j == 7 else None)
                a1 = sc.op("act", lambda a, hj=hj: a.activation(
                    out=silu_t[:], in_=pb[hj % 2][:, 0:256], func=AF.Silu),
                    waits=[("pe", gmm)] + ([("dve", sc.c("dve"))] if hj >= 1 else []),
                    inc="act")
                sc.op("dve", lambda v, hj=hj: v.tensor_mul(
                    out=h_fm[:, hj * 256:(hj + 1) * 256], in0=pb[2 + hj % 2][:, 0:256],
                    in1=silu_t[:]),
                    waits=[("act", a1), ("pe", umm)], inc="dve")
            hdone = sc.c("dve")
            for st_i in range(2):
                tt = bb * 2 + st_i
                xv2 = sc.op("sync", lambda e, tt=tt: e.dma_start(
                    out=xres[:], in_=x1_dram[tt * 128:(tt + 1) * 128, :]),
                    waits=[("st1", b1_st), ("dve", sc.c("dve"))], inc="ld2", dma=True)
                dmm = []
                for oj in range(2):
                    for hj in range(24):
                        w = []
                        if hj == 0:
                            w = [("dve", hdone)]
                            w.append(("dve", sc.c("dve")))
                        mm = sc.op("pe", lambda t, oj=oj, hj=hj, st_i=st_i: t.matmul(
                            pb[4 + oj][:],
                            h_fm[:, hj * 256 + st_i * 128: hj * 256 + (st_i + 1) * 128],
                            wd_sb[:, hj * D + oj * 512: hj * D + (oj + 1) * 512],
                            start=(hj == 0), stop=(hj == 23)),
                            waits=w, inc="pe" if hj == 23 else None)
                    dmm.append(mm)
                for oj in range(2):
                    sc.op("dve", lambda v, oj=oj: v.tensor_add(
                        out=pre[:, oj * 512:(oj + 1) * 512], in0=pb[4 + oj][:],
                        in1=xres[:, oj * 512:(oj + 1) * 512]),
                        waits=[("pe", dmm[oj]), ("ld2", xv2)], inc="dve")
                v2 = sc.op("dve", lambda v: v.reduce_sum(out=mean[:, 0:1], in_=pre[:], axis=AXL.X),
                           inc="dve")
                v3 = sc.op("dve", lambda v: v.tensor_scalar_mul(
                    out=nmean[:, 0:1], in0=mean[:, 0:1], scalar1=-1.0 / D),
                    waits=[("dve", v2)], inc="dve")
                a1 = sc.op("act", lambda a: a.activation(
                    out=xout[:], in_=pre[:], func=AF.Square, bias=nmean[:, 0:1], scale=1.0,
                    accum_out=var[:, 0:1]),
                    waits=[("dve", v3), ("st", sc.c("st")), ("st1", sc.c("st1"))], inc="act")
                v4 = sc.op("dve", lambda v: v.tensor_scalar(
                    out=var[:, 0:1], in0=var[:, 0:1], scalar1=1.0 / D, scalar2=1e-5,
                    op0=ALU.mult, op1=ALU.add),
                    waits=[("act", a1)], inc="dve")
                a2 = sc.op("act", lambda a: a.activation(
                    out=rstd[:, 0:1], in_=var[:, 0:1], func=AF.Sqrt),
                    waits=[("dve", v4)], inc="act")
                v5 = sc.op("dve", lambda v: v.reciprocal(out=rstd[:, 0:1], in_=rstd[:, 0:1]),
                           waits=[("act", a2)], inc="dve")
                v6 = sc.op("dve", lambda v: v.tensor_scalar(
                    out=xout[:], in0=pre[:], scalar1=nmean[:, 0:1], scalar2=rstd[:, 0:1],
                    op0=ALU.add, op1=ALU.mult),
                    waits=[("dve", v5)], inc="dve")
                if not fused:
                    sc.op("sync", lambda e, tt=tt: e.dma_start(
                        out=dram["x2o"][tt * 128:(tt + 1) * 128, :], in_=xout[:]),
                        waits=[("dve", v6)], inc="st", dma=True)
                else:
                    sc.op("sync", lambda e, tt=tt: e.dma_start(
                        out=x2_dram[tt * 128:(tt + 1) * 128, :], in_=xout[:]),
                        waits=[("dve", v6)], inc="st1", dma=True)
                    v7 = sc.op("dve", lambda v: v.tensor_copy(out=xout_bf[:], in_=xout[:]),
                               waits=[("dve", v6)], inc="dve")
                    for sj in range(8):
                        tv = sc.op("pe", lambda t, sj=sj: t.transpose(
                            ptr[:], xout_bf[:, sj * 128:(sj + 1) * 128], ident[:]),
                            waits=[("dve", sc.c("dve"))], inc="pe")
                        sc.op("dve", lambda v, sj=sj: v.tensor_copy(
                            out=x1fm_t[:, sj * 128:(sj + 1) * 128], in_=ptr[:]),
                            waits=[("pe", tv)], inc="dve")
                    vl2 = sc.c("dve")
                    sc.op("sync", lambda e, tt=tt: e.dma_start(
                        out=x2fm_dram[:, tt * 128:(tt + 1) * 128].rearrange("(j p) t -> p j t", p=128),
                        in_=x1fm_t[:].rearrange("p (j t) -> p j t", j=8)),
                        waits=[("dve", vl2)], inc="st1", dma=True)
        if not fused:
            sc.op("sync", lambda e: None, waits=[("st", sc.c("st"))])
        else:
            sc.op("pool", lambda g, ai=dram["ag2_in"], ao=dram["ag2_out"]: g.collective_compute(
                "AllGather", ALU.bypass, replica_groups=[[0, 1, 2, 3], [4, 5, 6, 7]],
                ins=[ai.opt()], outs=[ao.opt()]),
                waits=[("st1", sc.c("st1"))], inc="cc")
        sc.flush(block)
        es2.__exit__(None, None, None)




def build_C(sign=1.0):
    nc = bass.Bass("TRN2", target_bir_lowering=False)
    dram = {}
    dram["x2fm"] = nc.dram_tensor("x2fm", [D, T], F32, kind="ExternalInput").ap()
    for w in ("q", "k", "v"):
        dram[f"w{w}C"] = nc.dram_tensor(f"w{w}C", [D, HPC * HD], F32, kind="ExternalInput").ap()
    dram["bqkvC"] = nc.dram_tensor("bqkvC", [3, HPC * HD], F32, kind="ExternalInput").ap()
    dram["gain"] = nc.dram_tensor("gain", [1, 1], F32, kind="ExternalInput").ap()
    dram["yfmC"] = nc.dram_tensor("yfm", [HPC * HD, T], BF16, kind="ExternalOutput").ap()
    es0 = ExitStack()
    with es0:
        sc = Sched(nc, es0)
        block = es0.enter_context(nc.Block())
        phase_C(nc, sc, block, dram, sign, fused=False)
    return nc


def phase_C(nc, sc, block, dram, sign, fused):
    wC = {w: dram[f"w{w}C"] for w in ("q", "k", "v")}
    bqkv = dram["bqkvC"]
    gain_d = dram["gain"]
    xfm = None if fused else dram["x2fm"]
    ag2_out = dram.get("ag2_out")

    TT = 512
    NTT = T // TT
    NQS = T // 128
    NQT = T // 512
    es = ExitStack()
    with es:
        def sb(name, shape, dt=F32):
            return es.enter_context(nc.sbuf_tensor("C_" + name, shape, dt))

        w_sb = {w: sb(f"w{w}", [128, 8 * 256], BF16) for w in ("q", "k", "v")}
        bias_sb = sb("bias", [128, 6], F32)
        x_t = [sb(f"x_t{i}", [128, 8 * TT], BF16) for i in range(2)]
        q_fm = sb("q_fm", [128, 2 * T], BF16)
        k_fm = sb("k_fm", [128, 2 * T], BF16)
        vt = sb("vt", [128, HPC * NCHUNK * 65], BF16)
        v_stage = sb("v_stage", [128, TT], F32)
        ident = sb("ident", [128, 128], BF16)
        identf = sb("identf", [128, 128], F32)
        onesc = sb("onesc", [65, 128], BF16)
        onesf = sb("onesf", [1, 128], F32)
        ksum = sb("ksum", [128, 2], F32)
        ksumb = sb("ksumb", [128, 2], BF16)
        negmu = [sb(f"negmu{h}", [128, NQS]) for h in range(HPC)]
        absum = [sb(f"absum{h}", [128, NQS]) for h in range(HPC)]
        epsc = [sb(f"epsc{h}", [128, NQS]) for h in range(HPC)]
        acc8 = [sb(f"acc8_{i}", [128, 8], F32) for i in range(2)]
        uabs = sb("uabs", [128, 1024], BF16)
        gaincol = sb("gaincol", [128, 1], F32)
        g4inv = sb("g4inv", [128, 1], F32)
        scr = sb("scr", [128, 2], F32)
        madt = sb("madt", [128, NQS], F32)
        m2t = sb("m2t", [128, NQS], F32)
        u_t = [sb(f"u_t{i}", [128, 1024], BF16) for i in range(2)]
        w_t = [sb(f"w_t{i}", [128, 1024], BF16) for i in range(2)]
        r_t = [sb(f"r_t{i}", [128, 1024], BF16) for i in range(2)]
        dmask = [sb(f"dmask{i}", [128, 1024], BF16) for i in range(4)]
        negmurow = [sb(f"negmurow{i}", [1, 512], BF16) for i in range(2)]
        epsrow = [sb(f"epsrow{i}", [1, 512], F32) for i in range(2)]
        denrow = [sb(f"denrow{i}", [1, 512], F32) for i in range(2)]
        rden = [sb(f"rdenC{i}", [1, 512], F32) for i in range(2)]
        bcast_sb = [sb(f"bcastC{i}", [64, 512], F32) for i in range(2)]
        y_fm = sb("y_fmC", [128, 2 * T], BF16)

        pa = es.enter_context(nc.psum_tensor("C_pa", [128, 2048], F32))
        pa2 = es.enter_context(nc.psum_tensor("C_pa2", [128, 2048], F32))
        ptrf = pa2[:, 1536:1664]
        pproj = [pa[:, i * 512:(i + 1) * 512] for i in range(4)] + \
                [pa2[:, i * 512:(i + 1) * 512] for i in range(2)]
        psF = {0: [pa[:, 0:1024], pa[:, 1024:2048]],
               1: [pa2[:, 0:1024], pa2[:, 1024:2048]]}
        psc2 = [pa[:, 0:1024], pa[:, 1024:2048]]
        py = {0: pa2[0:65, 0:512], 1: pa2[0:65, 512:1024]}
        prow = {0: pa2[:, 1024:1536], 1: pa2[:, 1536:2048]}
        pmu = prow[0]

        make_ident(sc, ident)
        v = sc.op("pool", lambda g: g.memset(identf[:], 1.0), inc="pool")
        sc.op("pool", lambda g: g.affine_select(
            out=identf[:], in_=identf[:], pattern=[[-1, 128]],
            compare_op=ALU.is_equal, fill=0.0, base=0, channel_multiplier=1),
            waits=[("pool", v)], inc="pool")
        sc.op("pool", lambda g: g.memset(onesc[:], 1.0), inc="pool")
        sc.op("pool", lambda g: g.memset(onesf[:], 1.0), inc="pool")
        sc.op("pool", lambda g: g.memset(vt[:], 1.0), inc="pool")
        for j in range(4):
            v = sc.op("pool", lambda g, j=j: g.memset(dmask[j][:], 1.0), inc="pool")
            for hh in range(2):
                v = sc.op("pool", lambda g, j=j, hh=hh: g.affine_select(
                    out=dmask[j][:, hh * 512:(hh + 1) * 512], in_=dmask[j][:, hh * 512:(hh + 1) * 512],
                    pattern=[[1, 512]],
                    compare_op=ALU.is_ge, fill=0.0, base=-128 * j, channel_multiplier=-1),
                    waits=[("pool", v)], inc="pool")
        init_done = sc.c("pool")

        for w in ("q", "k", "v"):
            for j in range(8):
                sc.op("pool", lambda g, w=w, j=j: g.dma_start(
                    out=w_sb[w][:, j * 256:(j + 1) * 256],
                    in_=wC[w][j * 128:(j + 1) * 128, :]), inc="ld", dma=True)
        for wi in range(3):
            for p in range(2):
                sc.op("pool", lambda g, wi=wi, p=p: g.dma_start(
                    out=bias_sb[:, wi * 2 + p: wi * 2 + p + 1],
                    in_=bqkv[wi:wi + 1, p * 128:(p + 1) * 128].rearrange("a e -> e a")),
                    inc="ld", dma=True)
        gv = sc.op("pool", lambda g: g.dma_start(out=scr[0:1, 0:1], in_=gain_d),
                   inc="ld", dma=True)
        ld_w = sc.c("ld")

        gmm = sc.op("pe", lambda t: t.matmul(
            pmu[:, 0:1], onesf[:], scr[0:1, 0:1], start=True, stop=True),
            waits=[("ld", gv), ("pool", init_done)], inc="pe")
        a_g = sc.op("act", lambda a: a.copy(out=gaincol[:, 0:1], in_=pmu[:, 0:1]),
                    waits=[("pe", gmm)], inc="act")
        v1 = sc.op("dve", lambda v: v.tensor_mul(out=scr[:, 0:1], in0=gaincol[:, 0:1],
                                                 in1=gaincol[:, 0:1]),
                   waits=[("act", a_g)], inc="dve")
        v2 = sc.op("dve", lambda v: v.tensor_mul(out=scr[:, 1:2], in0=scr[:, 0:1],
                                                 in1=scr[:, 0:1]),
                   waits=[("dve", v1)], inc="dve")
        g4_done = sc.op("dve", lambda v: v.reciprocal(out=g4inv[:, 0:1], in_=scr[:, 1:2]),
                        waits=[("dve", v2)], inc="dve")

        pe_tile_done = {}
        dve_tile_done = {}
        for ti in range(NTT):
            buf = ti % 2
            waits = [("pe", pe_tile_done[ti - 2])] if ti >= 2 else []
            if fused:
                rr = ti // 4
                t0 = (ti % 4) * TT
                xv = sc.op("pool", lambda g, buf=buf, rr=rr, t0=t0: g.dma_start(
                    out=x_t[buf][:].rearrange("p (j t) -> p j t", j=8),
                    in_=ag2_out[rr * D:(rr + 1) * D, t0:t0 + TT].rearrange("(j p) t -> p j t", p=128)),
                    waits=waits + [("cc", dram["cc_wait"])], inc=f"lx{buf}", dma=True)
            else:
                xv = sc.op("pool", lambda g, buf=buf, ti=ti: g.dma_start(
                    out=x_t[buf][:].rearrange("p (j t) -> p j t", j=8),
                    in_=xfm[:, ti * TT:(ti + 1) * TT].rearrange("(j p) t -> p j t", p=128)),
                    waits=waits, inc=f"lx{buf}", dma=True)
            for wi, w in enumerate(("q", "k", "v")):
                for p in range(2):
                    bank = pproj[wi * 2 + p]
                    bcol = bias_sb[:, wi * 2 + p: wi * 2 + p + 1]
                    for j in range(8):
                        wts = []
                        if j == 0:
                            wts = [("ld", ld_w), (f"lx{buf}", xv)]
                            if ti > 0:
                                wts.append(("dve", dve_tile_done[ti - 1]))
                            if ti == 0 and wi == 1 and p == 0:
                                pass
                        mm = sc.op("pe", lambda t, bank=bank, w=w, j=j, p=p, buf=buf: t.matmul(
                            bank, w_sb[w][:, j * 256 + p * 128: j * 256 + (p + 1) * 128],
                            x_t[buf][:, j * TT:(j + 1) * TT],
                            start=(j == 0), stop=(j == 7)),
                            waits=wts, inc="pe" if j == 7 else None)
                    if w == "q":
                        dst = q_fm[:, p * T + ti * TT: p * T + (ti + 1) * TT]
                        sc.op("dve", lambda v, bank=bank, bcol=bcol, dst=dst: v.tensor_scalar(
                            out=dst, in0=bank, scalar1=bcol, scalar2=float(sign),
                            op0=ALU.add, op1=ALU.mult),
                            waits=[("pe", mm), ("ld", ld_w)], inc="dve")
                    elif w == "k":
                        dst = k_fm[:, p * T + ti * TT: p * T + (ti + 1) * TT]
                        sc.op("dve", lambda v, bank=bank, bcol=bcol, dst=dst: v.tensor_scalar_add(
                            out=dst, in0=bank, scalar1=bcol),
                            waits=[("pe", mm), ("ld", ld_w)], inc="dve")
                    else:
                        sc.op("dve", lambda v, bank=bank, bcol=bcol: v.tensor_scalar_add(
                            out=v_stage[:], in0=bank, scalar1=bcol),
                            waits=[("pe", mm), ("ld", ld_w)], inc="dve")
                        for sti in range(4):
                            c = ti * 4 + sti
                            tv = sc.op("pe", lambda t, sti=sti: t.transpose(
                                ptrf, v_stage[:, sti * 128:(sti + 1) * 128], identf[:]),
                                waits=[("dve", sc.c("dve")), ("pool", init_done)], inc="pe")
                            for hh in range(2):
                                h = 2 * p + hh
                                sc.op("dve", lambda v, hh=hh, h=h, c=c: v.tensor_copy(
                                    out=vt[:, (h * NCHUNK + c) * 65:(h * NCHUNK + c) * 65 + 64],
                                    in_=ptrf[:, hh * 64:(hh + 1) * 64]),
                                    waits=[("pe", tv)], inc="dve")
            pe_tile_done[ti] = sc.c("pe")
            dve_tile_done[ti] = sc.c("dve")
        proj_dve = sc.c("dve")

        kv = {}
        for p in range(2):
            r0 = sc.op("dve", lambda v, p=p: v.reduce_sum(
                out=ksum[:, p:p + 1], in_=k_fm[:, p * T:(p + 1) * T], axis=AXL.X),
                waits=[("dve", proj_dve)], inc="dve")
            kv[p] = sc.op("dve", lambda v, p=p: v.tensor_copy(
                out=ksumb[:, p:p + 1], in_=ksum[:, p:p + 1]),
                waits=[("dve", r0)], inc="dve")

        mu_done = {}
        for h in range(HPC):
            par, pair = h % 2, h // 2
            ro = par * 64
            for qs in range(NQS):
                w = []
                if qs == 0:
                    w = [("dve", kv[pair]), ("act", a_g)]
                    if h > 0:
                        w.append(("dve", mu_done[h - 1]))
                mm = sc.op("pe", lambda t, ro=ro, pair=pair, qs=qs: t.matmul(
                    pmu[:, qs:qs + 1],
                    q_fm[ro:ro + 64, pair * T + qs * 128: pair * T + (qs + 1) * 128],
                    ksumb[ro:ro + 64, pair:pair + 1],
                    start=True, stop=True, tile_position=(ro, 0)),
                    waits=w, inc="pe" if qs == NQS - 1 else None)
            sc.op("dve", lambda v, h=h: v.tensor_scalar_mul(
                out=negmu[h][:], in0=pmu[:, 0:NQS], scalar1=-1.0 / T),
                waits=[("pe", mm)], inc="dve")
            mu_done[h] = sc.c("dve")
        mu_all = sc.c("dve")

        eps_done = {}
        bank_last_act = {}
        for pair in range(2):
            hA, hB = 2 * pair, 2 * pair + 1
            prev_reduce = None
            for qs in range(NQS):
                acts = {}
                for kq in range(8):
                    half = kq % 2
                    for a_i, (h, ro) in enumerate(((hA, 0), (hB, 64))):
                        for kk in range(2):
                            kt0 = (kq * 2 + kk) * 512
                            w = []
                            if kk == 0:
                                bk = (a_i, half)
                                if bk in bank_last_act:
                                    w.append(("act", bank_last_act[bk]))
                                if pair == 0 and qs == 0 and kq < 2:
                                    w.append(("dve", mu_all))
                            mm = sc.op("pe", lambda t, ro=ro, pair=pair, qs=qs, kt0=kt0,
                                       a_i=a_i, half=half, kk=kk: t.matmul(
                                psF[a_i][half][:, kk * 512:(kk + 1) * 512],
                                q_fm[ro:ro + 64, pair * T + qs * 128: pair * T + (qs + 1) * 128],
                                k_fm[ro:ro + 64, pair * T + kt0: pair * T + kt0 + 512],
                                start=True, stop=True, tile_position=(ro, 0)),
                                waits=w, inc="pe" if kk == 1 else None)
                        h_ = hA if a_i == 0 else hB
                        aw = [("pe", mm)]
                        if prev_reduce is not None and kq == 0:
                            aw.append(("dve", prev_reduce))
                        bank_last_act[(a_i, half)] = sc.op(
                            "act", lambda a, a_i=a_i, half=half, h_=h_, qs=qs, kq=kq: a.activation(
                                out=uabs[:], in_=psF[a_i][half],
                                func=AF.Abs, bias=negmu[h_][:, qs:qs + 1], scale=1.0,
                                accum_out=acc8[a_i][:, kq:kq + 1]),
                            waits=aw, inc="act")
                rs = None
                for a_i, h_ in ((0, hA), (1, hB)):
                    rs = sc.op("dve", lambda v, a_i=a_i, h_=h_, qs=qs: v.reduce_sum(
                        out=absum[h_][:, qs:qs + 1], in_=acc8[a_i][:], axis=AXL.X),
                        waits=[("act", max(bank_last_act[(a_i, 0)], bank_last_act[(a_i, 1)]))],
                        inc="dve")
                prev_reduce = rs
            for h_ in (hA, hB):
                w1 = sc.op("dve", lambda v, h_=h_: v.tensor_scalar(
                    out=madt[:], in0=absum[h_][:], scalar1=1.0 / T, scalar2=1e-6,
                    op0=ALU.mult, op1=ALU.add), waits=[("dve", prev_reduce)], inc="dve")
                w2 = sc.op("dve", lambda v: v.tensor_mul(out=m2t[:], in0=madt[:], in1=madt[:]),
                           inc="dve")
                w3 = sc.op("dve", lambda v: v.tensor_mul(out=madt[:], in0=m2t[:], in1=m2t[:]),
                           inc="dve")
                eps_done[h_] = sc.op("dve", lambda v, h_=h_: v.tensor_scalar(
                    out=epsc[h_][:], in0=madt[:], scalar1=g4inv[:, 0:1], scalar2=1e-6,
                    op0=ALU.mult, op1=ALU.mult),
                    waits=[("dve", g4_done)], inc="dve")
        stats_done = sc.c("dve")
        stats_act = sc.c("act")

        ymult_done = 0
        prev_pair_last = {}
        relu_done = {}
        u_free = {}
        r_free = {}
        bc_free = {}
        prev_avmm = {}
        for pair in range(2):
            hA, hB = 2 * pair, 2 * pair + 1
            for qt in range(NQT):
                nrows = {}
                for a_i, h_ in ((0, hA), (1, hB)):
                    ro = (h_ % 2) * 64
                    for jj in range(4):
                        qs = qt * 4 + jj
                        w = [("dve", stats_done)] if jj == 0 else []
                        key = ("rowbank", a_i)
                        if jj == 0 and key in prev_pair_last:
                            w.append(("act", prev_pair_last[key]))
                        if jj == 0 and a_i in prev_avmm:
                            w.append(("pe", prev_avmm[a_i]))
                        tv = sc.op("pe", lambda t, a_i=a_i, h_=h_, qs=qs, jj=jj: t.transpose(
                            prow[a_i][0:1, jj * 128:(jj + 1) * 128],
                            negmu[h_][:, qs:qs + 1], identf[:]),
                            waits=w, inc="pe" if jj == 3 else None)
                    nrows[a_i] = sc.op("act", lambda a, a_i=a_i: a.copy(
                        out=negmurow[a_i][:], in_=prow[a_i][0:1, 0:512]),
                        waits=[("pe", tv)], inc="act")
                    prev_pair_last[("rowbank", a_i)] = nrows[a_i]
                kcmax = 4 * qt + 4
                avmm = {}
                roA, roB = (hA % 2) * 64, (hB % 2) * 64

                def sched_av(kc, kcmax=kcmax, hA=hA, hB=hB):
                    buf = kc % 2
                    for a_i, h_ in ((0, hA), (1, hB)):
                        w = [("dve", r2_done[buf])]
                        if kc == 0:
                            w.append(("dve", ymult_done))
                        avmm[a_i] = sc.op("pe", lambda t, a_i=a_i, h_=h_, kc=kc, buf=buf, kcmax=kcmax: t.matmul(
                            py[a_i],
                            vt[:, (h_ * NCHUNK + kc) * 65:(h_ * NCHUNK + kc) * 65 + 65],
                            r_t[buf][:, a_i * 512:(a_i + 1) * 512],
                            start=(kc == 0), stop=(kc == kcmax - 1)),
                            waits=w, inc="pe")
                        r_free[buf] = avmm[a_i]

                r2_done = {}
                for kc in range(kcmax):
                    buf = kc % 2
                    w = []
                    if "relu" in relu_done:
                        pass
                    if (buf,) in relu_done:
                        w.append(("act", relu_done[(buf,)]))
                    else:
                        w.append(("act", nrows[1]))
                    for a_i, (h_, ro) in enumerate(((hA, roA), (hB, roB))):
                        sc.op("pe", lambda t, buf=buf, a_i=a_i, ro=ro, pair=pair, kc=kc, qt=qt: t.matmul(
                            psc2[buf][:, a_i * 512:(a_i + 1) * 512],
                            k_fm[ro:ro + 64, pair * T + kc * 128: pair * T + (kc + 1) * 128],
                            q_fm[ro:ro + 64, pair * T + qt * 512: pair * T + (qt + 1) * 512],
                            start=True, stop=False, tile_position=(ro, 0)),
                            waits=(w if a_i == 0 else []), inc="pe")
                    for a_i in (0, 1):
                        bmm = sc.op("pe", lambda t, buf=buf, a_i=a_i: t.matmul(
                            psc2[buf][:, a_i * 512:(a_i + 1) * 512],
                            onesc[0:1, :], negmurow[a_i][:],
                            start=False, stop=True, tile_position=(0, 0)),
                            inc="pe")
                    if kc > 0:
                        sched_av(kc - 1)
                    rv = sc.op("act", lambda a, buf=buf: a.activation(
                        out=u_t[buf][:], in_=psc2[buf][:], func=AF.Relu),
                        waits=[("pe", bmm)] +
                              ([("dve", u_free[buf])] if buf in u_free else []),
                        inc="act")
                    relu_done[(buf,)] = rv
                    d1 = sc.op("dve", lambda v, buf=buf: v.tensor_mul(
                        out=w_t[buf][:], in0=u_t[buf][:], in1=u_t[buf][:]),
                        waits=[("act", rv)], inc="dve")
                    u_free[buf] = d1
                    d2 = sc.op("dve", lambda v, buf=buf: v.tensor_mul(
                        out=r_t[buf][:], in0=w_t[buf][:], in1=w_t[buf][:]),
                        waits=([("pe", r_free[buf])] if buf in r_free else []),
                        inc="dve")
                    if kc >= 4 * qt:
                        d2 = sc.op("dve", lambda v, buf=buf, kc=kc, qt=qt: v.tensor_mul(
                            out=r_t[buf][:], in0=r_t[buf][:],
                            in1=dmask[kc - 4 * qt][:]),
                            inc="dve")
                    r2_done[buf] = d2
                sched_av(kcmax - 1)
                for a_i, h_ in ((0, hA), (1, hB)):
                    ro = (h_ % 2) * 64
                    for jj in range(4):
                        qs = qt * 4 + jj
                        w = [("pe", avmm[a_i])] if jj == 0 else []
                        tv = sc.op("pe", lambda t, a_i=a_i, h_=h_, qs=qs, jj=jj: t.transpose(
                            prow[a_i][0:1, jj * 128:(jj + 1) * 128],
                            epsc[h_][:, qs:qs + 1], identf[:]),
                            waits=w, inc="pe" if jj == 3 else None)
                    ev = sc.op("act", lambda a, a_i=a_i: a.copy(
                        out=epsrow[a_i][:], in_=prow[a_i][0:1, 0:512]),
                        waits=[("pe", tv)], inc="act")
                    d1 = sc.op("dve", lambda v, a_i=a_i: v.tensor_add(
                        out=denrow[a_i][:], in0=py[a_i][64:65, 0:512], in1=epsrow[a_i][:]),
                        waits=[("act", ev), ("pe", avmm[a_i])], inc="dve")
                    d2 = sc.op("dve", lambda v, a_i=a_i: v.reciprocal(
                        out=rden[a_i][:], in_=denrow[a_i][:]),
                        waits=[("dve", d1)], inc="dve")
                    bmm = sc.op("pe", lambda t, a_i=a_i: t.matmul(
                        prow[a_i][0:64, 0:512], onesf[0:1, 0:64], rden[a_i][:],
                        start=True, stop=True),
                        waits=[("dve", d2), ("act", ev)], inc="pe")
                    bev = sc.op("act", lambda a, a_i=a_i: a.copy(
                        out=bcast_sb[a_i][:], in_=prow[a_i][0:64, 0:512]),
                        waits=[("pe", bmm)] +
                              ([("dve", bc_free[a_i])] if a_i in bc_free else []),
                        inc="act")
                    prev_pair_last[("rowbank", a_i)] = bev
                    ydst = y_fm[ro:ro + 64, (h_ // 2) * T + qt * 512:(h_ // 2) * T + (qt + 1) * 512]
                    ym = sc.op("dve", lambda v, a_i=a_i, ydst=ydst: v.tensor_mul(
                        out=ydst, in0=py[a_i][0:64, 0:512], in1=bcast_sb[a_i][:]),
                        waits=[("act", bev)], inc="dve")
                    bc_free[a_i] = ym
                    ymult_done = ym
                prev_avmm = dict(avmm)

        yl = sc.c("dve")
        if not fused:
            for p in range(2):
                sc.op("sync", lambda e, p=p: e.dma_start(
                    out=dram["yfmC"][p * 128:(p + 1) * 128, :], in_=y_fm[:, p * T:(p + 1) * T]),
                    waits=[("dve", yl)], inc="st", dma=True)
            sc.op("sync", lambda e: None, waits=[("st", sc.c("st"))])
        else:
            a2a_in, a2a_out = dram["a2a3_in"], dram["a2a3_out"]
            for jj in range(8):
                for p in range(2):
                    sc.op("sync", lambda e, jj=jj, p=p: e.dma_start(
                        out=a2a_in[jj * 256 + p * 128: jj * 256 + (p + 1) * 128, :],
                        in_=y_fm[:, p * T + (jj % 4) * TPC: p * T + (jj % 4) * TPC + TPC]),
                        waits=[("dve", yl)], inc="st1", dma=True)

            def _a2a(g, a2a_in=a2a_in, a2a_out=a2a_out):
                with allow_ici_all_to_all():
                    return g.collective_compute(
                        "AllToAll", ALU.bypass, replica_groups=[list(range(8))],
                        ins=[a2a_in.opt()], outs=[a2a_out.opt()])
            sc.op("pool", _a2a, waits=[("st1", sc.c("st1"))], inc="cc")
        sc.flush(block)




def build_D(stage=4):
    nc = bass.Bass("TRN2", target_bir_lowering=False)
    dram = {}
    dram["y1fm_sl"] = nc.dram_tensor("y1fm_sl", [D, TPC], BF16, kind="ExternalInput").ap()
    dram["x2_sl"] = nc.dram_tensor("x2_sl", [TPC, D], F32, kind="ExternalInput").ap()
    dram["wproj"] = nc.dram_tensor("wproj", [D, D], F32, kind="ExternalInput").ap()
    dram["bproj"] = nc.dram_tensor("bproj", [1, D], F32, kind="ExternalInput").ap()
    dram["wLT"] = nc.dram_tensor("wLT", [D, D], F32, kind="ExternalInput").ap()
    dram["bL"] = nc.dram_tensor("bL", [1, D], F32, kind="ExternalInput").ap()
    dram["ldT"] = nc.dram_tensor("ldT", [100, 4], F32, kind="ExternalInput").ap()
    dram["da"] = nc.dram_tensor("da", [100, D], F32, kind="ExternalInput").ap()
    dram["luT"] = nc.dram_tensor("luT", [4, 50], F32, kind="ExternalInput").ap()
    dram["uaT"] = nc.dram_tensor("uaT", [50, D], F32, kind="ExternalInput").ap()
    dram["x4o"] = nc.dram_tensor("x4o", [TPC, D], F32, kind="ExternalOutput").ap()
    dram["x3_scr"] = nc.dram_tensor("x3_scr", [TPC, D], F32).ap()
    dram["x3fm_scr"] = nc.dram_tensor("x3fm_scr", [D, TPC], BF16).ap()
    es0 = ExitStack()
    with es0:
        sc = Sched(nc, es0)
        block = es0.enter_context(nc.Block())
        phase_D(nc, sc, block, dram, fused=False, stage=stage)
    return nc


def phase_D(nc, sc, block, dram, fused, stage=4):
    bp_d = dram["bproj"]
    wl_d = dram["wLT"]
    bl_d = dram["bL"]
    ldT_d = dram["ldT"]
    da_d = dram["da"]
    luT_d = dram["luT"]
    uaT_d = dram["uaT"]
    x4o = dram["x4o"]
    x3_dram = dram["x3_scr"]
    x3fm_dram = dram["x3fm_scr"]
    if fused:
        yfm = dram["a2a3_out"]
        wp_d = dram["wp2"]
        NCJ = 16
        x_sl = dram["x2_scr"]
    else:
        yfm = dram["y1fm_sl"]
        wp_d = dram["wproj"]
        NCJ = 8
        x_sl = dram["x2_sl"]

    es = ExitStack()
    with es:
        def sb(name, shape, dt=F32):
            return es.enter_context(nc.sbuf_tensor("D_" + name, shape, dt))

        yfm_t = sb("yfm_t", [128, NCJ * 128], BF16)
        wp_sb = sb("wp_sb", [128, NCJ * D], BF16)
        weff = sb("weff", [128, 8 * D], BF16)
        bp_bc = sb("bp_bc", [128, D], F32)
        bl_bc = sb("bl_bc", [128, D], F32)
        bp_sb = sb("bp_sb", [1, D], F32)
        bl_sb = sb("bl_sb", [1, D], F32)
        xres = sb("xres", [128, D], F32)
        pre = sb("pre", [128, D], F32)
        mean = sb("mean", [128, 1], F32)
        nmean = sb("nmean", [128, 1], F32)
        var = sb("var", [128, 1], F32)
        rstd = sb("rstd", [128, 1], F32)
        xout = sb("xout", [128, D], F32)
        xout_bf = sb("xout_bf", [128, D], BF16)
        ident = sb("ident", [128, 128], BF16)
        onesf = sb("onesf", [1, 128], F32)
        x3fm_t = sb("x3fm_t", [128, 8 * 128], BF16)
        ldT_sb = sb("ldT_sb", [100, 4], BF16)
        da_sb = sb("da_sb", [100, D], BF16)
        luT_sb = sb("luT_sb", [4, 50], BF16)
        uaT_sb = sb("uaT_sb", [50, D], BF16)
        a1_sb = sb("a1_sb", [4, D], BF16)
        a2_sb = sb("a2_sb", [50, D], BF16)

        pb = [es.enter_context(nc.psum_tensor(f"D_pbD{i}", [128, 512], F32)) for i in range(4)]
        ptr = es.enter_context(nc.psum_tensor("D_ptrD", [128, 128], BF16))

        make_ident(sc, ident)
        sc.op("pool", lambda g: g.memset(onesf[:], 1.0), inc="pool")
        init_done = sc.c("pool")

        for j in range(NCJ):
            sc.op("pool", lambda g, j=j: g.dma_start(
                out=wp_sb[:, j * D:(j + 1) * D], in_=wp_d[j * 128:(j + 1) * 128, :]),
                inc="ld", dma=True)
        for nm, dst, srcd in (("ldT", ldT_sb, ldT_d), ("da", da_sb, da_d),
                              ("luT", luT_sb, luT_d), ("uaT", uaT_sb, uaT_d)):
            sc.op("pool", lambda g, dst=dst, srcd=srcd: g.dma_start(out=dst[:], in_=srcd),
                  inc="ld", dma=True)
        bv = sc.op("pool", lambda g: g.dma_start(out=bp_sb[:], in_=bp_d), inc="ld", dma=True)
        bv = sc.op("pool", lambda g: g.dma_start(out=bl_sb[:], in_=bl_d), inc="ld", dma=True)
        ld_w = sc.c("ld")

        evs = []
        for (row, bc) in ((bp_sb, bp_bc), (bl_sb, bl_bc)):
            for oj in range(2):
                mm = sc.op("pe", lambda t, row=row, oj=oj: t.matmul(
                    pb[3][:], onesf[:], row[:, oj * 512:(oj + 1) * 512],
                    start=True, stop=True),
                    waits=[("ld", ld_w), ("pool", init_done)] +
                          ([("act", evs[-1])] if evs else []),
                    inc="pe")
                evs.append(sc.op("act", lambda a, bc=bc, oj=oj: a.copy(
                    out=bc[:, oj * 512:(oj + 1) * 512], in_=pb[3][:]),
                    waits=[("pe", mm)], inc="act"))
        bias_done = evs[-1]

        prev = bias_done
        for oj in range(2 if stage >= 0 else 0):
            mm = sc.op("pe", lambda t, oj=oj: t.matmul(
                pb[3][0:4, :], ldT_sb[:], da_sb[:, oj * 512:(oj + 1) * 512],
                start=True, stop=True),
                waits=[("act", prev)], inc="pe")
            prev = sc.op("act", lambda a, oj=oj: a.copy(
                out=a1_sb[:, oj * 512:(oj + 1) * 512], in_=pb[3][0:4, :]),
                waits=[("pe", mm)], inc="act")
        for oj in range(2 if stage >= 1 else 0):
            mm = sc.op("pe", lambda t, oj=oj: t.matmul(
                pb[3][0:50, :], luT_sb[:], a1_sb[:, oj * 512:(oj + 1) * 512],
                start=True, stop=True),
                waits=[("act", prev)], inc="pe")
            prev = sc.op("act", lambda a, oj=oj: a.copy(
                out=a2_sb[:, oj * 512:(oj + 1) * 512], in_=pb[3][0:50, :]),
                waits=[("pe", mm)], inc="act")
        wl_stage = sb("wl_stage", [128, D], BF16)
        wdone = prev
        dv = None
        for j in range(8 if stage >= 2 else 0):
            lv = sc.op("pool", lambda g, j=j: g.dma_start(
                out=wl_stage[:], in_=wl_d[j * 128:(j + 1) * 128, :]),
                waits=[("dve", sc.c("dve"))], inc="lst", dma=True)
            for oj in range(2):
                mm = sc.op("pe", lambda t, j=j, oj=oj: t.matmul(
                    pb[2][:], a2_sb[:, j * 128:(j + 1) * 128],
                    uaT_sb[:, oj * 512:(oj + 1) * 512],
                    start=True, stop=True),
                    waits=[("act", wdone)] + ([("dve", dv)] if dv else []), inc="pe")
                dv = sc.op("dve", lambda v, j=j, oj=oj: v.tensor_add(
                    out=weff[:, j * D + oj * 512: j * D + (oj + 1) * 512],
                    in0=pb[2][:], in1=wl_stage[:, oj * 512:(oj + 1) * 512]),
                    waits=[("pe", mm), ("lst", lv)], inc="dve")
        weff_done = sc.c("dve")

        NT1 = TPC // 128
        def ln_tile(tt, src_pre, out_dram, fm_dram, fm_tile, extra_store=None):
            v2 = sc.op("dve", lambda v: v.reduce_sum(out=mean[:, 0:1], in_=src_pre, axis=AXL.X),
                       inc="dve")
            v3 = sc.op("dve", lambda v: v.tensor_scalar_mul(
                out=nmean[:, 0:1], in0=mean[:, 0:1], scalar1=-1.0 / D),
                waits=[("dve", v2)], inc="dve")
            a1 = sc.op("act", lambda a: a.activation(
                out=xout[:], in_=src_pre, func=AF.Square, bias=nmean[:, 0:1], scale=1.0,
                accum_out=var[:, 0:1]),
                waits=[("dve", v3), ("st", sc.c("st")), ("st1", sc.c("st1"))], inc="act")
            v4 = sc.op("dve", lambda v: v.tensor_scalar(
                out=var[:, 0:1], in0=var[:, 0:1], scalar1=1.0 / D, scalar2=1e-5,
                op0=ALU.mult, op1=ALU.add),
                waits=[("act", a1)], inc="dve")
            a2 = sc.op("act", lambda a: a.activation(
                out=rstd[:, 0:1], in_=var[:, 0:1], func=AF.Sqrt),
                waits=[("dve", v4)], inc="act")
            v5 = sc.op("dve", lambda v: v.reciprocal(out=rstd[:, 0:1], in_=rstd[:, 0:1]),
                       waits=[("act", a2)], inc="dve")
            v6 = sc.op("dve", lambda v: v.tensor_scalar(
                out=xout[:], in0=src_pre, scalar1=nmean[:, 0:1], scalar2=rstd[:, 0:1],
                op0=ALU.add, op1=ALU.mult),
                waits=[("dve", v5)], inc="dve")
            stsem = "st1" if fm_dram is not None else "st"
            if fm_dram is not None:
                v7 = sc.op("dve", lambda v: v.tensor_copy(out=xout_bf[:], in_=xout[:]),
                           inc="dve")
                for sj in range(8):
                    tv = sc.op("pe", lambda t, sj=sj: t.transpose(
                        ptr[:], xout_bf[:, sj * 128:(sj + 1) * 128], ident[:]),
                        waits=[("dve", sc.c("dve")), ("pool", init_done)], inc="pe")
                    sc.op("dve", lambda v, sj=sj: v.tensor_copy(
                        out=fm_tile[:, sj * 128:(sj + 1) * 128], in_=ptr[:]),
                        waits=[("pe", tv)], inc="dve")
                vlast = sc.c("dve")
                sc.op("sync", lambda e, tt=tt: e.dma_start(
                    out=fm_dram[:, tt * 128:(tt + 1) * 128].rearrange("(j p) t -> p j t", p=128),
                    in_=fm_tile[:].rearrange("p (j t) -> p j t", j=8)),
                    waits=[("dve", vlast)], inc=stsem, dma=True)
            else:
                vlast = v6
            sc.op("sync", lambda e, tt=tt: e.dma_start(
                out=out_dram[tt * 128:(tt + 1) * 128, :], in_=xout[:]),
                waits=[("dve", vlast)], inc=stsem, dma=True)

        for tt in range(NT1 if stage >= 3 else 0):
            xv = sc.op("sync", lambda e, tt=tt: e.dma_start(
                out=xres[:], in_=x_sl[tt * 128:(tt + 1) * 128, :]),
                waits=([("dve", sc.c("dve"))] if tt else []), inc="ld2", dma=True)
            ccw = [("cc", dram["cc_wait"])] if fused else []
            yv = sc.op("sync", lambda e, tt=tt: e.dma_start(
                out=yfm_t[:].rearrange("p (j t) -> p j t", j=NCJ),
                in_=yfm[:, tt * 128:(tt + 1) * 128].rearrange("(j p) t -> p j t", p=128)),
                waits=([("pe", sc.c("pe"))] if tt else []) + ccw, inc="ld2", dma=True)
            mms = []
            for oj in range(2):
                for j in range(NCJ):
                    w = []
                    if j == 0:
                        w = [("ld", ld_w), ("ld2", max(xv, yv))]
                        if tt > 0:
                            w.append(("dve", sc.c("dve")))
                    mm = sc.op("pe", lambda t, oj=oj, j=j: t.matmul(
                        pb[oj][:], yfm_t[:, j * 128:(j + 1) * 128],
                        wp_sb[:, j * D + oj * 512: j * D + (oj + 1) * 512],
                        start=(j == 0), stop=(j == NCJ - 1)),
                        waits=w, inc="pe" if j == NCJ - 1 else None)
                mms.append(mm)
            for oj in range(2):
                sc.op("dve", lambda v, oj=oj: v.tensor_add(
                    out=pre[:, oj * 512:(oj + 1) * 512], in0=pb[oj][:],
                    in1=bp_bc[:, oj * 512:(oj + 1) * 512]),
                    waits=[("pe", mms[oj]), ("act", bias_done)], inc="dve")
            sc.op("dve", lambda v: v.tensor_add(out=pre[:], in0=pre[:], in1=xres[:]),
                  waits=[("ld2", max(xv, yv))], inc="dve")
            ln_tile(tt, pre[:], x3_dram, x3fm_dram, x3fm_t)
        d1_st = sc.c("st1")

        for tt in range(NT1 if stage >= 4 else 0):
            xv = sc.op("sync", lambda e, tt=tt: e.dma_start(
                out=xres[:], in_=x3_dram[tt * 128:(tt + 1) * 128, :]),
                waits=[("st1", d1_st), ("dve", sc.c("dve"))], inc="ld2", dma=True)
            yv = sc.op("sync", lambda e, tt=tt: e.dma_start(
                out=yfm_t[:, 0:1024].rearrange("p (j t) -> p j t", j=8),
                in_=x3fm_dram[:, tt * 128:(tt + 1) * 128].rearrange("(j p) t -> p j t", p=128)),
                waits=[("st1", d1_st), ("pe", sc.c("pe"))], inc="ld2", dma=True)
            mms = []
            for oj in range(2):
                for j in range(8):
                    w = []
                    if j == 0:
                        w = [("dve", weff_done), ("ld2", max(xv, yv))]
                        if tt > 0:
                            w.append(("dve", sc.c("dve")))
                    mm = sc.op("pe", lambda t, oj=oj, j=j: t.matmul(
                        pb[oj][:], yfm_t[:, j * 128:(j + 1) * 128],
                        weff[:, j * D + oj * 512: j * D + (oj + 1) * 512],
                        start=(j == 0), stop=(j == 7)),
                        waits=w, inc="pe" if j == 7 else None)
                mms.append(mm)
            for oj in range(2):
                sc.op("dve", lambda v, oj=oj: v.tensor_add(
                    out=pre[:, oj * 512:(oj + 1) * 512], in0=pb[oj][:],
                    in1=bl_bc[:, oj * 512:(oj + 1) * 512]),
                    waits=[("pe", mms[oj]), ("act", bias_done)], inc="dve")
            sc.op("dve", lambda v: v.tensor_add(out=pre[:], in0=pre[:], in1=xres[:]),
                  waits=[("ld2", max(xv, yv))], inc="dve")
            ln_tile(tt, pre[:], x4o, None, None)
        sc.op("sync", lambda e: None, waits=[("st", sc.c("st"))])
        sc.flush(block)


HEAD_OF_CORE = [(c // 4, 4 * (c % 4)) for c in range(8)]


def kernel_unfused(**inp):
    x = np.asarray(inp["L0_attn_x"])
    Wq = np.asarray(inp["L0_attn_Wq"])
    Wk = np.asarray(inp["L0_attn_Wk"])
    Wv = np.asarray(inp["L0_attn_Wv"])
    bq = np.asarray(inp["L0_attn_bq"])
    bk = np.asarray(inp["L0_attn_bk"])
    bv = np.asarray(inp["L0_attn_bv"])

    nc = build_A()
    in_maps = []
    for c in range(8):
        b, h0 = HEAD_OF_CORE[c]
        sl = slice(h0 * HD, (h0 + HPC) * HD)
        in_maps.append({
            "xfm": np.ascontiguousarray(x[b].T),
            "wqT": np.ascontiguousarray(Wq[sl, :].T),
            "wkT": np.ascontiguousarray(Wk[sl, :].T),
            "wvT": np.ascontiguousarray(Wv[sl, :].T),
            "bqkv": np.ascontiguousarray(np.stack([bq[sl], bk[sl], bv[sl]])),
        })
    res = run_bass_kernel_spmd(nc, in_maps, core_ids=list(range(8)))
    y_fm = [np.concatenate([np.asarray(res.results[4 * b_ + i]["yfm"]) for i in range(4)], axis=0)
            for b_ in range(B)]

    nc = build_B(debug=False)
    in_maps = []
    for c in range(8):
        b, q = c // 4, c % 4
        ts = slice(q * TPC, (q + 1) * TPC)
        in_maps.append({
            "yfm_sl": np.ascontiguousarray(y_fm[b][:, ts]),
            "x_sl": np.ascontiguousarray(x[b][ts, :]),
            "woT": np.ascontiguousarray(np.asarray(inp["L0_attn_Wo"]).T),
            "bo": np.ascontiguousarray(np.asarray(inp["L0_attn_bo"])[None, :]),
            "wgT": np.ascontiguousarray(np.asarray(inp["L0_ffn_w_gate"]).T),
            "wuT": np.ascontiguousarray(np.asarray(inp["L0_ffn_w_up"]).T),
            "wdT": np.ascontiguousarray(np.asarray(inp["L0_ffn_w_down"]).T),
        })
    res = run_bass_kernel_spmd(nc, in_maps, core_ids=list(range(8)))
    x2 = np.zeros((B, T, D), np.float32)
    for c in range(8):
        b, q = c // 4, c % 4
        ts = slice(q * TPC, (q + 1) * TPC)
        x2[b, ts] = np.asarray(res.results[c]["x2o"])

    W_attn = np.asarray(inp["L1_attn_W_attn"])
    b_attn = np.asarray(inp["L1_attn_b_attn"])
    gain = np.asarray(inp["L1_attn_score_gain"]).reshape(1, 1)
    sign = 1.0 if float(gain[0, 0]) >= 0 else -1.0
    nc = build_C(sign=sign)
    in_maps = []
    for c in range(8):
        b, h0 = HEAD_OF_CORE[c]
        cs = slice(h0 * HD, (h0 + HPC) * HD)
        in_maps.append({
            "x2fm": np.ascontiguousarray(x2[b].T),
            "wqC": np.ascontiguousarray(W_attn[:, cs]),
            "wkC": np.ascontiguousarray(W_attn[:, 1024 + h0 * HD: 1024 + (h0 + HPC) * HD]),
            "wvC": np.ascontiguousarray(W_attn[:, 2048 + h0 * HD: 2048 + (h0 + HPC) * HD]),
            "bqkvC": np.ascontiguousarray(np.stack([
                b_attn[cs], b_attn[1024 + h0 * HD: 1024 + (h0 + HPC) * HD],
                b_attn[2048 + h0 * HD: 2048 + (h0 + HPC) * HD]])),
            "gain": np.ascontiguousarray(gain.astype(np.float32)),
        })
    res = run_bass_kernel_spmd(nc, in_maps, core_ids=list(range(8)))
    y1_fm = [np.concatenate([np.asarray(res.results[4 * b_ + i]["yfm"]) for i in range(4)], axis=0)
             for b_ in range(B)]

    nc = build_D()
    in_maps = []
    for c in range(8):
        b, q = c // 4, c % 4
        ts = slice(q * TPC, (q + 1) * TPC)
        in_maps.append({
            "y1fm_sl": np.ascontiguousarray(y1_fm[b][:, ts]),
            "x2_sl": np.ascontiguousarray(x2[b][ts, :]),
            "wproj": np.ascontiguousarray(np.asarray(inp["L1_attn_W_proj"])),
            "bproj": np.ascontiguousarray(np.asarray(inp["L1_attn_b_proj"])[None, :]),
            "wLT": np.ascontiguousarray(np.asarray(inp["L1_ffn_W"]).T),
            "bL": np.ascontiguousarray(np.asarray(inp["L1_ffn_b"])[None, :]),
            "ldT": np.ascontiguousarray(np.asarray(inp["L1_ffn_lora_down"]).T),
            "da": np.ascontiguousarray(np.asarray(inp["L1_ffn_down_aux"])),
            "luT": np.ascontiguousarray(np.asarray(inp["L1_ffn_lora_up"]).T),
            "uaT": np.ascontiguousarray(np.asarray(inp["L1_ffn_up_aux"]).T),
        })
    res = run_bass_kernel_spmd(nc, in_maps, core_ids=list(range(8)))
    out = np.zeros((B, T, D), np.float32)
    for c in range(8):
        b, q = c // 4, c % 4
        ts = slice(q * TPC, (q + 1) * TPC)
        out[b, ts] = np.asarray(res.results[c]["x4o"])
    return out


def build_full(sign=1.0):
    nc = bass.Bass("TRN2", target_bir_lowering=False)
    dram = {}
    dram["xfm"] = nc.dram_tensor("xfm", [D, T], F32, kind="ExternalInput").ap()
    for w in ("q", "k", "v"):
        dram[f"w{w}T"] = nc.dram_tensor(f"w{w}T", [D, HPC * HD], F32, kind="ExternalInput").ap()
    dram["bqkv"] = nc.dram_tensor("bqkv", [3, HPC * HD], F32, kind="ExternalInput").ap()
    dram["x_sl"] = nc.dram_tensor("x_sl", [TPC, D], F32, kind="ExternalInput").ap()
    dram["wo2T"] = nc.dram_tensor("wo2T", [2 * D, D], F32, kind="ExternalInput").ap()
    dram["bo"] = nc.dram_tensor("bo", [1, D], F32, kind="ExternalInput").ap()
    dram["wgT"] = nc.dram_tensor("wgT", [D, 3072], I32, kind="ExternalInput").ap()
    dram["wuT"] = nc.dram_tensor("wuT", [D, 3072], I32, kind="ExternalInput").ap()
    dram["wdT"] = nc.dram_tensor("wdT", [3072, D], I32, kind="ExternalInput").ap()
    for w in ("q", "k", "v"):
        dram[f"w{w}C"] = nc.dram_tensor(f"w{w}C", [D, HPC * HD], F32, kind="ExternalInput").ap()
    dram["bqkvC"] = nc.dram_tensor("bqkvC", [3, HPC * HD], F32, kind="ExternalInput").ap()
    dram["gain"] = nc.dram_tensor("gain", [1, 1], F32, kind="ExternalInput").ap()
    dram["wp2"] = nc.dram_tensor("wp2", [2 * D, D], F32, kind="ExternalInput").ap()
    dram["bproj"] = nc.dram_tensor("bproj", [1, D], F32, kind="ExternalInput").ap()
    dram["wLT"] = nc.dram_tensor("wLT", [D, D], F32, kind="ExternalInput").ap()
    dram["bL"] = nc.dram_tensor("bL", [1, D], F32, kind="ExternalInput").ap()
    dram["ldT"] = nc.dram_tensor("ldT", [100, 4], F32, kind="ExternalInput").ap()
    dram["da"] = nc.dram_tensor("da", [100, D], F32, kind="ExternalInput").ap()
    dram["luT"] = nc.dram_tensor("luT", [4, 50], F32, kind="ExternalInput").ap()
    dram["uaT"] = nc.dram_tensor("uaT", [50, D], F32, kind="ExternalInput").ap()
    dram["x4o"] = nc.dram_tensor("x4o", [TPC, D], F32, kind="ExternalOutput").ap()
    for n, sh, dt in (("x1_scr", [TPC, D], F32), ("x1fm_scr", [D, TPC], BF16),
                      ("x2_scr", [TPC, D], F32), ("x3_scr", [TPC, D], F32),
                      ("x3fm_scr", [D, TPC], BF16),
                      ("a2a1_in", [2048, TPC], BF16), ("a2a1_out", [2048, TPC], BF16),
                      ("ag2_in", [D, TPC], BF16), ("ag2_out", [4 * D, TPC], BF16),
                      ("a2a3_in", [2048, TPC], BF16), ("a2a3_out", [2048, TPC], BF16)):
        dram[n] = nc.dram_tensor(n, sh, dt).ap()
    es0 = ExitStack()
    with es0:
        sc = Sched(nc, es0)
        block = es0.enter_context(nc.Block())
        phase_A(nc, sc, block, dram, fused=True)
        sc.barrier()
        dram["cc_wait"] = 1
        phase_B(nc, sc, block, dram, fused=True)
        sc.barrier()
        dram["cc_wait"] = 2
        phase_C(nc, sc, block, dram, sign, fused=True)
        sc.barrier()
        dram["cc_wait"] = 3
        phase_D(nc, sc, block, dram, fused=True)
    return nc


def kernel(**inp):
    inp = {k: np.asarray(v) for k, v in inp.items()}
    x = inp["L0_attn_x"]
    gain = inp["L1_attn_score_gain"].reshape(1, 1).astype(np.float32)
    sign = 1.0 if float(gain[0, 0]) >= 0 else -1.0
    W_attn = inp["L1_attn_W_attn"]
    b_attn = inp["L1_attn_b_attn"]
    woT = np.ascontiguousarray(inp["L0_attn_Wo"].T)
    wproj = np.ascontiguousarray(inp["L1_attn_W_proj"])
    zeros = np.zeros((D, D), np.float32)
    nc = build_full(sign=sign)
    in_maps = []
    for c in range(8):
        b, h0 = HEAD_OF_CORE[c]
        q = c % 4
        sl = slice(h0 * HD, (h0 + HPC) * HD)
        ts = slice(q * TPC, (q + 1) * TPC)
        wo2 = np.concatenate([woT, zeros], 0) if b == 0 else np.concatenate([zeros, woT], 0)
        wp2 = np.concatenate([wproj, zeros], 0) if b == 0 else np.concatenate([zeros, wproj], 0)
        in_maps.append({
            "xfm": np.ascontiguousarray(x[b].T),
            "wqT": np.ascontiguousarray(inp["L0_attn_Wq"][sl, :].T),
            "wkT": np.ascontiguousarray(inp["L0_attn_Wk"][sl, :].T),
            "wvT": np.ascontiguousarray(inp["L0_attn_Wv"][sl, :].T),
            "bqkv": np.ascontiguousarray(np.stack([
                inp["L0_attn_bq"][sl], inp["L0_attn_bk"][sl], inp["L0_attn_bv"][sl]])),
            "x_sl": np.ascontiguousarray(x[b][ts, :]),
            "wo2T": np.ascontiguousarray(wo2),
            "bo": np.ascontiguousarray(inp["L0_attn_bo"][None, :]),
            "wgT": np.ascontiguousarray(inp["L0_ffn_w_gate"].T),
            "wuT": np.ascontiguousarray(inp["L0_ffn_w_up"].T),
            "wdT": np.ascontiguousarray(inp["L0_ffn_w_down"].T),
            "wqC": np.ascontiguousarray(W_attn[:, sl]),
            "wkC": np.ascontiguousarray(W_attn[:, 1024 + h0 * HD: 1024 + (h0 + HPC) * HD]),
            "wvC": np.ascontiguousarray(W_attn[:, 2048 + h0 * HD: 2048 + (h0 + HPC) * HD]),
            "bqkvC": np.ascontiguousarray(np.stack([
                b_attn[sl], b_attn[1024 + h0 * HD: 1024 + (h0 + HPC) * HD],
                b_attn[2048 + h0 * HD: 2048 + (h0 + HPC) * HD]])),
            "gain": np.ascontiguousarray(gain),
            "wp2": np.ascontiguousarray(wp2),
            "bproj": np.ascontiguousarray(inp["L1_attn_b_proj"][None, :]),
            "wLT": np.ascontiguousarray(inp["L1_ffn_W"].T),
            "bL": np.ascontiguousarray(inp["L1_ffn_b"][None, :]),
            "ldT": np.ascontiguousarray(inp["L1_ffn_lora_down"].T),
            "da": np.ascontiguousarray(inp["L1_ffn_down_aux"]),
            "luT": np.ascontiguousarray(inp["L1_ffn_lora_up"].T),
            "uaT": np.ascontiguousarray(inp["L1_ffn_up_aux"].T),
        })
    res = run_bass_kernel_spmd(nc, in_maps, core_ids=list(range(8)))
    out = np.zeros((B, T, D), np.float32)
    for c in range(8):
        b, _ = HEAD_OF_CORE[c]
        q = c % 4
        out[b, q * TPC:(q + 1) * TPC] = np.asarray(res.results[c]["x4o"])
    return out


# version 04
# speedup vs baseline: 1.4576x; 1.0825x over previous
import numpy as np
import ml_dtypes
from contextlib import ExitStack

import concourse.bass as bass
import concourse.mybir as mybir
from concourse.bass_utils import run_bass_kernel_spmd
from concourse.aot_env import allow_ici_all_to_all

F32 = mybir.dt.float32
BF16 = mybir.dt.bfloat16
I32 = mybir.dt.int32
AF = mybir.ActivationFunctionType
ALU = mybir.AluOpType
AXL = mybir.AxisListType

B = 2
T = 8192
D = 1024
NH = 16
HD = 64
HPC = 4
TPC = 2048
CHUNK = 128
NCHUNK = T // CHUNK


class Sem:
    def __init__(self, h):
        self.h = h
        self.c = 0


class Sched:
    ENGS = ("pe", "act", "dve", "pool", "sync")

    def __init__(self, nc, es):
        self.nc = nc
        self.q = {e: [] for e in ("pe", "act", "dve", "pool", "sync")}
        self.sem = {n: Sem(es.enter_context(nc.semaphore(f"s_{n}")))
                    for n in ("ld", "ld2", "lx0", "lx1", "lst", "st1", "pe", "act",
                              "dve", "pool", "sync", "st", "cc")}
        self.pending = {}

    def op(self, eng, fn, waits=(), inc=None, dma=False):
        s = self.sem[inc] if inc else None
        n = 16 if dma else 1
        val = None
        if s is not None:
            s.c += n
            val = s.c
        if self.pending.get(eng):
            waits = tuple(self.pending.pop(eng)) + tuple(waits)
        waits = tuple((self.sem[w], v) for (w, v) in waits)

        def g(e, fn=fn, waits=waits, s=s, n=n):
            for (ss, v) in waits:
                e.wait_ge(ss.h, v)
            ins = fn(e)
            if s is not None:
                ins.then_inc(s.h, n)
        self.q[eng].append(g)
        return val

    def c(self, name):
        return self.sem[name].c

    def barrier(self):
        snap = [(n, s.c) for n, s in self.sem.items() if s.c > 0]
        for e in self.ENGS:
            self.pending[e] = list(snap)

    def flush(self, block):
        self.emit(block)
        self.q = {e: [] for e in self.ENGS}

    def emit(self, block):
        @block.tensor
        def _(eng):
            for f in self.q["pe"]:
                f(eng)

        @block.scalar
        def _(eng):
            for f in self.q["act"]:
                f(eng)

        @block.vector
        def _(eng):
            for f in self.q["dve"]:
                f(eng)

        @block.gpsimd
        def _(eng):
            for f in self.q["pool"]:
                f(eng)

        @block.sync
        def _(eng):
            for f in self.q["sync"]:
                f(eng)


def make_ident(sc, ident):
    v = sc.op("pool", lambda g: g.memset(ident[:], 1.0), inc="pool")
    return sc.op("pool", lambda g: g.affine_select(
        out=ident[:], in_=ident[:], pattern=[[-1, 128]],
        compare_op=ALU.is_equal, fill=0.0, base=0, channel_multiplier=1),
        waits=[("pool", v)], inc="pool")


def make_triu(sc, triu):
    v = sc.op("pool", lambda g: g.memset(triu[:], 1.0), inc="pool")
    return sc.op("pool", lambda g: g.affine_select(
        out=triu[:], in_=triu[:], pattern=[[1, 128]],
        compare_op=ALU.is_ge, fill=0.0, base=0, channel_multiplier=-1),
        waits=[("pool", v)], inc="pool")


def build_A(debug=False):
    nc = bass.Bass("TRN2", target_bir_lowering=False)
    dram = {}
    dram["xfm"] = nc.dram_tensor("xfm", [D, T], F32, kind="ExternalInput").ap()
    for w in ("q", "k", "v"):
        dram[f"w{w}T"] = nc.dram_tensor(f"w{w}T", [D, HPC * HD], F32, kind="ExternalInput").ap()
    dram["bqkv"] = nc.dram_tensor("bqkv", [3, HPC * HD], F32, kind="ExternalInput").ap()
    dram["yfm"] = nc.dram_tensor("yfm", [HPC * HD, T], BF16, kind="ExternalOutput").ap()
    es0 = ExitStack()
    with es0:
        sc = Sched(nc, es0)
        block = es0.enter_context(nc.Block())
        phase_A(nc, sc, block, dram, fused=False)
    return nc


def phase_A(nc, sc, block, dram, fused):
    xfm = dram["xfm"]
    wT = {w: dram[f"w{w}T"] for w in ("q", "k", "v")}
    bqkv = dram["bqkv"]
    TT = 512
    NTT = T // TT
    es = ExitStack()
    with es:
        def sb(name, shape, dt=F32):
            return es.enter_context(nc.sbuf_tensor("A_" + name, shape, dt))

        def psum(name, shape, dt=F32):
            return es.enter_context(nc.psum_tensor("A_" + name, shape, dt))

        w_sb = {w: sb(f"w{w}", [128, 8 * 256], BF16) for w in ("q", "k", "v")}
        bias_sb = sb("bias", [128, 6], F32)
        x_t = [sb(f"x_t{i}", [128, 8 * TT], BF16) for i in range(2)]
        q_fm = sb("q_fm", [128, 2 * T], BF16)
        k_fm = sb("k_fm", [128, 2 * T], BF16)
        kt = sb("kt", [128, HPC * NCHUNK * HD], BF16)
        vt = sb("vt", [128, HPC * NCHUNK * 65], BF16)
        v_stage = sb("v_stage", [128, TT], BF16)
        phi_m = sb("phi_m", [128, TT], F32)
        phi_e = sb("phi_e", [128, TT], F32)
        phi_r = sb("phi_r", [128, TT], F32)
        masked = [sb(f"masked{i}", [128, 128], BF16) for i in range(2)]
        triu = sb("triu", [128, 128], BF16)
        ident = sb("ident", [128, 128], BF16)
        S_arena = sb("S_arena", [128, 2 * 65], BF16)
        ones1 = sb("ones1", [1, 64], F32)
        rden = sb("rden", [1, 128], F32)
        bcast_sb = sb("bcast_sb", [64, 128], F32)
        y_fm = sb("y_fm", [128, 2 * T], BF16)

        pp = [psum(f"pp{i}", [128, TT]) for i in range(6)]
        ptr = psum("ptr", [128, 128], BF16)
        psc_bank = psum("psc", [128, 512])
        psc = [psc_bank[:, 0:128], psc_bank[:, 128:256]]
        pnum = [pp[0][0:65, 0:128], pp[0][0:65, 128:256]]
        pS_h = [pp[1], pp[3], pp[4], pp[5]]
        pbc = pp[2][0:64, 0:128]

        make_ident(sc, ident)
        make_triu(sc, triu)
        sc.op("pool", lambda g: g.memset(ones1[:], 1.0), inc="pool")
        sc.op("pool", lambda g: g.memset(vt[:], 1.0), inc="pool")
        sc.op("pool", lambda g: g.memset(S_arena[:], 0.0), inc="pool")
        init_done = sc.c("pool")

        for w in ("q", "k", "v"):
            for j in range(8):
                sc.op("pool", lambda g, w=w, j=j: g.dma_start(
                    out=w_sb[w][:, j * 256:(j + 1) * 256],
                    in_=wT[w][j * 128:(j + 1) * 128, :]), inc="ld", dma=True)
        for wi in range(3):
            for p in range(2):
                sc.op("pool", lambda g, wi=wi, p=p: g.dma_start(
                    out=bias_sb[:, wi * 2 + p: wi * 2 + p + 1],
                    in_=bqkv[wi:wi + 1, p * 128:(p + 1) * 128].rearrange("a e -> e a")),
                    inc="ld", dma=True)
        ld_w = sc.c("ld")

        pe_tile_done = {}
        dve_tile_done = {}
        for ti in range(NTT):
            buf = ti % 2
            waits = [("pe", pe_tile_done[ti - 2])] if ti >= 2 else []
            xv = sc.op("pool", lambda g, buf=buf, ti=ti: g.dma_start(
                out=x_t[buf][:].rearrange("p (j t) -> p j t", j=8),
                in_=xfm[:, ti * TT:(ti + 1) * TT].rearrange("(j p) t -> p j t", p=128)),
                waits=waits, inc=f"lx{buf}", dma=True)

            for wi, w in enumerate(("q", "k", "v")):
                for p in range(2):
                    bank = pp[wi * 2 + p]
                    bcol = bias_sb[:, wi * 2 + p: wi * 2 + p + 1]
                    for j in range(8):
                        wts = []
                        if j == 0:
                            wts = [("ld", ld_w), (f"lx{buf}", xv)]
                            if ti > 0:
                                wts.append(("dve", dve_tile_done[ti - 1]))
                        mm = sc.op("pe", lambda t, bank=bank, w=w, j=j, p=p, buf=buf: t.matmul(
                            bank[:],
                            w_sb[w][:, j * 256 + p * 128: j * 256 + (p + 1) * 128],
                            x_t[buf][:, j * TT:(j + 1) * TT],
                            start=(j == 0), stop=(j == 7)),
                            waits=wts, inc="pe" if j == 7 else None)
                    if w in ("q", "k"):
                        dst = q_fm if w == "q" else k_fm
                        dslc = dst[:, p * T + ti * TT: p * T + (ti + 1) * TT]
                        v1 = sc.op("dve", lambda v, bank=bank, bcol=bcol: v.tensor_scalar(
                            out=phi_m[:], in0=bank[:], scalar1=bcol, scalar2=0.0,
                            op0=ALU.add, op1=ALU.min),
                            waits=[("pe", mm), ("ld", ld_w)], inc="dve")
                        a1 = sc.op("act", lambda a: a.activation(
                            out=phi_e[:], in_=phi_m[:], func=AF.Exp),
                            waits=[("dve", v1)], inc="act")
                        sc.op("dve", lambda v, bank=bank, bcol=bcol: v.tensor_scalar(
                            out=phi_r[:], in0=bank[:], scalar1=bcol, scalar2=0.0,
                            op0=ALU.add, op1=ALU.max), inc="dve")
                        v3 = sc.op("dve", lambda v, dslc=dslc: v.tensor_add(
                            out=dslc, in0=phi_e[:], in1=phi_r[:]),
                            waits=[("act", a1)], inc="dve")
                        if w == "k":
                            for sti in range(4):
                                c = ti * 4 + sti
                                src = dst[:, p * T + c * 128:p * T + (c + 1) * 128]
                                tv = sc.op("pe", lambda t, src=src: t.transpose(
                                    ptr[:], src, ident[:]),
                                    waits=[("dve", sc.c("dve")), ("pool", init_done)],
                                    inc="pe")
                                for hh in range(2):
                                    h = 2 * p + hh
                                    sc.op("dve", lambda v, hh=hh, h=h, c=c: v.tensor_copy(
                                        out=kt[:, (h * NCHUNK + c) * 64:(h * NCHUNK + c) * 64 + 64],
                                        in_=ptr[:, hh * 64:(hh + 1) * 64]),
                                        waits=[("pe", tv)], inc="dve")
                    else:
                        vv = sc.op("dve", lambda v, bank=bank, bcol=bcol: v.tensor_scalar_add(
                            out=v_stage[:], in0=bank[:], scalar1=bcol),
                            waits=[("pe", mm), ("ld", ld_w)], inc="dve")
                        for sti in range(4):
                            c = ti * 4 + sti
                            tv = sc.op("pe", lambda t, sti=sti: t.transpose(
                                ptr[:], v_stage[:, sti * 128:(sti + 1) * 128], ident[:]),
                                waits=[("dve", sc.c("dve")), ("pool", init_done)],
                                inc="pe")
                            for hh in range(2):
                                h = 2 * p + hh
                                sc.op("dve", lambda v, hh=hh, h=h, c=c: v.tensor_copy(
                                    out=vt[:, (h * NCHUNK + c) * 65:(h * NCHUNK + c) * 65 + 64],
                                    in_=ptr[:, hh * 64:(hh + 1) * 64]),
                                    waits=[("pe", tv)], inc="dve")
            pe_tile_done[ti] = sc.c("pe")
            dve_tile_done[ti] = sc.c("dve")
        phase1_dve = sc.c("dve")
        phase1_pe = sc.c("pe")

        prev = {}
        NJ = NCHUNK * HPC
        mask_done = [0] * NJ
        num1_done = [0] * NJ
        supd_done = [0] * NJ
        ymult_done = [0] * NJ
        recip_done = [0] * NJ
        sevac_done = [0] * NJ
        bc_done = [0] * NJ
        bcevac_done = [0] * NJ
        for i in range(NCHUNK):
            for h in range(HPC):
                j = i * HPC + h
                par = h % 2
                pair = h // 2
                buf = j % 2
                rowoff = par * 64
                qk_off = pair * T + i * 128
                Sap = S_arena[rowoff:rowoff + 64, pair * 65: pair * 65 + 65]
                pSap = pS_h[h][rowoff:rowoff + 64, 0:65]
                vslc = vt[:, (h * NCHUNK + i) * 65:(h * NCHUNK + i) * 65 + 65]
                kslc = kt[:, (h * NCHUNK + i) * 64:(h * NCHUNK + i) * 64 + 64]

                if j > 0:
                    bcw = [("dve", recip_done[j - 1])]
                    if j >= 2:
                        bcw.append(("act", bcevac_done[j - 2]))
                    bc_done[j - 1] = sc.op("pe", lambda t: t.matmul(
                        pbc, ones1[:], rden[:], start=True, stop=True),
                        waits=bcw, inc="pe")
                wts = [("dve", phase1_dve)]
                if j >= 2:
                    wts.append(("dve", mask_done[j - 2]))
                scd = sc.op("pe", lambda t, rowoff=rowoff, qk_off=qk_off, buf=buf: t.matmul(
                    psc[buf][:],
                    k_fm[rowoff:rowoff + 64, qk_off:qk_off + 128],
                    q_fm[rowoff:rowoff + 64, qk_off:qk_off + 128],
                    start=True, stop=True, tile_position=(rowoff, 0)),
                    waits=wts, inc="pe")
                mwts = [("pe", scd)]
                if j >= 2:
                    mwts.append(("pe", num1_done[j - 2]))
                mask_done[j] = sc.op("dve", lambda v, buf=buf: v.tensor_mul(
                    out=masked[buf][:], in0=psc[buf][:], in1=triu[:]),
                    waits=mwts, inc="dve")
                nwts = [("dve", mask_done[j])]
                if j >= 2:
                    nwts.append(("dve", ymult_done[j - 2]))
                num1_done[j] = sc.op("pe", lambda t, buf=buf, vslc=vslc: t.matmul(
                    pnum[buf], vslc, masked[buf][:], start=True, stop=False),
                    waits=nwts, inc="pe")
                n2wts = []
                if j >= HPC:
                    n2wts = [("act", sevac_done[j - HPC])]
                sc.op("pe", lambda t, buf=buf, Sap=Sap, rowoff=rowoff, qk_off=qk_off: t.matmul(
                    pnum[buf], Sap,
                    q_fm[rowoff:rowoff + 64, qk_off:qk_off + 128],
                    start=False, stop=True, tile_position=(rowoff, 0)),
                    waits=n2wts, inc="pe")
                supd_done[j] = sc.op("pe", lambda t, pSap=pSap, kslc=kslc, vslc=vslc, i=i, rowoff=rowoff: t.matmul(
                    pSap, kslc, vslc, start=(i == 0), stop=(i == NCHUNK - 1),
                    tile_position=(0, rowoff)),
                    inc="pe")
                sevac_done[j] = sc.op("act", lambda a, Sap=Sap, pSap=pSap: a.copy(
                    out=Sap, in_=pSap),
                    waits=[("pe", supd_done[j])], inc="act")
                rwts = [("pe", supd_done[j])]
                if j > 0:
                    rwts.append(("pe", bc_done[j - 1]))
                recip_done[j] = sc.op("dve", lambda v, buf=buf: v.reciprocal(
                    out=rden[:], in_=pnum[buf][64:65, 0:128]),
                    waits=rwts, inc="dve")
                if j > 0:
                    bwts = [("pe", bc_done[j - 1])]
                    if j >= 2:
                        bwts.append(("dve", ymult_done[j - 2]))
                    bcevac_done[j - 1] = sc.op("act", lambda a: a.copy(
                        out=bcast_sb[:], in_=pbc),
                        waits=bwts, inc="act")
                    jj = j - 1
                    pj = (jj // HPC, jj % HPC)
                    ii, hh = pj
                    ydst = y_fm[(hh % 2) * 64:(hh % 2) * 64 + 64,
                                (hh // 2) * T + ii * 128:(hh // 2) * T + (ii + 1) * 128]
                    ymult_done[jj] = sc.op("dve", lambda v, buf2=jj % 2, ydst=ydst: v.tensor_mul(
                        out=ydst, in0=pnum[buf2][0:64, 0:128], in1=bcast_sb[:]),
                        waits=[("act", bcevac_done[jj])], inc="dve")
        j = NJ
        bc_done_last = sc.op("pe", lambda t: t.matmul(
            pbc, ones1[:], rden[:], start=True, stop=True),
            waits=[("dve", recip_done[NJ - 1])], inc="pe")
        bclast = sc.op("act", lambda a: a.copy(out=bcast_sb[:], in_=pbc),
                       waits=[("pe", bc_done_last), ("dve", ymult_done[NJ - 2])], inc="act")
        jj = NJ - 1
        ii, hh = jj // HPC, jj % HPC
        ydst = y_fm[(hh % 2) * 64:(hh % 2) * 64 + 64,
                    (hh // 2) * T + ii * 128:(hh // 2) * T + (ii + 1) * 128]
        yl = sc.op("dve", lambda v: v.tensor_mul(
            out=ydst, in0=pnum[jj % 2][0:64, 0:128], in1=bcast_sb[:]),
            waits=[("act", bclast)], inc="dve")

        if not fused:
            for p in range(2):
                sc.op("sync", lambda e, p=p: e.dma_start(
                    out=dram["yfm"][p * 128:(p + 1) * 128, :], in_=y_fm[:, p * T:(p + 1) * T]),
                    waits=[("dve", yl)], inc="st", dma=True)
            sc.op("sync", lambda e: None, waits=[("st", sc.c("st"))])
        else:
            a2a_in, a2a_out = dram["a2a1_in"], dram["a2a1_out"]
            for jj in range(8):
                for p in range(2):
                    sc.op("sync", lambda e, jj=jj, p=p: e.dma_start(
                        out=a2a_in[jj * 256 + p * 128: jj * 256 + (p + 1) * 128, :],
                        in_=y_fm[:, p * T + (jj % 4) * TPC: p * T + (jj % 4) * TPC + TPC]),
                        waits=[("dve", yl)], inc="st1", dma=True)
            def _a2a(g, a2a_in=a2a_in, a2a_out=a2a_out):
                with allow_ici_all_to_all():
                    return g.collective_compute(
                        "AllToAll", ALU.bypass, replica_groups=[list(range(8))],
                        ins=[a2a_in.opt()], outs=[a2a_out.opt()])
            sc.op("pool", _a2a, waits=[("st1", sc.c("st1"))], inc="cc")
        sc.flush(block)




def build_B(debug=False):
    nc = bass.Bass("TRN2", target_bir_lowering=False)
    dram = {}
    dram["yfm_sl"] = nc.dram_tensor("yfm_sl", [D, TPC], BF16, kind="ExternalInput").ap()
    dram["x_sl"] = nc.dram_tensor("x_sl", [TPC, D], F32, kind="ExternalInput").ap()
    dram["woT"] = nc.dram_tensor("woT", [D, D], F32, kind="ExternalInput").ap()
    dram["bo"] = nc.dram_tensor("bo", [1, D], F32, kind="ExternalInput").ap()
    dram["wgT"] = nc.dram_tensor("wgT", [D, 3072], I32, kind="ExternalInput").ap()
    dram["wuT"] = nc.dram_tensor("wuT", [D, 3072], I32, kind="ExternalInput").ap()
    dram["wdT"] = nc.dram_tensor("wdT", [3072, D], I32, kind="ExternalInput").ap()
    dram["x2o"] = nc.dram_tensor("x2o", [TPC, D], F32, kind="ExternalOutput").ap()
    dram["x1_scr"] = nc.dram_tensor("x1_scr", [TPC, D], F32).ap()
    dram["x1fm_scr"] = nc.dram_tensor("x1fm_scr", [D, TPC], BF16).ap()
    es0 = ExitStack()
    with es0:
        sc = Sched(nc, es0)
        block = es0.enter_context(nc.Block())
        phase_B(nc, sc, block, dram, fused=False)
    return nc


def phase_B(nc, sc, block, dram, fused):
    x_sl = dram["x_sl"]
    woT_d = dram.get("woT")
    bo_d = dram["bo"]
    wgT_d = dram["wgT"]
    wuT_d = dram["wuT"]
    wdT_d = dram["wdT"]
    x1_dram = dram["x1_scr"]
    x1fm_dram = dram["x1fm_scr"]
    if fused:
        yfm = dram["a2a1_out"]
        wo2_d = dram["wo2T"]
        NCJ = 16
        x2_dram = dram["x2_scr"]
        x2fm_dram = dram["ag2_in"]
    else:
        yfm = dram["yfm_sl"]
        wo2_d = dram["woT"]
        NCJ = 8
        x2_dram = None
        x2fm_dram = None
    es = ExitStack()
    with es:
        def sb(name, shape, dt=F32):
            return es.enter_context(nc.sbuf_tensor("B_" + name, shape, dt))

        def psum(name, shape, dt=F32):
            return es.enter_context(nc.psum_tensor("B_" + name, shape, dt))

        bo_bc = sb("bo_bc", [128, D], F32)
        bo_sb = sb("bo_sb", [1, D], F32)
        xres = sb("xres", [128, D], F32)
        pre = sb("pre", [128, D], F32)
        mean = sb("mean", [128, 1], F32)
        nmean = sb("nmean", [128, 1], F32)
        var = sb("var", [128, 1], F32)
        rstd = sb("rstd", [128, 1], F32)
        xout = sb("xout", [128, D], F32)
        xout_bf = sb("xout_bf", [128, D], BF16)
        ident = sb("ident", [128, 128], BF16)
        ones1 = sb("ones1", [1, 128], F32)
        x1fm_t = sb("x1fm_t", [128, 8 * 128], BF16)

        pb = [psum(f"pb{i}", [128, 512]) for i in range(6)]
        ptr = psum("ptr", [128, 128], BF16)

        make_ident(sc, ident)
        sc.op("pool", lambda g: g.memset(ones1[:], 1.0), inc="pool")
        init_done = sc.c("pool")

        es1 = ExitStack()
        with es1:
            yfm_t = es1.enter_context(nc.sbuf_tensor("B_yfm_t", [128, NCJ * 128], BF16))
            woT_sb = es1.enter_context(nc.sbuf_tensor("B_woT_sb", [128, NCJ * D], BF16))
            for j in range(NCJ):
                sc.op("pool", lambda g, j=j: g.dma_start(
                    out=woT_sb[:, j * D:(j + 1) * D],
                    in_=wo2_d[j * 128:(j + 1) * 128, :]), inc="ld", dma=True)
            ld_w = sc.c("ld")

            bv = sc.op("pool", lambda g: g.dma_start(out=bo_sb[:], in_=bo_d), inc="ld", dma=True)
            for oj in range(2):
                mm = sc.op("pe", lambda t, oj=oj: t.matmul(
                    pb[4][:, 0:512], ones1[:], bo_sb[:, oj * 512:(oj + 1) * 512],
                    start=True, stop=True),
                    waits=[("ld", bv), ("pool", init_done)] + ([("act", sc.c("act"))] if oj else []),
                    inc="pe")
                sc.op("act", lambda a, oj=oj: a.copy(
                    out=bo_bc[:, oj * 512:(oj + 1) * 512], in_=pb[4][:, 0:512]),
                    waits=[("pe", mm)], inc="act")
            bo_done = sc.c("act")

            NT1 = TPC // 128
            for tt in range(NT1):
                xv = sc.op("sync", lambda e, tt=tt: e.dma_start(
                    out=xres[:], in_=x_sl[tt * 128:(tt + 1) * 128, :]),
                    waits=([("dve", sc.c("dve"))] if tt else []), inc="ld2", dma=True)
                ccw = [("cc", dram["cc_wait"])] if fused else []
                yv = sc.op("sync", lambda e, tt=tt: e.dma_start(
                    out=yfm_t[:].rearrange("p (j t) -> p j t", j=NCJ),
                    in_=yfm[:, tt * 128:(tt + 1) * 128].rearrange("(j p) t -> p j t", p=128)),
                    waits=([("pe", sc.c("pe"))] if tt else []) + ccw, inc="ld2", dma=True)
                mms = []
                for oj in range(2):
                    for j in range(NCJ):
                        w = []
                        if j == 0:
                            w = [("ld", ld_w), ("ld2", max(xv, yv))]
                            if tt > 0:
                                w.append(("dve", sc.c("dve")))
                        mm = sc.op("pe", lambda t, oj=oj, j=j: t.matmul(
                            pb[oj][:],
                            yfm_t[:, j * 128:(j + 1) * 128],
                            woT_sb[:, j * D + oj * 512: j * D + (oj + 1) * 512],
                            start=(j == 0), stop=(j == NCJ - 1)),
                            waits=w, inc="pe" if j == NCJ - 1 else None)
                    mms.append(mm)
                for oj in range(2):
                    sc.op("dve", lambda v, oj=oj: v.tensor_add(
                        out=pre[:, oj * 512:(oj + 1) * 512], in0=pb[oj][:],
                        in1=bo_bc[:, oj * 512:(oj + 1) * 512]),
                        waits=[("pe", mms[oj]), ("act", bo_done)], inc="dve")
                v1 = sc.op("dve", lambda v: v.tensor_add(out=pre[:], in0=pre[:], in1=xres[:]),
                           waits=[("ld2", max(xv, yv))], inc="dve")
                v2 = sc.op("dve", lambda v: v.reduce_sum(out=mean[:, 0:1], in_=pre[:], axis=AXL.X),
                           inc="dve")
                v3 = sc.op("dve", lambda v: v.tensor_scalar_mul(
                    out=nmean[:, 0:1], in0=mean[:, 0:1], scalar1=-1.0 / D),
                    waits=[("dve", v2)], inc="dve")
                a1 = sc.op("act", lambda a: a.activation(
                    out=xout[:], in_=pre[:], func=AF.Square, bias=nmean[:, 0:1], scale=1.0,
                    accum_out=var[:, 0:1]),
                    waits=[("dve", v3), ("st1", sc.c("st1")), ("st", sc.c("st"))], inc="act")
                v4 = sc.op("dve", lambda v: v.tensor_scalar(
                    out=var[:, 0:1], in0=var[:, 0:1], scalar1=1.0 / D, scalar2=1e-5,
                    op0=ALU.mult, op1=ALU.add),
                    waits=[("act", a1)], inc="dve")
                a2 = sc.op("act", lambda a: a.activation(
                    out=rstd[:, 0:1], in_=var[:, 0:1], func=AF.Sqrt),
                    waits=[("dve", v4)], inc="act")
                v5 = sc.op("dve", lambda v: v.reciprocal(out=rstd[:, 0:1], in_=rstd[:, 0:1]),
                           waits=[("act", a2)], inc="dve")
                v6 = sc.op("dve", lambda v: v.tensor_scalar(
                    out=xout[:], in0=pre[:], scalar1=nmean[:, 0:1], scalar2=rstd[:, 0:1],
                    op0=ALU.add, op1=ALU.mult),
                    waits=[("dve", v5)], inc="dve")
                v7 = sc.op("dve", lambda v: v.tensor_copy(out=xout_bf[:], in_=xout[:]),
                           inc="dve")
                for sj in range(8):
                    tv = sc.op("pe", lambda t, sj=sj: t.transpose(
                        ptr[:], xout_bf[:, sj * 128:(sj + 1) * 128], ident[:]),
                        waits=[("dve", v7), ("pool", init_done)] +
                              ([("dve", sc.c("dve"))] if sj else []), inc="pe")
                    sc.op("dve", lambda v, sj=sj: v.tensor_copy(
                        out=x1fm_t[:, sj * 128:(sj + 1) * 128], in_=ptr[:]),
                        waits=[("pe", tv)], inc="dve")
                vlast = sc.c("dve")
                sc.op("sync", lambda e, tt=tt: e.dma_start(
                    out=x1_dram[tt * 128:(tt + 1) * 128, :], in_=xout[:]),
                    waits=[("dve", vlast)], inc="st1", dma=True)
                sc.op("sync", lambda e, tt=tt: e.dma_start(
                    out=x1fm_dram[:, tt * 128:(tt + 1) * 128].rearrange("(j p) t -> p j t", p=128),
                    in_=x1fm_t[:].rearrange("p (j t) -> p j t", j=8)),
                    waits=[("dve", vlast)], inc="st1", dma=True)

            b1_st = sc.c("st1")
            sc.flush(block)
        es2 = ExitStack()
        es2.__enter__()
        wg_sb = es2.enter_context(nc.sbuf_tensor("B_wg_sb", [128, 8 * 3072], BF16))
        wu_sb = es2.enter_context(nc.sbuf_tensor("B_wu_sb", [128, 8 * 3072], BF16))
        wd_sb = es2.enter_context(nc.sbuf_tensor("B_wd_sb", [128, 24 * D], BF16))
        stage_i = es2.enter_context(nc.sbuf_tensor("B_stage_i", [128, 512], I32))
        h_fm = es2.enter_context(nc.sbuf_tensor("B_h_fm", [128, 24 * 512], BF16))
        silu_t = es2.enter_context(nc.sbuf_tensor("B_silu_t", [128, 512], F32))
        x1fm_blk = es2.enter_context(nc.sbuf_tensor("B_x1fm_blk", [128, 8 * 512], BF16))
        for (wsb, wdr, nch, wid) in ((wg_sb, wgT_d, 8, 3072), (wu_sb, wuT_d, 8, 3072),
                                     (wd_sb, wdT_d, 24, D)):
            for j in range(nch):
                for cc2 in range(wid // 512):
                    lv = sc.op("pool", lambda g, wdr=wdr, j=j, cc2=cc2: g.dma_start(
                        out=stage_i[:], in_=wdr[j * 128:(j + 1) * 128, cc2 * 512:(cc2 + 1) * 512]),
                        waits=[("dve", sc.c("dve"))], inc="lst", dma=True)
                    sc.op("dve", lambda v, wsb=wsb, j=j, wid=wid, cc2=cc2: v.tensor_scalar_mul(
                        out=wsb[:, j * wid + cc2 * 512: j * wid + (cc2 + 1) * 512], in0=stage_i[:],
                        scalar1=1.0 / 127.0),
                        waits=[("lst", lv)], inc="dve")
        dq_done = sc.c("dve")

        NB = TPC // 512
        for bb in range(NB):
            xv = sc.op("sync", lambda e, bb=bb: e.dma_start(
                out=x1fm_blk[:].rearrange("p (j t) -> p j t", j=8),
                in_=x1fm_dram[:, bb * 512:(bb + 1) * 512].rearrange("(j p) t -> p j t", p=128)),
                waits=[("st1", b1_st)] + ([("dve", sc.c("dve"))] if bb else []),
                inc="ld2", dma=True)
            for hj in range(24):
                gmm = umm = None
                for j in range(8):
                    w = []
                    if j == 0:
                        w = [("ld2", xv), ("dve", dq_done)]
                        if hj >= 2:
                            w.append(("dve", sc.c("dve")))
                    gmm = sc.op("pe", lambda t, hj=hj, j=j: t.matmul(
                        pb[hj % 2][:],
                        wg_sb[:, j * 3072 + hj * 128: j * 3072 + (hj + 1) * 128],
                        x1fm_blk[:, j * 512:(j + 1) * 512],
                        start=(j == 0), stop=(j == 7)),
                        waits=w, inc="pe" if j == 7 else None)
                for j in range(8):
                    umm = sc.op("pe", lambda t, hj=hj, j=j: t.matmul(
                        pb[2 + hj % 2][:],
                        wu_sb[:, j * 3072 + hj * 128: j * 3072 + (hj + 1) * 128],
                        x1fm_blk[:, j * 512:(j + 1) * 512],
                        start=(j == 0), stop=(j == 7)),
                        inc="pe" if j == 7 else None)
                a1 = sc.op("act", lambda a, hj=hj: a.activation(
                    out=silu_t[:], in_=pb[hj % 2][:], func=AF.Silu),
                    waits=[("pe", gmm)] + ([("dve", sc.c("dve"))] if hj >= 1 else []),
                    inc="act")
                sc.op("dve", lambda v, hj=hj: v.tensor_mul(
                    out=h_fm[:, hj * 512:(hj + 1) * 512], in0=pb[2 + hj % 2][:],
                    in1=silu_t[:]),
                    waits=[("act", a1), ("pe", umm)], inc="dve")
            hdone = sc.c("dve")
            for st_i in range(4):
                tt = bb * 4 + st_i
                xv2 = sc.op("sync", lambda e, tt=tt: e.dma_start(
                    out=xres[:], in_=x1_dram[tt * 128:(tt + 1) * 128, :]),
                    waits=[("st1", b1_st), ("dve", sc.c("dve"))], inc="ld2", dma=True)
                dmm = []
                for oj in range(2):
                    for hj in range(24):
                        w = []
                        if hj == 0:
                            w = [("dve", hdone)]
                            w.append(("dve", sc.c("dve")))
                        mm = sc.op("pe", lambda t, oj=oj, hj=hj, st_i=st_i: t.matmul(
                            pb[4 + oj][:],
                            h_fm[:, hj * 512 + st_i * 128: hj * 512 + (st_i + 1) * 128],
                            wd_sb[:, hj * D + oj * 512: hj * D + (oj + 1) * 512],
                            start=(hj == 0), stop=(hj == 23)),
                            waits=w, inc="pe" if hj == 23 else None)
                    dmm.append(mm)
                for oj in range(2):
                    sc.op("dve", lambda v, oj=oj: v.tensor_add(
                        out=pre[:, oj * 512:(oj + 1) * 512], in0=pb[4 + oj][:],
                        in1=xres[:, oj * 512:(oj + 1) * 512]),
                        waits=[("pe", dmm[oj]), ("ld2", xv2)], inc="dve")
                v2 = sc.op("dve", lambda v: v.reduce_sum(out=mean[:, 0:1], in_=pre[:], axis=AXL.X),
                           inc="dve")
                v3 = sc.op("dve", lambda v: v.tensor_scalar_mul(
                    out=nmean[:, 0:1], in0=mean[:, 0:1], scalar1=-1.0 / D),
                    waits=[("dve", v2)], inc="dve")
                a1 = sc.op("act", lambda a: a.activation(
                    out=xout[:], in_=pre[:], func=AF.Square, bias=nmean[:, 0:1], scale=1.0,
                    accum_out=var[:, 0:1]),
                    waits=[("dve", v3), ("st", sc.c("st")), ("st1", sc.c("st1"))], inc="act")
                v4 = sc.op("dve", lambda v: v.tensor_scalar(
                    out=var[:, 0:1], in0=var[:, 0:1], scalar1=1.0 / D, scalar2=1e-5,
                    op0=ALU.mult, op1=ALU.add),
                    waits=[("act", a1)], inc="dve")
                a2 = sc.op("act", lambda a: a.activation(
                    out=rstd[:, 0:1], in_=var[:, 0:1], func=AF.Sqrt),
                    waits=[("dve", v4)], inc="act")
                v5 = sc.op("dve", lambda v: v.reciprocal(out=rstd[:, 0:1], in_=rstd[:, 0:1]),
                           waits=[("act", a2)], inc="dve")
                v6 = sc.op("dve", lambda v: v.tensor_scalar(
                    out=xout[:], in0=pre[:], scalar1=nmean[:, 0:1], scalar2=rstd[:, 0:1],
                    op0=ALU.add, op1=ALU.mult),
                    waits=[("dve", v5)], inc="dve")
                if not fused:
                    sc.op("sync", lambda e, tt=tt: e.dma_start(
                        out=dram["x2o"][tt * 128:(tt + 1) * 128, :], in_=xout[:]),
                        waits=[("dve", v6)], inc="st", dma=True)
                else:
                    sc.op("sync", lambda e, tt=tt: e.dma_start(
                        out=x2_dram[tt * 128:(tt + 1) * 128, :], in_=xout[:]),
                        waits=[("dve", v6)], inc="st1", dma=True)
                    v7 = sc.op("dve", lambda v: v.tensor_copy(out=xout_bf[:], in_=xout[:]),
                               waits=[("dve", v6)], inc="dve")
                    for sj in range(8):
                        tv = sc.op("pe", lambda t, sj=sj: t.transpose(
                            ptr[:], xout_bf[:, sj * 128:(sj + 1) * 128], ident[:]),
                            waits=[("dve", sc.c("dve"))], inc="pe")
                        sc.op("dve", lambda v, sj=sj: v.tensor_copy(
                            out=x1fm_t[:, sj * 128:(sj + 1) * 128], in_=ptr[:]),
                            waits=[("pe", tv)], inc="dve")
                    vl2 = sc.c("dve")
                    sc.op("sync", lambda e, tt=tt: e.dma_start(
                        out=x2fm_dram[:, tt * 128:(tt + 1) * 128].rearrange("(j p) t -> p j t", p=128),
                        in_=x1fm_t[:].rearrange("p (j t) -> p j t", j=8)),
                        waits=[("dve", vl2)], inc="st1", dma=True)
        if not fused:
            sc.op("sync", lambda e: None, waits=[("st", sc.c("st"))])
        else:
            sc.op("pool", lambda g, ai=dram["ag2_in"], ao=dram["ag2_out"]: g.collective_compute(
                "AllGather", ALU.bypass, replica_groups=[[0, 1, 2, 3], [4, 5, 6, 7]],
                ins=[ai.opt()], outs=[ao.opt()]),
                waits=[("st1", sc.c("st1"))], inc="cc")
        sc.flush(block)
        es2.__exit__(None, None, None)




def build_C(sign=1.0):
    nc = bass.Bass("TRN2", target_bir_lowering=False)
    dram = {}
    dram["x2fm"] = nc.dram_tensor("x2fm", [D, T], F32, kind="ExternalInput").ap()
    for w in ("q", "k", "v"):
        dram[f"w{w}C"] = nc.dram_tensor(f"w{w}C", [D, HPC * HD], F32, kind="ExternalInput").ap()
    dram["bqkvC"] = nc.dram_tensor("bqkvC", [3, HPC * HD], F32, kind="ExternalInput").ap()
    dram["gain"] = nc.dram_tensor("gain", [1, 1], F32, kind="ExternalInput").ap()
    dram["yfmC"] = nc.dram_tensor("yfm", [HPC * HD, T], BF16, kind="ExternalOutput").ap()
    es0 = ExitStack()
    with es0:
        sc = Sched(nc, es0)
        block = es0.enter_context(nc.Block())
        phase_C(nc, sc, block, dram, sign, fused=False)
    return nc


def phase_C(nc, sc, block, dram, sign, fused):
    wC = {w: dram[f"w{w}C"] for w in ("q", "k", "v")}
    bqkv = dram["bqkvC"]
    gain_d = dram["gain"]
    xfm = None if fused else dram["x2fm"]
    ag2_out = dram.get("ag2_out")

    TT = 512
    NTT = T // TT
    NQS = T // 128
    NQT = T // 512
    es = ExitStack()
    with es:
        def sb(name, shape, dt=F32):
            return es.enter_context(nc.sbuf_tensor("C_" + name, shape, dt))

        bias_sb = sb("bias", [128, 6], F32)
        q_fm = sb("q_fm", [128, 2 * T], BF16)
        k_fm = sb("k_fm", [128, 2 * T], BF16)
        vt = sb("vt", [128, HPC * NCHUNK * 65], BF16)
        v_stage = sb("v_stage", [128, TT], F32)
        ident = sb("ident", [128, 128], BF16)
        identf = sb("identf", [128, 128], F32)
        onesc = sb("onesc", [65, 128], BF16)
        onesf = sb("onesf", [1, 128], F32)
        ksum = sb("ksum", [128, 2], F32)
        ksumb = sb("ksumb", [128, 2], BF16)
        negmu = [sb(f"negmu{h}", [128, NQS]) for h in range(HPC)]
        absum = [sb(f"absum{h}", [128, NQS]) for h in range(HPC)]
        epsc = [sb(f"epsc{h}", [128, NQS]) for h in range(HPC)]
        acc8 = [sb(f"acc8_{i}", [128, 8], F32) for i in range(2)]
        uabs = sb("uabs", [128, 1024], BF16)
        uabs_d = [sb(f"uabs_d{i}", [128, 1024], BF16) for i in range(2)]
        gaincol = sb("gaincol", [128, 1], F32)
        g4inv = sb("g4inv", [128, 1], F32)
        scr = sb("scr", [128, 2], F32)
        madt = sb("madt", [128, NQS], F32)
        m2t = sb("m2t", [128, NQS], F32)
        u_t = [sb(f"u_t{i}", [128, 1024], BF16) for i in range(3)]
        r_t = [sb(f"r_t{i}", [128, 1024], BF16) for i in range(3)]
        dmask = [sb(f"dmask{i}", [128, 512], BF16) for i in range(4)]
        negmurow = [sb(f"negmurow{i}", [1, 512], BF16) for i in range(2)]
        epsrow = [sb(f"epsrow{i}", [1, 512], F32) for i in range(2)]
        denrow = [sb(f"denrow{i}", [1, 512], F32) for i in range(2)]
        rden = [sb(f"rdenC{i}", [1, 512], F32) for i in range(2)]
        bcast_sb = [sb(f"bcastC{i}", [64, 512], F32) for i in range(2)]
        y_fm = sb("y_fmC", [128, 2 * T], BF16)

        pa = es.enter_context(nc.psum_tensor("C_pa", [128, 2048], F32))
        pa2 = es.enter_context(nc.psum_tensor("C_pa2", [128, 2048], F32))
        ptrf = pa2[:, 1536:1664]
        pproj = [pa[:, i * 512:(i + 1) * 512] for i in range(4)] + \
                [pa2[:, i * 512:(i + 1) * 512] for i in range(2)]
        psF = {0: [pa[:, 0:1024], pa[:, 1024:2048]],
               1: [pa2[:, 0:1024], pa2[:, 1024:2048]]}
        psc2 = [pa[:, 0:1024], pa[:, 1024:2048]]
        py = {0: pa2[0:65, 0:512], 1: pa2[0:65, 512:1024]}
        prow = {0: pa2[:, 1024:1536], 1: pa2[:, 1536:2048]}
        pmu = prow[0]

        make_ident(sc, ident)
        v = sc.op("pool", lambda g: g.memset(identf[:], 1.0), inc="pool")
        sc.op("pool", lambda g: g.affine_select(
            out=identf[:], in_=identf[:], pattern=[[-1, 128]],
            compare_op=ALU.is_equal, fill=0.0, base=0, channel_multiplier=1),
            waits=[("pool", v)], inc="pool")
        sc.op("pool", lambda g: g.memset(onesc[:], 1.0), inc="pool")
        sc.op("pool", lambda g: g.memset(onesf[:], 1.0), inc="pool")
        sc.op("pool", lambda g: g.memset(vt[:], 1.0), inc="pool")
        for j in range(4):
            v = sc.op("pool", lambda g, j=j: g.memset(dmask[j][:], 1.0), inc="pool")
            v = sc.op("pool", lambda g, j=j: g.affine_select(
                out=dmask[j][:], in_=dmask[j][:], pattern=[[1, 512]],
                compare_op=ALU.is_ge, fill=0.0, base=-128 * j, channel_multiplier=-1),
                waits=[("pool", v)], inc="pool")
        init_done = sc.c("pool")

        for w in ("q", "k", "v"):
            for j in range(8):
                sc.op("pool", lambda g, w=w, j=j: g.dma_start(
                    out=w_sb[w][:, j * 256:(j + 1) * 256],
                    in_=wC[w][j * 128:(j + 1) * 128, :]), inc="ld", dma=True)
        for wi in range(3):
            for p in range(2):
                sc.op("pool", lambda g, wi=wi, p=p: g.dma_start(
                    out=bias_sb[:, wi * 2 + p: wi * 2 + p + 1],
                    in_=bqkv[wi:wi + 1, p * 128:(p + 1) * 128].rearrange("a e -> e a")),
                    inc="ld", dma=True)
        gv = sc.op("pool", lambda g: g.dma_start(out=scr[0:1, 0:1], in_=gain_d),
                   inc="ld", dma=True)
        ld_w = sc.c("ld")

        gmm = sc.op("pe", lambda t: t.matmul(
            pmu[:, 0:1], onesf[:], scr[0:1, 0:1], start=True, stop=True),
            waits=[("ld", gv), ("pool", init_done)], inc="pe")
        a_g = sc.op("act", lambda a: a.copy(out=gaincol[:, 0:1], in_=pmu[:, 0:1]),
                    waits=[("pe", gmm)], inc="act")
        v1 = sc.op("dve", lambda v: v.tensor_mul(out=scr[:, 0:1], in0=gaincol[:, 0:1],
                                                 in1=gaincol[:, 0:1]),
                   waits=[("act", a_g)], inc="dve")
        v2 = sc.op("dve", lambda v: v.tensor_mul(out=scr[:, 1:2], in0=scr[:, 0:1],
                                                 in1=scr[:, 0:1]),
                   waits=[("dve", v1)], inc="dve")
        g4_done = sc.op("dve", lambda v: v.reciprocal(out=g4inv[:, 0:1], in_=scr[:, 1:2]),
                        waits=[("dve", v2)], inc="dve")

        es1 = ExitStack()
        with es1:
            w_sb = {w: es1.enter_context(nc.sbuf_tensor("C_w" + w, [128, 8 * 256], BF16)) for w in ("q", "k", "v")}
            x_t = [es1.enter_context(nc.sbuf_tensor(f"C_x_t{i}", [128, 8 * TT], BF16)) for i in range(2)]
            pe_tile_done = {}
            dve_tile_done = {}
            for ti in range(NTT):
                buf = ti % 2
                waits = [("pe", pe_tile_done[ti - 2])] if ti >= 2 else []
                if fused:
                    rr = ti // 4
                    t0 = (ti % 4) * TT
                    xv = sc.op("pool", lambda g, buf=buf, rr=rr, t0=t0: g.dma_start(
                        out=x_t[buf][:].rearrange("p (j t) -> p j t", j=8),
                        in_=ag2_out[rr * D:(rr + 1) * D, t0:t0 + TT].rearrange("(j p) t -> p j t", p=128)),
                        waits=waits + [("cc", dram["cc_wait"])], inc=f"lx{buf}", dma=True)
                else:
                    xv = sc.op("pool", lambda g, buf=buf, ti=ti: g.dma_start(
                        out=x_t[buf][:].rearrange("p (j t) -> p j t", j=8),
                        in_=xfm[:, ti * TT:(ti + 1) * TT].rearrange("(j p) t -> p j t", p=128)),
                        waits=waits, inc=f"lx{buf}", dma=True)
                for wi, w in enumerate(("q", "k", "v")):
                    for p in range(2):
                        bank = pproj[wi * 2 + p]
                        bcol = bias_sb[:, wi * 2 + p: wi * 2 + p + 1]
                        for j in range(8):
                            wts = []
                            if j == 0:
                                wts = [("ld", ld_w), (f"lx{buf}", xv)]
                                if ti > 0:
                                    wts.append(("dve", dve_tile_done[ti - 1]))
                                if ti == 0 and wi == 1 and p == 0:
                                    pass
                            mm = sc.op("pe", lambda t, bank=bank, w=w, j=j, p=p, buf=buf: t.matmul(
                                bank, w_sb[w][:, j * 256 + p * 128: j * 256 + (p + 1) * 128],
                                x_t[buf][:, j * TT:(j + 1) * TT],
                                start=(j == 0), stop=(j == 7)),
                                waits=wts, inc="pe" if j == 7 else None)
                        if w == "q":
                            dst = q_fm[:, p * T + ti * TT: p * T + (ti + 1) * TT]
                            sc.op("dve", lambda v, bank=bank, bcol=bcol, dst=dst: v.tensor_scalar(
                                out=dst, in0=bank, scalar1=bcol, scalar2=float(sign),
                                op0=ALU.add, op1=ALU.mult),
                                waits=[("pe", mm), ("ld", ld_w)], inc="dve")
                        elif w == "k":
                            dst = k_fm[:, p * T + ti * TT: p * T + (ti + 1) * TT]
                            sc.op("dve", lambda v, bank=bank, bcol=bcol, dst=dst: v.tensor_scalar_add(
                                out=dst, in0=bank, scalar1=bcol),
                                waits=[("pe", mm), ("ld", ld_w)], inc="dve")
                        else:
                            sc.op("dve", lambda v, bank=bank, bcol=bcol: v.tensor_scalar_add(
                                out=v_stage[:], in0=bank, scalar1=bcol),
                                waits=[("pe", mm), ("ld", ld_w)], inc="dve")
                            for sti in range(4):
                                c = ti * 4 + sti
                                tv = sc.op("pe", lambda t, sti=sti: t.transpose(
                                    ptrf, v_stage[:, sti * 128:(sti + 1) * 128], identf[:]),
                                    waits=[("dve", sc.c("dve")), ("pool", init_done)], inc="pe")
                                for hh in range(2):
                                    h = 2 * p + hh
                                    sc.op("dve", lambda v, hh=hh, h=h, c=c: v.tensor_copy(
                                        out=vt[:, (h * NCHUNK + c) * 65:(h * NCHUNK + c) * 65 + 64],
                                        in_=ptrf[:, hh * 64:(hh + 1) * 64]),
                                        waits=[("pe", tv)], inc="dve")
                pe_tile_done[ti] = sc.c("pe")
                dve_tile_done[ti] = sc.c("dve")

            sc.flush(block)
        proj_dve = sc.c("dve")

        kv = {}
        for p in range(2):
            r0 = sc.op("dve", lambda v, p=p: v.reduce_sum(
                out=ksum[:, p:p + 1], in_=k_fm[:, p * T:(p + 1) * T], axis=AXL.X),
                waits=[("dve", proj_dve)], inc="dve")
            kv[p] = sc.op("dve", lambda v, p=p: v.tensor_copy(
                out=ksumb[:, p:p + 1], in_=ksum[:, p:p + 1]),
                waits=[("dve", r0)], inc="dve")

        mu_done = {}
        for h in range(HPC):
            par, pair = h % 2, h // 2
            ro = par * 64
            for qs in range(NQS):
                w = []
                if qs == 0:
                    w = [("dve", kv[pair]), ("act", a_g)]
                    if h > 0:
                        w.append(("dve", mu_done[h - 1]))
                mm = sc.op("pe", lambda t, ro=ro, pair=pair, qs=qs: t.matmul(
                    pmu[:, qs:qs + 1],
                    q_fm[ro:ro + 64, pair * T + qs * 128: pair * T + (qs + 1) * 128],
                    ksumb[ro:ro + 64, pair:pair + 1],
                    start=True, stop=True, tile_position=(ro, 0)),
                    waits=w, inc="pe" if qs == NQS - 1 else None)
            sc.op("dve", lambda v, h=h: v.tensor_scalar_mul(
                out=negmu[h][:], in0=pmu[:, 0:NQS], scalar1=-1.0 / T),
                waits=[("pe", mm)], inc="dve")
            mu_done[h] = sc.c("dve")
        mu_all = sc.c("dve")

        DVE_KQ = (2, 5)
        eps_done = {}
        bank_last = {}
        for pair in range(2):
            hA, hB = 2 * pair, 2 * pair + 1
            prev_reduce = None
            for qs in range(NQS):
                for kq in range(8):
                    half = kq % 2
                    for a_i, (h, ro) in enumerate(((hA, 0), (hB, 64))):
                        for kk in range(2):
                            kt0 = (kq * 2 + kk) * 512
                            w = []
                            if kk == 0:
                                bk = (a_i, half)
                                if bk in bank_last:
                                    w.append(bank_last[bk])
                                if pair == 0 and qs == 0 and kq < 2:
                                    w.append(("dve", mu_all))
                            mm = sc.op("pe", lambda t, ro=ro, pair=pair, qs=qs, kt0=kt0,
                                       a_i=a_i, half=half, kk=kk: t.matmul(
                                psF[a_i][half][:, kk * 512:(kk + 1) * 512],
                                q_fm[ro:ro + 64, pair * T + qs * 128: pair * T + (qs + 1) * 128],
                                k_fm[ro:ro + 64, pair * T + kt0: pair * T + kt0 + 512],
                                start=True, stop=True, tile_position=(ro, 0)),
                                waits=w, inc="pe" if kk == 1 else None)
                        h_ = hA if a_i == 0 else hB
                        if kq in DVE_KQ:
                            dv1 = sc.op("dve", lambda v, a_i=a_i, half=half, h_=h_, qs=qs: v.tensor_scalar(
                                out=uabs_d[a_i][:], in0=psF[a_i][half],
                                scalar1=negmu[h_][:, qs:qs + 1], scalar2=None,
                                op0=ALU.add, op1=ALU.bypass),
                                waits=[("pe", mm)], inc="dve")
                            bank_last[(a_i, half)] = ("dve", dv1)
                            sc.op("dve", lambda v, a_i=a_i, kq=kq: v.reduce_sum(
                                out=acc8[a_i][:, kq:kq + 1], in_=uabs_d[a_i][:],
                                axis=AXL.X, apply_absolute_value=True),
                                inc="dve")
                        else:
                            aw = [("pe", mm)]
                            if prev_reduce is not None and kq == 0:
                                aw.append(("dve", prev_reduce))
                            bank_last[(a_i, half)] = ("act", sc.op(
                                "act", lambda a, a_i=a_i, half=half, h_=h_, qs=qs, kq=kq: a.activation(
                                    out=uabs[:], in_=psF[a_i][half],
                                    func=AF.Abs, bias=negmu[h_][:, qs:qs + 1], scale=1.0,
                                    accum_out=acc8[a_i][:, kq:kq + 1]),
                                waits=aw, inc="act"))
                rs = None
                for a_i, h_ in ((0, hA), (1, hB)):
                    la = bank_last[(a_i, 0)]
                    lb = bank_last[(a_i, 1)]
                    rs = sc.op("dve", lambda v, a_i=a_i, h_=h_, qs=qs: v.reduce_sum(
                        out=absum[h_][:, qs:qs + 1], in_=acc8[a_i][:], axis=AXL.X),
                        waits=[la, lb], inc="dve")
                prev_reduce = rs
            for h_ in (hA, hB):
                w1 = sc.op("dve", lambda v, h_=h_: v.tensor_scalar(
                    out=madt[:], in0=absum[h_][:], scalar1=1.0 / T, scalar2=1e-6,
                    op0=ALU.mult, op1=ALU.add), waits=[("dve", prev_reduce)], inc="dve")
                w2 = sc.op("dve", lambda v: v.tensor_mul(out=m2t[:], in0=madt[:], in1=madt[:]),
                           inc="dve")
                w3 = sc.op("dve", lambda v: v.tensor_mul(out=madt[:], in0=m2t[:], in1=m2t[:]),
                           inc="dve")
                eps_done[h_] = sc.op("dve", lambda v, h_=h_: v.tensor_scalar(
                    out=epsc[h_][:], in0=madt[:], scalar1=g4inv[:, 0:1], scalar2=1e-6,
                    op0=ALU.mult, op1=ALU.mult),
                    waits=[("dve", g4_done)], inc="dve")
        stats_done = sc.c("dve")
        stats_act = sc.c("act")

        ymult_done = 0
        prev_pair_last = {}
        relu_done = {}
        u_free = {}
        r_free = {}
        bc_free = {}
        prev_avmm = {}
        for pair in range(2):
            hA, hB = 2 * pair, 2 * pair + 1
            for qt in range(NQT):
                nrows = {}
                for a_i, h_ in ((0, hA), (1, hB)):
                    ro = (h_ % 2) * 64
                    for jj in range(4):
                        qs = qt * 4 + jj
                        w = [("dve", stats_done)] if jj == 0 else []
                        key = ("rowbank", a_i)
                        if jj == 0 and key in prev_pair_last:
                            w.append(("act", prev_pair_last[key]))
                        if jj == 0 and a_i in prev_avmm:
                            w.append(("pe", prev_avmm[a_i]))
                        tv = sc.op("pe", lambda t, a_i=a_i, h_=h_, qs=qs, jj=jj: t.transpose(
                            prow[a_i][0:1, jj * 128:(jj + 1) * 128],
                            negmu[h_][:, qs:qs + 1], identf[:]),
                            waits=w, inc="pe" if jj == 3 else None)
                    nrows[a_i] = sc.op("act", lambda a, a_i=a_i: a.copy(
                        out=negmurow[a_i][:], in_=prow[a_i][0:1, 0:512]),
                        waits=[("pe", tv)], inc="act")
                    prev_pair_last[("rowbank", a_i)] = nrows[a_i]
                kcmax = 4 * qt + 4
                avmm = {}
                roA, roB = (hA % 2) * 64, (hB % 2) * 64

                def sched_av(kc, kcmax=kcmax, hA=hA, hB=hB):
                    buf3 = kc % 3
                    for a_i, h_ in ((0, hA), (1, hB)):
                        w = [("dve", r2_done[buf3])]
                        if kc == 0:
                            w.append(("dve", ymult_done))
                        avmm[a_i] = sc.op("pe", lambda t, a_i=a_i, h_=h_, kc=kc, buf3=buf3, kcmax=kcmax: t.matmul(
                            py[a_i],
                            vt[:, (h_ * NCHUNK + kc) * 65:(h_ * NCHUNK + kc) * 65 + 65],
                            r_t[buf3][:, a_i * 512:(a_i + 1) * 512],
                            start=(kc == 0), stop=(kc == kcmax - 1)),
                            waits=w, inc="pe")
                        r_free[buf3] = avmm[a_i]

                r2_done = {}
                for kc in range(kcmax):
                    buf = kc % 2
                    w = []
                    if "relu" in relu_done:
                        pass
                    if (buf,) in relu_done:
                        w.append(("act", relu_done[(buf,)]))
                    else:
                        w.append(("act", nrows[1]))
                    for a_i, (h_, ro) in enumerate(((hA, roA), (hB, roB))):
                        sc.op("pe", lambda t, buf=buf, a_i=a_i, ro=ro, pair=pair, kc=kc, qt=qt: t.matmul(
                            psc2[buf][:, a_i * 512:(a_i + 1) * 512],
                            k_fm[ro:ro + 64, pair * T + kc * 128: pair * T + (kc + 1) * 128],
                            q_fm[ro:ro + 64, pair * T + qt * 512: pair * T + (qt + 1) * 512],
                            start=True, stop=False, tile_position=(ro, 0)),
                            waits=(w if a_i == 0 else []), inc="pe")
                    for a_i in (0, 1):
                        bmm = sc.op("pe", lambda t, buf=buf, a_i=a_i: t.matmul(
                            psc2[buf][:, a_i * 512:(a_i + 1) * 512],
                            onesc[0:1, :], negmurow[a_i][:],
                            start=False, stop=True, tile_position=(0, 0)),
                            inc="pe")
                    if kc >= 2:
                        sched_av(kc - 2)
                    buf3 = kc % 3
                    rv = sc.op("act", lambda a, buf=buf, buf3=buf3: a.activation(
                        out=u_t[buf3][:], in_=psc2[buf][:], func=AF.Relu),
                        waits=[("pe", bmm)] +
                              ([("dve", u_free[buf3])] if buf3 in u_free else []),
                        inc="act")
                    relu_done[(buf,)] = rv
                    d1 = sc.op("dve", lambda v, buf3=buf3: v.tensor_mul(
                        out=u_t[buf3][:], in0=u_t[buf3][:], in1=u_t[buf3][:]),
                        waits=[("act", rv)], inc="dve")
                    d2 = sc.op("dve", lambda v, buf3=buf3: v.tensor_mul(
                        out=r_t[buf3][:], in0=u_t[buf3][:], in1=u_t[buf3][:]),
                        waits=([("pe", r_free[buf3])] if buf3 in r_free else []),
                        inc="dve")
                    u_free[buf3] = d2
                    if kc >= 4 * qt:
                        for hh in range(2):
                            d2 = sc.op("dve", lambda v, buf3=buf3, kc=kc, qt=qt, hh=hh: v.tensor_mul(
                                out=r_t[buf3][:, hh * 512:(hh + 1) * 512],
                                in0=r_t[buf3][:, hh * 512:(hh + 1) * 512],
                                in1=dmask[kc - 4 * qt][:]),
                                inc="dve")
                    r2_done[buf3] = d2
                if kcmax >= 2:
                    sched_av(kcmax - 2)
                sched_av(kcmax - 1)
                for a_i, h_ in ((0, hA), (1, hB)):
                    ro = (h_ % 2) * 64
                    for jj in range(4):
                        qs = qt * 4 + jj
                        w = [("pe", avmm[a_i])] if jj == 0 else []
                        tv = sc.op("pe", lambda t, a_i=a_i, h_=h_, qs=qs, jj=jj: t.transpose(
                            prow[a_i][0:1, jj * 128:(jj + 1) * 128],
                            epsc[h_][:, qs:qs + 1], identf[:]),
                            waits=w, inc="pe" if jj == 3 else None)
                    ev = sc.op("act", lambda a, a_i=a_i: a.copy(
                        out=epsrow[a_i][:], in_=prow[a_i][0:1, 0:512]),
                        waits=[("pe", tv)], inc="act")
                    d1 = sc.op("dve", lambda v, a_i=a_i: v.tensor_add(
                        out=denrow[a_i][:], in0=py[a_i][64:65, 0:512], in1=epsrow[a_i][:]),
                        waits=[("act", ev), ("pe", avmm[a_i])], inc="dve")
                    d2 = sc.op("dve", lambda v, a_i=a_i: v.reciprocal(
                        out=rden[a_i][:], in_=denrow[a_i][:]),
                        waits=[("dve", d1)], inc="dve")
                    bmm = sc.op("pe", lambda t, a_i=a_i: t.matmul(
                        prow[a_i][0:64, 0:512], onesf[0:1, 0:64], rden[a_i][:],
                        start=True, stop=True),
                        waits=[("dve", d2), ("act", ev)], inc="pe")
                    bev = sc.op("act", lambda a, a_i=a_i: a.copy(
                        out=bcast_sb[a_i][:], in_=prow[a_i][0:64, 0:512]),
                        waits=[("pe", bmm)] +
                              ([("dve", bc_free[a_i])] if a_i in bc_free else []),
                        inc="act")
                    prev_pair_last[("rowbank", a_i)] = bev
                    ydst = y_fm[ro:ro + 64, (h_ // 2) * T + qt * 512:(h_ // 2) * T + (qt + 1) * 512]
                    ym = sc.op("dve", lambda v, a_i=a_i, ydst=ydst: v.tensor_mul(
                        out=ydst, in0=py[a_i][0:64, 0:512], in1=bcast_sb[a_i][:]),
                        waits=[("act", bev)], inc="dve")
                    bc_free[a_i] = ym
                    ymult_done = ym
                prev_avmm = dict(avmm)

        yl = sc.c("dve")
        if not fused:
            for p in range(2):
                sc.op("sync", lambda e, p=p: e.dma_start(
                    out=dram["yfmC"][p * 128:(p + 1) * 128, :], in_=y_fm[:, p * T:(p + 1) * T]),
                    waits=[("dve", yl)], inc="st", dma=True)
            sc.op("sync", lambda e: None, waits=[("st", sc.c("st"))])
        else:
            a2a_in, a2a_out = dram["a2a3_in"], dram["a2a3_out"]
            for jj in range(8):
                for p in range(2):
                    sc.op("sync", lambda e, jj=jj, p=p: e.dma_start(
                        out=a2a_in[jj * 256 + p * 128: jj * 256 + (p + 1) * 128, :],
                        in_=y_fm[:, p * T + (jj % 4) * TPC: p * T + (jj % 4) * TPC + TPC]),
                        waits=[("dve", yl)], inc="st1", dma=True)

            def _a2a(g, a2a_in=a2a_in, a2a_out=a2a_out):
                with allow_ici_all_to_all():
                    return g.collective_compute(
                        "AllToAll", ALU.bypass, replica_groups=[list(range(8))],
                        ins=[a2a_in.opt()], outs=[a2a_out.opt()])
            sc.op("pool", _a2a, waits=[("st1", sc.c("st1"))], inc="cc")
        sc.flush(block)




def build_D(stage=4):
    nc = bass.Bass("TRN2", target_bir_lowering=False)
    dram = {}
    dram["y1fm_sl"] = nc.dram_tensor("y1fm_sl", [D, TPC], BF16, kind="ExternalInput").ap()
    dram["x2_sl"] = nc.dram_tensor("x2_sl", [TPC, D], F32, kind="ExternalInput").ap()
    dram["wproj"] = nc.dram_tensor("wproj", [D, D], F32, kind="ExternalInput").ap()
    dram["bproj"] = nc.dram_tensor("bproj", [1, D], F32, kind="ExternalInput").ap()
    dram["wLT"] = nc.dram_tensor("wLT", [D, D], F32, kind="ExternalInput").ap()
    dram["bL"] = nc.dram_tensor("bL", [1, D], F32, kind="ExternalInput").ap()
    dram["ldT"] = nc.dram_tensor("ldT", [100, 4], F32, kind="ExternalInput").ap()
    dram["da"] = nc.dram_tensor("da", [100, D], F32, kind="ExternalInput").ap()
    dram["luT"] = nc.dram_tensor("luT", [4, 50], F32, kind="ExternalInput").ap()
    dram["uaT"] = nc.dram_tensor("uaT", [50, D], F32, kind="ExternalInput").ap()
    dram["x4o"] = nc.dram_tensor("x4o", [TPC, D], F32, kind="ExternalOutput").ap()
    dram["x3_scr"] = nc.dram_tensor("x3_scr", [TPC, D], F32).ap()
    dram["x3fm_scr"] = nc.dram_tensor("x3fm_scr", [D, TPC], BF16).ap()
    es0 = ExitStack()
    with es0:
        sc = Sched(nc, es0)
        block = es0.enter_context(nc.Block())
        phase_D(nc, sc, block, dram, fused=False, stage=stage)
    return nc


def phase_D(nc, sc, block, dram, fused, stage=4):
    bp_d = dram["bproj"]
    wl_d = dram["wLT"]
    bl_d = dram["bL"]
    ldT_d = dram["ldT"]
    da_d = dram["da"]
    luT_d = dram["luT"]
    uaT_d = dram["uaT"]
    x4o = dram["x4o"]
    x3_dram = dram["x3_scr"]
    x3fm_dram = dram["x3fm_scr"]
    if fused:
        yfm = dram["a2a3_out"]
        wp_d = dram["wp2"]
        NCJ = 16
        x_sl = dram["x2_scr"]
    else:
        yfm = dram["y1fm_sl"]
        wp_d = dram["wproj"]
        NCJ = 8
        x_sl = dram["x2_sl"]

    es = ExitStack()
    with es:
        def sb(name, shape, dt=F32):
            return es.enter_context(nc.sbuf_tensor("D_" + name, shape, dt))

        yfm_t = sb("yfm_t", [128, NCJ * 128], BF16)
        wp_sb = sb("wp_sb", [128, NCJ * D], BF16)
        weff = sb("weff", [128, 8 * D], BF16)
        bp_bc = sb("bp_bc", [128, D], F32)
        bl_bc = sb("bl_bc", [128, D], F32)
        bp_sb = sb("bp_sb", [1, D], F32)
        bl_sb = sb("bl_sb", [1, D], F32)
        xres = sb("xres", [128, D], F32)
        pre = sb("pre", [128, D], F32)
        mean = sb("mean", [128, 1], F32)
        nmean = sb("nmean", [128, 1], F32)
        var = sb("var", [128, 1], F32)
        rstd = sb("rstd", [128, 1], F32)
        xout = sb("xout", [128, D], F32)
        xout_bf = sb("xout_bf", [128, D], BF16)
        ident = sb("ident", [128, 128], BF16)
        onesf = sb("onesf", [1, 128], F32)
        x3fm_t = sb("x3fm_t", [128, 8 * 128], BF16)
        ldT_sb = sb("ldT_sb", [100, 4], BF16)
        da_sb = sb("da_sb", [100, D], BF16)
        luT_sb = sb("luT_sb", [4, 50], BF16)
        uaT_sb = sb("uaT_sb", [50, D], BF16)
        a1_sb = sb("a1_sb", [4, D], BF16)
        a2_sb = sb("a2_sb", [50, D], BF16)

        pb = [es.enter_context(nc.psum_tensor(f"D_pbD{i}", [128, 512], F32)) for i in range(4)]
        ptr = es.enter_context(nc.psum_tensor("D_ptrD", [128, 128], BF16))

        make_ident(sc, ident)
        sc.op("pool", lambda g: g.memset(onesf[:], 1.0), inc="pool")
        init_done = sc.c("pool")

        for j in range(NCJ):
            sc.op("pool", lambda g, j=j: g.dma_start(
                out=wp_sb[:, j * D:(j + 1) * D], in_=wp_d[j * 128:(j + 1) * 128, :]),
                inc="ld", dma=True)
        for nm, dst, srcd in (("ldT", ldT_sb, ldT_d), ("da", da_sb, da_d),
                              ("luT", luT_sb, luT_d), ("uaT", uaT_sb, uaT_d)):
            sc.op("pool", lambda g, dst=dst, srcd=srcd: g.dma_start(out=dst[:], in_=srcd),
                  inc="ld", dma=True)
        bv = sc.op("pool", lambda g: g.dma_start(out=bp_sb[:], in_=bp_d), inc="ld", dma=True)
        bv = sc.op("pool", lambda g: g.dma_start(out=bl_sb[:], in_=bl_d), inc="ld", dma=True)
        ld_w = sc.c("ld")

        evs = []
        for (row, bc) in ((bp_sb, bp_bc), (bl_sb, bl_bc)):
            for oj in range(2):
                mm = sc.op("pe", lambda t, row=row, oj=oj: t.matmul(
                    pb[3][:], onesf[:], row[:, oj * 512:(oj + 1) * 512],
                    start=True, stop=True),
                    waits=[("ld", ld_w), ("pool", init_done)] +
                          ([("act", evs[-1])] if evs else []),
                    inc="pe")
                evs.append(sc.op("act", lambda a, bc=bc, oj=oj: a.copy(
                    out=bc[:, oj * 512:(oj + 1) * 512], in_=pb[3][:]),
                    waits=[("pe", mm)], inc="act"))
        bias_done = evs[-1]

        prev = bias_done
        for oj in range(2 if stage >= 0 else 0):
            mm = sc.op("pe", lambda t, oj=oj: t.matmul(
                pb[3][0:4, :], ldT_sb[:], da_sb[:, oj * 512:(oj + 1) * 512],
                start=True, stop=True),
                waits=[("act", prev)], inc="pe")
            prev = sc.op("act", lambda a, oj=oj: a.copy(
                out=a1_sb[:, oj * 512:(oj + 1) * 512], in_=pb[3][0:4, :]),
                waits=[("pe", mm)], inc="act")
        for oj in range(2 if stage >= 1 else 0):
            mm = sc.op("pe", lambda t, oj=oj: t.matmul(
                pb[3][0:50, :], luT_sb[:], a1_sb[:, oj * 512:(oj + 1) * 512],
                start=True, stop=True),
                waits=[("act", prev)], inc="pe")
            prev = sc.op("act", lambda a, oj=oj: a.copy(
                out=a2_sb[:, oj * 512:(oj + 1) * 512], in_=pb[3][0:50, :]),
                waits=[("pe", mm)], inc="act")
        wl_stage = sb("wl_stage", [128, D], BF16)
        wdone = prev
        dv = None
        for j in range(8 if stage >= 2 else 0):
            lv = sc.op("pool", lambda g, j=j: g.dma_start(
                out=wl_stage[:], in_=wl_d[j * 128:(j + 1) * 128, :]),
                waits=[("dve", sc.c("dve"))], inc="lst", dma=True)
            for oj in range(2):
                mm = sc.op("pe", lambda t, j=j, oj=oj: t.matmul(
                    pb[2][:], a2_sb[:, j * 128:(j + 1) * 128],
                    uaT_sb[:, oj * 512:(oj + 1) * 512],
                    start=True, stop=True),
                    waits=[("act", wdone)] + ([("dve", dv)] if dv else []), inc="pe")
                dv = sc.op("dve", lambda v, j=j, oj=oj: v.tensor_add(
                    out=weff[:, j * D + oj * 512: j * D + (oj + 1) * 512],
                    in0=pb[2][:], in1=wl_stage[:, oj * 512:(oj + 1) * 512]),
                    waits=[("pe", mm), ("lst", lv)], inc="dve")
        weff_done = sc.c("dve")

        NT1 = TPC // 128
        def ln_tile(tt, src_pre, out_dram, fm_dram, fm_tile, extra_store=None):
            v2 = sc.op("dve", lambda v: v.reduce_sum(out=mean[:, 0:1], in_=src_pre, axis=AXL.X),
                       inc="dve")
            v3 = sc.op("dve", lambda v: v.tensor_scalar_mul(
                out=nmean[:, 0:1], in0=mean[:, 0:1], scalar1=-1.0 / D),
                waits=[("dve", v2)], inc="dve")
            a1 = sc.op("act", lambda a: a.activation(
                out=xout[:], in_=src_pre, func=AF.Square, bias=nmean[:, 0:1], scale=1.0,
                accum_out=var[:, 0:1]),
                waits=[("dve", v3), ("st", sc.c("st")), ("st1", sc.c("st1"))], inc="act")
            v4 = sc.op("dve", lambda v: v.tensor_scalar(
                out=var[:, 0:1], in0=var[:, 0:1], scalar1=1.0 / D, scalar2=1e-5,
                op0=ALU.mult, op1=ALU.add),
                waits=[("act", a1)], inc="dve")
            a2 = sc.op("act", lambda a: a.activation(
                out=rstd[:, 0:1], in_=var[:, 0:1], func=AF.Sqrt),
                waits=[("dve", v4)], inc="act")
            v5 = sc.op("dve", lambda v: v.reciprocal(out=rstd[:, 0:1], in_=rstd[:, 0:1]),
                       waits=[("act", a2)], inc="dve")
            v6 = sc.op("dve", lambda v: v.tensor_scalar(
                out=xout[:], in0=src_pre, scalar1=nmean[:, 0:1], scalar2=rstd[:, 0:1],
                op0=ALU.add, op1=ALU.mult),
                waits=[("dve", v5)], inc="dve")
            stsem = "st1" if fm_dram is not None else "st"
            if fm_dram is not None:
                v7 = sc.op("dve", lambda v: v.tensor_copy(out=xout_bf[:], in_=xout[:]),
                           inc="dve")
                for sj in range(8):
                    tv = sc.op("pe", lambda t, sj=sj: t.transpose(
                        ptr[:], xout_bf[:, sj * 128:(sj + 1) * 128], ident[:]),
                        waits=[("dve", sc.c("dve")), ("pool", init_done)], inc="pe")
                    sc.op("dve", lambda v, sj=sj: v.tensor_copy(
                        out=fm_tile[:, sj * 128:(sj + 1) * 128], in_=ptr[:]),
                        waits=[("pe", tv)], inc="dve")
                vlast = sc.c("dve")
                sc.op("sync", lambda e, tt=tt: e.dma_start(
                    out=fm_dram[:, tt * 128:(tt + 1) * 128].rearrange("(j p) t -> p j t", p=128),
                    in_=fm_tile[:].rearrange("p (j t) -> p j t", j=8)),
                    waits=[("dve", vlast)], inc=stsem, dma=True)
            else:
                vlast = v6
            sc.op("sync", lambda e, tt=tt: e.dma_start(
                out=out_dram[tt * 128:(tt + 1) * 128, :], in_=xout[:]),
                waits=[("dve", vlast)], inc=stsem, dma=True)

        for tt in range(NT1 if stage >= 3 else 0):
            xv = sc.op("sync", lambda e, tt=tt: e.dma_start(
                out=xres[:], in_=x_sl[tt * 128:(tt + 1) * 128, :]),
                waits=([("dve", sc.c("dve"))] if tt else []), inc="ld2", dma=True)
            ccw = [("cc", dram["cc_wait"])] if fused else []
            yv = sc.op("sync", lambda e, tt=tt: e.dma_start(
                out=yfm_t[:].rearrange("p (j t) -> p j t", j=NCJ),
                in_=yfm[:, tt * 128:(tt + 1) * 128].rearrange("(j p) t -> p j t", p=128)),
                waits=([("pe", sc.c("pe"))] if tt else []) + ccw, inc="ld2", dma=True)
            mms = []
            for oj in range(2):
                for j in range(NCJ):
                    w = []
                    if j == 0:
                        w = [("ld", ld_w), ("ld2", max(xv, yv))]
                        if tt > 0:
                            w.append(("dve", sc.c("dve")))
                    mm = sc.op("pe", lambda t, oj=oj, j=j: t.matmul(
                        pb[oj][:], yfm_t[:, j * 128:(j + 1) * 128],
                        wp_sb[:, j * D + oj * 512: j * D + (oj + 1) * 512],
                        start=(j == 0), stop=(j == NCJ - 1)),
                        waits=w, inc="pe" if j == NCJ - 1 else None)
                mms.append(mm)
            for oj in range(2):
                sc.op("dve", lambda v, oj=oj: v.tensor_add(
                    out=pre[:, oj * 512:(oj + 1) * 512], in0=pb[oj][:],
                    in1=bp_bc[:, oj * 512:(oj + 1) * 512]),
                    waits=[("pe", mms[oj]), ("act", bias_done)], inc="dve")
            sc.op("dve", lambda v: v.tensor_add(out=pre[:], in0=pre[:], in1=xres[:]),
                  waits=[("ld2", max(xv, yv))], inc="dve")
            ln_tile(tt, pre[:], x3_dram, x3fm_dram, x3fm_t)
        d1_st = sc.c("st1")

        for tt in range(NT1 if stage >= 4 else 0):
            xv = sc.op("sync", lambda e, tt=tt: e.dma_start(
                out=xres[:], in_=x3_dram[tt * 128:(tt + 1) * 128, :]),
                waits=[("st1", d1_st), ("dve", sc.c("dve"))], inc="ld2", dma=True)
            yv = sc.op("sync", lambda e, tt=tt: e.dma_start(
                out=yfm_t[:, 0:1024].rearrange("p (j t) -> p j t", j=8),
                in_=x3fm_dram[:, tt * 128:(tt + 1) * 128].rearrange("(j p) t -> p j t", p=128)),
                waits=[("st1", d1_st), ("pe", sc.c("pe"))], inc="ld2", dma=True)
            mms = []
            for oj in range(2):
                for j in range(8):
                    w = []
                    if j == 0:
                        w = [("dve", weff_done), ("ld2", max(xv, yv))]
                        if tt > 0:
                            w.append(("dve", sc.c("dve")))
                    mm = sc.op("pe", lambda t, oj=oj, j=j: t.matmul(
                        pb[oj][:], yfm_t[:, j * 128:(j + 1) * 128],
                        weff[:, j * D + oj * 512: j * D + (oj + 1) * 512],
                        start=(j == 0), stop=(j == 7)),
                        waits=w, inc="pe" if j == 7 else None)
                mms.append(mm)
            for oj in range(2):
                sc.op("dve", lambda v, oj=oj: v.tensor_add(
                    out=pre[:, oj * 512:(oj + 1) * 512], in0=pb[oj][:],
                    in1=bl_bc[:, oj * 512:(oj + 1) * 512]),
                    waits=[("pe", mms[oj]), ("act", bias_done)], inc="dve")
            sc.op("dve", lambda v: v.tensor_add(out=pre[:], in0=pre[:], in1=xres[:]),
                  waits=[("ld2", max(xv, yv))], inc="dve")
            ln_tile(tt, pre[:], x4o, None, None)
        sc.op("sync", lambda e: None, waits=[("st", sc.c("st"))])
        sc.flush(block)


HEAD_OF_CORE = [(c // 4, 4 * (c % 4)) for c in range(8)]


def kernel_unfused(**inp):
    x = np.asarray(inp["L0_attn_x"])
    Wq = np.asarray(inp["L0_attn_Wq"])
    Wk = np.asarray(inp["L0_attn_Wk"])
    Wv = np.asarray(inp["L0_attn_Wv"])
    bq = np.asarray(inp["L0_attn_bq"])
    bk = np.asarray(inp["L0_attn_bk"])
    bv = np.asarray(inp["L0_attn_bv"])

    nc = build_A()
    in_maps = []
    for c in range(8):
        b, h0 = HEAD_OF_CORE[c]
        sl = slice(h0 * HD, (h0 + HPC) * HD)
        in_maps.append({
            "xfm": np.ascontiguousarray(x[b].T),
            "wqT": np.ascontiguousarray(Wq[sl, :].T),
            "wkT": np.ascontiguousarray(Wk[sl, :].T),
            "wvT": np.ascontiguousarray(Wv[sl, :].T),
            "bqkv": np.ascontiguousarray(np.stack([bq[sl], bk[sl], bv[sl]])),
        })
    res = run_bass_kernel_spmd(nc, in_maps, core_ids=list(range(8)))
    y_fm = [np.concatenate([np.asarray(res.results[4 * b_ + i]["yfm"]) for i in range(4)], axis=0)
            for b_ in range(B)]

    nc = build_B(debug=False)
    in_maps = []
    for c in range(8):
        b, q = c // 4, c % 4
        ts = slice(q * TPC, (q + 1) * TPC)
        in_maps.append({
            "yfm_sl": np.ascontiguousarray(y_fm[b][:, ts]),
            "x_sl": np.ascontiguousarray(x[b][ts, :]),
            "woT": np.ascontiguousarray(np.asarray(inp["L0_attn_Wo"]).T),
            "bo": np.ascontiguousarray(np.asarray(inp["L0_attn_bo"])[None, :]),
            "wgT": np.ascontiguousarray(np.asarray(inp["L0_ffn_w_gate"]).T),
            "wuT": np.ascontiguousarray(np.asarray(inp["L0_ffn_w_up"]).T),
            "wdT": np.ascontiguousarray(np.asarray(inp["L0_ffn_w_down"]).T),
        })
    res = run_bass_kernel_spmd(nc, in_maps, core_ids=list(range(8)))
    x2 = np.zeros((B, T, D), np.float32)
    for c in range(8):
        b, q = c // 4, c % 4
        ts = slice(q * TPC, (q + 1) * TPC)
        x2[b, ts] = np.asarray(res.results[c]["x2o"])

    W_attn = np.asarray(inp["L1_attn_W_attn"])
    b_attn = np.asarray(inp["L1_attn_b_attn"])
    gain = np.asarray(inp["L1_attn_score_gain"]).reshape(1, 1)
    sign = 1.0 if float(gain[0, 0]) >= 0 else -1.0
    nc = build_C(sign=sign)
    in_maps = []
    for c in range(8):
        b, h0 = HEAD_OF_CORE[c]
        cs = slice(h0 * HD, (h0 + HPC) * HD)
        in_maps.append({
            "x2fm": np.ascontiguousarray(x2[b].T),
            "wqC": np.ascontiguousarray(W_attn[:, cs]),
            "wkC": np.ascontiguousarray(W_attn[:, 1024 + h0 * HD: 1024 + (h0 + HPC) * HD]),
            "wvC": np.ascontiguousarray(W_attn[:, 2048 + h0 * HD: 2048 + (h0 + HPC) * HD]),
            "bqkvC": np.ascontiguousarray(np.stack([
                b_attn[cs], b_attn[1024 + h0 * HD: 1024 + (h0 + HPC) * HD],
                b_attn[2048 + h0 * HD: 2048 + (h0 + HPC) * HD]])),
            "gain": np.ascontiguousarray(gain.astype(np.float32)),
        })
    res = run_bass_kernel_spmd(nc, in_maps, core_ids=list(range(8)))
    y1_fm = [np.concatenate([np.asarray(res.results[4 * b_ + i]["yfm"]) for i in range(4)], axis=0)
             for b_ in range(B)]

    nc = build_D()
    in_maps = []
    for c in range(8):
        b, q = c // 4, c % 4
        ts = slice(q * TPC, (q + 1) * TPC)
        in_maps.append({
            "y1fm_sl": np.ascontiguousarray(y1_fm[b][:, ts]),
            "x2_sl": np.ascontiguousarray(x2[b][ts, :]),
            "wproj": np.ascontiguousarray(np.asarray(inp["L1_attn_W_proj"])),
            "bproj": np.ascontiguousarray(np.asarray(inp["L1_attn_b_proj"])[None, :]),
            "wLT": np.ascontiguousarray(np.asarray(inp["L1_ffn_W"]).T),
            "bL": np.ascontiguousarray(np.asarray(inp["L1_ffn_b"])[None, :]),
            "ldT": np.ascontiguousarray(np.asarray(inp["L1_ffn_lora_down"]).T),
            "da": np.ascontiguousarray(np.asarray(inp["L1_ffn_down_aux"])),
            "luT": np.ascontiguousarray(np.asarray(inp["L1_ffn_lora_up"]).T),
            "uaT": np.ascontiguousarray(np.asarray(inp["L1_ffn_up_aux"]).T),
        })
    res = run_bass_kernel_spmd(nc, in_maps, core_ids=list(range(8)))
    out = np.zeros((B, T, D), np.float32)
    for c in range(8):
        b, q = c // 4, c % 4
        ts = slice(q * TPC, (q + 1) * TPC)
        out[b, ts] = np.asarray(res.results[c]["x4o"])
    return out


def build_full(sign=1.0):
    nc = bass.Bass("TRN2", target_bir_lowering=False)
    dram = {}
    dram["xfm"] = nc.dram_tensor("xfm", [D, T], F32, kind="ExternalInput").ap()
    for w in ("q", "k", "v"):
        dram[f"w{w}T"] = nc.dram_tensor(f"w{w}T", [D, HPC * HD], F32, kind="ExternalInput").ap()
    dram["bqkv"] = nc.dram_tensor("bqkv", [3, HPC * HD], F32, kind="ExternalInput").ap()
    dram["x_sl"] = nc.dram_tensor("x_sl", [TPC, D], F32, kind="ExternalInput").ap()
    dram["wo2T"] = nc.dram_tensor("wo2T", [2 * D, D], F32, kind="ExternalInput").ap()
    dram["bo"] = nc.dram_tensor("bo", [1, D], F32, kind="ExternalInput").ap()
    dram["wgT"] = nc.dram_tensor("wgT", [D, 3072], I32, kind="ExternalInput").ap()
    dram["wuT"] = nc.dram_tensor("wuT", [D, 3072], I32, kind="ExternalInput").ap()
    dram["wdT"] = nc.dram_tensor("wdT", [3072, D], I32, kind="ExternalInput").ap()
    for w in ("q", "k", "v"):
        dram[f"w{w}C"] = nc.dram_tensor(f"w{w}C", [D, HPC * HD], F32, kind="ExternalInput").ap()
    dram["bqkvC"] = nc.dram_tensor("bqkvC", [3, HPC * HD], F32, kind="ExternalInput").ap()
    dram["gain"] = nc.dram_tensor("gain", [1, 1], F32, kind="ExternalInput").ap()
    dram["wp2"] = nc.dram_tensor("wp2", [2 * D, D], F32, kind="ExternalInput").ap()
    dram["bproj"] = nc.dram_tensor("bproj", [1, D], F32, kind="ExternalInput").ap()
    dram["wLT"] = nc.dram_tensor("wLT", [D, D], F32, kind="ExternalInput").ap()
    dram["bL"] = nc.dram_tensor("bL", [1, D], F32, kind="ExternalInput").ap()
    dram["ldT"] = nc.dram_tensor("ldT", [100, 4], F32, kind="ExternalInput").ap()
    dram["da"] = nc.dram_tensor("da", [100, D], F32, kind="ExternalInput").ap()
    dram["luT"] = nc.dram_tensor("luT", [4, 50], F32, kind="ExternalInput").ap()
    dram["uaT"] = nc.dram_tensor("uaT", [50, D], F32, kind="ExternalInput").ap()
    dram["x4o"] = nc.dram_tensor("x4o", [TPC, D], F32, kind="ExternalOutput").ap()
    for n, sh, dt in (("x1_scr", [TPC, D], F32), ("x1fm_scr", [D, TPC], BF16),
                      ("x2_scr", [TPC, D], F32), ("x3_scr", [TPC, D], F32),
                      ("x3fm_scr", [D, TPC], BF16),
                      ("a2a1_in", [2048, TPC], BF16), ("a2a1_out", [2048, TPC], BF16),
                      ("ag2_in", [D, TPC], BF16), ("ag2_out", [4 * D, TPC], BF16),
                      ("a2a3_in", [2048, TPC], BF16), ("a2a3_out", [2048, TPC], BF16)):
        dram[n] = nc.dram_tensor(n, sh, dt).ap()
    es0 = ExitStack()
    with es0:
        sc = Sched(nc, es0)
        block = es0.enter_context(nc.Block())
        phase_A(nc, sc, block, dram, fused=True)
        sc.barrier()
        dram["cc_wait"] = 1
        phase_B(nc, sc, block, dram, fused=True)
        sc.barrier()
        dram["cc_wait"] = 2
        phase_C(nc, sc, block, dram, sign, fused=True)
        sc.barrier()
        dram["cc_wait"] = 3
        phase_D(nc, sc, block, dram, fused=True)
    return nc


def kernel(**inp):
    inp = {k: np.asarray(v) for k, v in inp.items()}
    x = inp["L0_attn_x"]
    gain = inp["L1_attn_score_gain"].reshape(1, 1).astype(np.float32)
    sign = 1.0 if float(gain[0, 0]) >= 0 else -1.0
    W_attn = inp["L1_attn_W_attn"]
    b_attn = inp["L1_attn_b_attn"]
    woT = np.ascontiguousarray(inp["L0_attn_Wo"].T)
    wproj = np.ascontiguousarray(inp["L1_attn_W_proj"])
    zeros = np.zeros((D, D), np.float32)
    nc = build_full(sign=sign)
    in_maps = []
    for c in range(8):
        b, h0 = HEAD_OF_CORE[c]
        q = c % 4
        sl = slice(h0 * HD, (h0 + HPC) * HD)
        ts = slice(q * TPC, (q + 1) * TPC)
        wo2 = np.concatenate([woT, zeros], 0) if b == 0 else np.concatenate([zeros, woT], 0)
        wp2 = np.concatenate([wproj, zeros], 0) if b == 0 else np.concatenate([zeros, wproj], 0)
        in_maps.append({
            "xfm": np.ascontiguousarray(x[b].T),
            "wqT": np.ascontiguousarray(inp["L0_attn_Wq"][sl, :].T),
            "wkT": np.ascontiguousarray(inp["L0_attn_Wk"][sl, :].T),
            "wvT": np.ascontiguousarray(inp["L0_attn_Wv"][sl, :].T),
            "bqkv": np.ascontiguousarray(np.stack([
                inp["L0_attn_bq"][sl], inp["L0_attn_bk"][sl], inp["L0_attn_bv"][sl]])),
            "x_sl": np.ascontiguousarray(x[b][ts, :]),
            "wo2T": np.ascontiguousarray(wo2),
            "bo": np.ascontiguousarray(inp["L0_attn_bo"][None, :]),
            "wgT": np.ascontiguousarray(inp["L0_ffn_w_gate"].T),
            "wuT": np.ascontiguousarray(inp["L0_ffn_w_up"].T),
            "wdT": np.ascontiguousarray(inp["L0_ffn_w_down"].T),
            "wqC": np.ascontiguousarray(W_attn[:, sl]),
            "wkC": np.ascontiguousarray(W_attn[:, 1024 + h0 * HD: 1024 + (h0 + HPC) * HD]),
            "wvC": np.ascontiguousarray(W_attn[:, 2048 + h0 * HD: 2048 + (h0 + HPC) * HD]),
            "bqkvC": np.ascontiguousarray(np.stack([
                b_attn[sl], b_attn[1024 + h0 * HD: 1024 + (h0 + HPC) * HD],
                b_attn[2048 + h0 * HD: 2048 + (h0 + HPC) * HD]])),
            "gain": np.ascontiguousarray(gain),
            "wp2": np.ascontiguousarray(wp2),
            "bproj": np.ascontiguousarray(inp["L1_attn_b_proj"][None, :]),
            "wLT": np.ascontiguousarray(inp["L1_ffn_W"].T),
            "bL": np.ascontiguousarray(inp["L1_ffn_b"][None, :]),
            "ldT": np.ascontiguousarray(inp["L1_ffn_lora_down"].T),
            "da": np.ascontiguousarray(inp["L1_ffn_down_aux"]),
            "luT": np.ascontiguousarray(inp["L1_ffn_lora_up"].T),
            "uaT": np.ascontiguousarray(inp["L1_ffn_up_aux"].T),
        })
    res = run_bass_kernel_spmd(nc, in_maps, core_ids=list(range(8)))
    out = np.zeros((B, T, D), np.float32)
    for c in range(8):
        b, _ = HEAD_OF_CORE[c]
        q = c % 4
        out[b, q * TPC:(q + 1) * TPC] = np.asarray(res.results[c]["x4o"])
    return out


# version 05
# speedup vs baseline: 1.4890x; 1.0216x over previous
import numpy as np
import ml_dtypes
from contextlib import ExitStack

import concourse.bass as bass
import concourse.mybir as mybir
from concourse.bass_utils import run_bass_kernel_spmd
from concourse.aot_env import allow_ici_all_to_all

F32 = mybir.dt.float32
BF16 = mybir.dt.bfloat16
I32 = mybir.dt.int32
AF = mybir.ActivationFunctionType
ALU = mybir.AluOpType
AXL = mybir.AxisListType

B = 2
T = 8192
D = 1024
NH = 16
HD = 64
HPC = 4
TPC = 2048
CHUNK = 128
NCHUNK = T // CHUNK


class Sem:
    def __init__(self, h):
        self.h = h
        self.c = 0


class Sched:
    ENGS = ("pe", "act", "dve", "pool", "sync")

    def __init__(self, nc, es):
        self.nc = nc
        self.q = {e: [] for e in ("pe", "act", "dve", "pool", "sync")}
        self.sem = {n: Sem(es.enter_context(nc.semaphore(f"s_{n}")))
                    for n in ("ld", "ld2", "lx0", "lx1", "lst", "st1", "pe", "act",
                              "dve", "pool", "sync", "st", "cc")}
        self.pending = {}

    def op(self, eng, fn, waits=(), inc=None, dma=False):
        s = self.sem[inc] if inc else None
        n = 16 if dma else 1
        val = None
        if s is not None:
            s.c += n
            val = s.c
        if self.pending.get(eng):
            waits = tuple(self.pending.pop(eng)) + tuple(waits)
        waits = tuple((self.sem[w], v) for (w, v) in waits)

        def g(e, fn=fn, waits=waits, s=s, n=n):
            for (ss, v) in waits:
                e.wait_ge(ss.h, v)
            ins = fn(e)
            if s is not None:
                ins.then_inc(s.h, n)
        self.q[eng].append(g)
        return val

    def c(self, name):
        return self.sem[name].c

    def barrier(self):
        snap = [(n, s.c) for n, s in self.sem.items() if s.c > 0]
        for e in self.ENGS:
            self.pending[e] = list(snap)

    def flush(self, block):
        self.emit(block)
        self.q = {e: [] for e in self.ENGS}

    def emit(self, block):
        @block.tensor
        def _(eng):
            for f in self.q["pe"]:
                f(eng)

        @block.scalar
        def _(eng):
            for f in self.q["act"]:
                f(eng)

        @block.vector
        def _(eng):
            for f in self.q["dve"]:
                f(eng)

        @block.gpsimd
        def _(eng):
            for f in self.q["pool"]:
                f(eng)

        @block.sync
        def _(eng):
            for f in self.q["sync"]:
                f(eng)


def make_ident(sc, ident):
    v = sc.op("pool", lambda g: g.memset(ident[:], 1.0), inc="pool")
    return sc.op("pool", lambda g: g.affine_select(
        out=ident[:], in_=ident[:], pattern=[[-1, 128]],
        compare_op=ALU.is_equal, fill=0.0, base=0, channel_multiplier=1),
        waits=[("pool", v)], inc="pool")


def make_triu(sc, triu):
    v = sc.op("pool", lambda g: g.memset(triu[:], 1.0), inc="pool")
    return sc.op("pool", lambda g: g.affine_select(
        out=triu[:], in_=triu[:], pattern=[[1, 128]],
        compare_op=ALU.is_ge, fill=0.0, base=0, channel_multiplier=-1),
        waits=[("pool", v)], inc="pool")


def build_A(debug=False):
    nc = bass.Bass("TRN2", target_bir_lowering=False)
    dram = {}
    dram["xfm"] = nc.dram_tensor("xfm", [D, T], F32, kind="ExternalInput").ap()
    for w in ("q", "k", "v"):
        dram[f"w{w}T"] = nc.dram_tensor(f"w{w}T", [D, HPC * HD], F32, kind="ExternalInput").ap()
    dram["bqkv"] = nc.dram_tensor("bqkv", [3, HPC * HD], F32, kind="ExternalInput").ap()
    dram["yfm"] = nc.dram_tensor("yfm", [HPC * HD, T], BF16, kind="ExternalOutput").ap()
    es0 = ExitStack()
    with es0:
        sc = Sched(nc, es0)
        block = es0.enter_context(nc.Block())
        phase_A(nc, sc, block, dram, fused=False)
    return nc


def phase_A(nc, sc, block, dram, fused):
    xfm = dram["xfm"]
    wT = {w: dram[f"w{w}T"] for w in ("q", "k", "v")}
    bqkv = dram["bqkv"]
    TT = 512
    NTT = T // TT
    es = ExitStack()
    with es:
        def sb(name, shape, dt=F32):
            return es.enter_context(nc.sbuf_tensor("A_" + name, shape, dt))

        def psum(name, shape, dt=F32):
            return es.enter_context(nc.psum_tensor("A_" + name, shape, dt))

        w_sb = {w: sb(f"w{w}", [128, 8 * 256], BF16) for w in ("q", "k", "v")}
        bias_sb = sb("bias", [128, 6], F32)
        x_t = [sb(f"x_t{i}", [128, 8 * TT], BF16) for i in range(2)]
        q_fm = sb("q_fm", [128, 2 * T], BF16)
        k_fm = sb("k_fm", [128, 2 * T], BF16)
        kt = sb("kt", [128, HPC * NCHUNK * HD], BF16)
        vt = sb("vt", [128, HPC * NCHUNK * 65], BF16)
        v_stage = sb("v_stage", [128, TT], BF16)
        phi_m = sb("phi_m", [128, TT], F32)
        phi_e = sb("phi_e", [128, TT], F32)
        phi_r = sb("phi_r", [128, TT], F32)
        masked = [sb(f"masked{i}", [128, 128], BF16) for i in range(2)]
        triu = sb("triu", [128, 128], BF16)
        ident = sb("ident", [128, 128], BF16)
        S_arena = sb("S_arena", [128, 2 * 65], BF16)
        ones1 = sb("ones1", [1, 64], F32)
        rden3 = sb("rden3", [1, 384], F32)
        bcast_sb = sb("bcast_sb", [64, 128], F32)
        y_fm = sb("y_fm", [128, 2 * T], BF16)

        pp = [psum(f"pp{i}", [128, TT]) for i in range(6)]
        ptr = psum("ptr", [128, 128], BF16)
        psc_bank = psum("psc", [128, 512])
        psc = [psc_bank[:, 0:128], psc_bank[:, 128:256]]
        pnum3 = [pp[0][0:65, 0:128], pp[0][0:65, 128:256], pp[0][0:65, 256:384]]
        pS_h = [pp[1], pp[3], pp[4], pp[5]]
        pbc = pp[2][0:64, 0:128]

        make_ident(sc, ident)
        make_triu(sc, triu)
        sc.op("pool", lambda g: g.memset(ones1[:], 1.0), inc="pool")
        sc.op("pool", lambda g: g.memset(vt[:], 1.0), inc="pool")
        sc.op("pool", lambda g: g.memset(S_arena[:], 0.0), inc="pool")
        init_done = sc.c("pool")

        for w in ("q", "k", "v"):
            for j in range(8):
                sc.op("pool", lambda g, w=w, j=j: g.dma_start(
                    out=w_sb[w][:, j * 256:(j + 1) * 256],
                    in_=wT[w][j * 128:(j + 1) * 128, :]), inc="ld", dma=True)
        for wi in range(3):
            for p in range(2):
                sc.op("pool", lambda g, wi=wi, p=p: g.dma_start(
                    out=bias_sb[:, wi * 2 + p: wi * 2 + p + 1],
                    in_=bqkv[wi:wi + 1, p * 128:(p + 1) * 128].rearrange("a e -> e a")),
                    inc="ld", dma=True)
        ld_w = sc.c("ld")

        pe_tile_done = {}
        dve_tile_done = {}
        for ti in range(NTT):
            buf = ti % 2
            waits = [("pe", pe_tile_done[ti - 2])] if ti >= 2 else []
            xv = sc.op("pool", lambda g, buf=buf, ti=ti: g.dma_start(
                out=x_t[buf][:].rearrange("p (j t) -> p j t", j=8),
                in_=xfm[:, ti * TT:(ti + 1) * TT].rearrange("(j p) t -> p j t", p=128)),
                waits=waits, inc=f"lx{buf}", dma=True)

            for wi, w in enumerate(("q", "k", "v")):
                for p in range(2):
                    bank = pp[wi * 2 + p]
                    bcol = bias_sb[:, wi * 2 + p: wi * 2 + p + 1]
                    for j in range(8):
                        wts = []
                        if j == 0:
                            wts = [("ld", ld_w), (f"lx{buf}", xv)]
                            if ti > 0:
                                wts.append(("dve", dve_tile_done[ti - 1]))
                        mm = sc.op("pe", lambda t, bank=bank, w=w, j=j, p=p, buf=buf: t.matmul(
                            bank[:],
                            w_sb[w][:, j * 256 + p * 128: j * 256 + (p + 1) * 128],
                            x_t[buf][:, j * TT:(j + 1) * TT],
                            start=(j == 0), stop=(j == 7)),
                            waits=wts, inc="pe" if j == 7 else None)
                    if w in ("q", "k"):
                        dst = q_fm if w == "q" else k_fm
                        dslc = dst[:, p * T + ti * TT: p * T + (ti + 1) * TT]
                        v1 = sc.op("dve", lambda v, bank=bank, bcol=bcol: v.tensor_scalar(
                            out=phi_m[:], in0=bank[:], scalar1=bcol, scalar2=0.0,
                            op0=ALU.add, op1=ALU.min),
                            waits=[("pe", mm), ("ld", ld_w)], inc="dve")
                        a1 = sc.op("act", lambda a: a.activation(
                            out=phi_e[:], in_=phi_m[:], func=AF.Exp),
                            waits=[("dve", v1)], inc="act")
                        sc.op("dve", lambda v, bank=bank, bcol=bcol: v.tensor_scalar(
                            out=phi_r[:], in0=bank[:], scalar1=bcol, scalar2=0.0,
                            op0=ALU.add, op1=ALU.max), inc="dve")
                        v3 = sc.op("dve", lambda v, dslc=dslc: v.tensor_add(
                            out=dslc, in0=phi_e[:], in1=phi_r[:]),
                            waits=[("act", a1)], inc="dve")
                        if w == "k":
                            for sti in range(4):
                                c = ti * 4 + sti
                                src = dst[:, p * T + c * 128:p * T + (c + 1) * 128]
                                tv = sc.op("pe", lambda t, src=src: t.transpose(
                                    ptr[:], src, ident[:]),
                                    waits=[("dve", sc.c("dve")), ("pool", init_done)],
                                    inc="pe")
                                for hh in range(2):
                                    h = 2 * p + hh
                                    sc.op("dve", lambda v, hh=hh, h=h, c=c: v.tensor_copy(
                                        out=kt[:, (h * NCHUNK + c) * 64:(h * NCHUNK + c) * 64 + 64],
                                        in_=ptr[:, hh * 64:(hh + 1) * 64]),
                                        waits=[("pe", tv)], inc="dve")
                    else:
                        vv = sc.op("dve", lambda v, bank=bank, bcol=bcol: v.tensor_scalar_add(
                            out=v_stage[:], in0=bank[:], scalar1=bcol),
                            waits=[("pe", mm), ("ld", ld_w)], inc="dve")
                        for sti in range(4):
                            c = ti * 4 + sti
                            tv = sc.op("pe", lambda t, sti=sti: t.transpose(
                                ptr[:], v_stage[:, sti * 128:(sti + 1) * 128], ident[:]),
                                waits=[("dve", sc.c("dve")), ("pool", init_done)],
                                inc="pe")
                            for hh in range(2):
                                h = 2 * p + hh
                                sc.op("dve", lambda v, hh=hh, h=h, c=c: v.tensor_copy(
                                    out=vt[:, (h * NCHUNK + c) * 65:(h * NCHUNK + c) * 65 + 64],
                                    in_=ptr[:, hh * 64:(hh + 1) * 64]),
                                    waits=[("pe", tv)], inc="dve")
            pe_tile_done[ti] = sc.c("pe")
            dve_tile_done[ti] = sc.c("dve")
        phase1_dve = sc.c("dve")
        phase1_pe = sc.c("pe")

        NJ = NCHUNK * HPC
        mask_done = [0] * NJ
        num1_done = [0] * NJ
        supd_done = [0] * NJ
        ymult_done = [0] * NJ
        recip_done = [0] * NJ
        sevac_done = [0] * NJ
        bc_done = [0] * NJ
        bcevac_done = [0] * NJ

        def jparts(j):
            i, h = j // HPC, j % HPC
            return i, h, h % 2, h // 2

        for j in range(NJ + 2):
            if j >= 2:
                jj = j - 2
                bcw = [("dve", recip_done[jj])]
                if jj > 0:
                    bcw.append(("act", bcevac_done[jj - 1]))
                bc_done[jj] = sc.op("pe", lambda t, jj=jj: t.matmul(
                    pbc, ones1[:], rden3[:, (jj % 3) * 128:(jj % 3) * 128 + 128],
                    start=True, stop=True),
                    waits=bcw, inc="pe")
            if j < NJ:
                i, h, par, pair = jparts(j)
                rowoff = par * 64
                qk_off = pair * T + i * 128
                wts = [("dve", phase1_dve)]
                if j >= 2:
                    wts.append(("dve", mask_done[j - 2]))
                sc.op("pe", lambda t, rowoff=rowoff, qk_off=qk_off, buf=j % 2: t.matmul(
                    psc[buf][:],
                    k_fm[rowoff:rowoff + 64, qk_off:qk_off + 128],
                    q_fm[rowoff:rowoff + 64, qk_off:qk_off + 128],
                    start=True, stop=True, tile_position=(rowoff, 0)),
                    waits=wts, inc="pe")
                sc_done = sc.c("pe")
            if 1 <= j <= NJ:
                jj = j - 1
                i, h, par, pair = jparts(jj)
                rowoff = par * 64
                qk_off = pair * T + i * 128
                Sap = S_arena[rowoff:rowoff + 64, pair * 65: pair * 65 + 65]
                pSap = pS_h[h][rowoff:rowoff + 64, 0:65]
                vslc = vt[:, (h * NCHUNK + i) * 65:(h * NCHUNK + i) * 65 + 65]
                kslc = kt[:, (h * NCHUNK + i) * 64:(h * NCHUNK + i) * 64 + 64]
                nb = pnum3[jj % 3]
                nwts = [("dve", mask_done[jj])]
                if jj >= 3:
                    nwts.append(("dve", ymult_done[jj - 3]))
                num1_done[jj] = sc.op("pe", lambda t, nb=nb, vslc=vslc, buf=jj % 2: t.matmul(
                    nb, vslc, masked[buf][:], start=True, stop=False),
                    waits=nwts, inc="pe")
                n2w = []
                if jj >= HPC:
                    n2w = [("act", sevac_done[jj - HPC])]
                sc.op("pe", lambda t, nb=nb, Sap=Sap, rowoff=rowoff, qk_off=qk_off: t.matmul(
                    nb, Sap, q_fm[rowoff:rowoff + 64, qk_off:qk_off + 128],
                    start=False, stop=True, tile_position=(rowoff, 0)),
                    waits=n2w, inc="pe")
                supd_done[jj] = sc.op("pe", lambda t, pSap=pSap, kslc=kslc, vslc=vslc, i=i, rowoff=rowoff: t.matmul(
                    pSap, kslc, vslc, start=(i == 0), stop=(i == NCHUNK - 1),
                    tile_position=(0, rowoff)),
                    inc="pe")
                sevac_done[jj] = sc.op("act", lambda a, Sap=Sap, pSap=pSap: a.copy(
                    out=Sap, in_=pSap),
                    waits=[("pe", supd_done[jj])], inc="act")
                recip_done[jj] = sc.op("dve", lambda v, nb=nb, jj=jj: v.reciprocal(
                    out=rden3[:, (jj % 3) * 128:(jj % 3) * 128 + 128],
                    in_=nb[64:65, 0:128]),
                    waits=[("pe", supd_done[jj])], inc="dve")
            if j < NJ:
                mwts = [("pe", sc_done)]
                if j >= 2:
                    mwts.append(("pe", num1_done[j - 2]))
                mask_done[j] = sc.op("dve", lambda v, buf=j % 2: v.tensor_mul(
                    out=masked[buf][:], in0=psc[buf][:], in1=triu[:]),
                    waits=mwts, inc="dve")
            if j >= 2:
                jj = j - 2
                i, h, par, pair = jparts(jj)
                bw = [("pe", bc_done[jj])]
                if jj >= 1:
                    bw.append(("dve", ymult_done[jj - 1]))
                bcevac_done[jj] = sc.op("act", lambda a: a.copy(
                    out=bcast_sb[:], in_=pbc),
                    waits=bw, inc="act")
                ydst = y_fm[(h % 2) * 64:(h % 2) * 64 + 64,
                            (h // 2) * T + i * 128:(h // 2) * T + (i + 1) * 128]
                ymult_done[jj] = sc.op("dve", lambda v, nb=pnum3[jj % 3], ydst=ydst: v.tensor_mul(
                    out=ydst, in0=nb[0:64, 0:128], in1=bcast_sb[:]),
                    waits=[("act", bcevac_done[jj])], inc="dve")
        yl = sc.c("dve")

        if not fused:
            for p in range(2):
                sc.op("sync", lambda e, p=p: e.dma_start(
                    out=dram["yfm"][p * 128:(p + 1) * 128, :], in_=y_fm[:, p * T:(p + 1) * T]),
                    waits=[("dve", yl)], inc="st", dma=True)
            sc.op("sync", lambda e: None, waits=[("st", sc.c("st"))])
        else:
            a2a_in, a2a_out = dram["a2a1_in"], dram["a2a1_out"]
            for jj in range(8):
                for p in range(2):
                    sc.op("sync", lambda e, jj=jj, p=p: e.dma_start(
                        out=a2a_in[jj * 256 + p * 128: jj * 256 + (p + 1) * 128, :],
                        in_=y_fm[:, p * T + (jj % 4) * TPC: p * T + (jj % 4) * TPC + TPC]),
                        waits=[("dve", yl)], inc="st1", dma=True)
            def _a2a(g, a2a_in=a2a_in, a2a_out=a2a_out):
                with allow_ici_all_to_all():
                    return g.collective_compute(
                        "AllToAll", ALU.bypass, replica_groups=[list(range(8))],
                        ins=[a2a_in.opt()], outs=[a2a_out.opt()])
            sc.op("pool", _a2a, waits=[("st1", sc.c("st1"))], inc="cc")
        sc.flush(block)




def build_B(debug=False):
    nc = bass.Bass("TRN2", target_bir_lowering=False)
    dram = {}
    dram["yfm_sl"] = nc.dram_tensor("yfm_sl", [D, TPC], BF16, kind="ExternalInput").ap()
    dram["x_sl"] = nc.dram_tensor("x_sl", [TPC, D], F32, kind="ExternalInput").ap()
    dram["woT"] = nc.dram_tensor("woT", [D, D], F32, kind="ExternalInput").ap()
    dram["bo"] = nc.dram_tensor("bo", [1, D], F32, kind="ExternalInput").ap()
    dram["wgT"] = nc.dram_tensor("wgT", [D, 3072], I32, kind="ExternalInput").ap()
    dram["wuT"] = nc.dram_tensor("wuT", [D, 3072], I32, kind="ExternalInput").ap()
    dram["wdT"] = nc.dram_tensor("wdT", [3072, D], I32, kind="ExternalInput").ap()
    dram["x2o"] = nc.dram_tensor("x2o", [TPC, D], F32, kind="ExternalOutput").ap()
    dram["x1_scr"] = nc.dram_tensor("x1_scr", [TPC, D], F32).ap()
    dram["x1fm_scr"] = nc.dram_tensor("x1fm_scr", [D, TPC], BF16).ap()
    es0 = ExitStack()
    with es0:
        sc = Sched(nc, es0)
        block = es0.enter_context(nc.Block())
        phase_B(nc, sc, block, dram, fused=False)
    return nc


def phase_B(nc, sc, block, dram, fused):
    x_sl = dram["x_sl"]
    woT_d = dram.get("woT")
    bo_d = dram["bo"]
    wgT_d = dram["wgT"]
    wuT_d = dram["wuT"]
    wdT_d = dram["wdT"]
    x1_dram = dram["x1_scr"]
    x1fm_dram = dram["x1fm_scr"]
    if fused:
        yfm = dram["a2a1_out"]
        wo2_d = dram["wo2T"]
        NCJ = 16
        x2_dram = dram["x2_scr"]
        x2fm_dram = dram["ag2_in"]
    else:
        yfm = dram["yfm_sl"]
        wo2_d = dram["woT"]
        NCJ = 8
        x2_dram = None
        x2fm_dram = None
    es = ExitStack()
    with es:
        def sb(name, shape, dt=F32):
            return es.enter_context(nc.sbuf_tensor("B_" + name, shape, dt))

        def psum(name, shape, dt=F32):
            return es.enter_context(nc.psum_tensor("B_" + name, shape, dt))

        bo_bc = sb("bo_bc", [128, D], F32)
        bo_sb = sb("bo_sb", [1, D], F32)
        xres = sb("xres", [128, D], F32)
        pre = sb("pre", [128, D], F32)
        mean = sb("mean", [128, 1], F32)
        nmean = sb("nmean", [128, 1], F32)
        var = sb("var", [128, 1], F32)
        rstd = sb("rstd", [128, 1], F32)
        xout = sb("xout", [128, D], F32)
        xout_bf = sb("xout_bf", [128, D], BF16)
        ident = sb("ident", [128, 128], BF16)
        ones1 = sb("ones1", [1, 128], F32)
        x1fm_t = sb("x1fm_t", [128, 8 * 128], BF16)

        pb = [psum(f"pb{i}", [128, 512]) for i in range(6)]
        ptr = psum("ptr", [128, 128], BF16)

        make_ident(sc, ident)
        sc.op("pool", lambda g: g.memset(ones1[:], 1.0), inc="pool")
        init_done = sc.c("pool")

        es1 = ExitStack()
        with es1:
            yfm_t = es1.enter_context(nc.sbuf_tensor("B_yfm_t", [128, NCJ * 128], BF16))
            woT_sb = es1.enter_context(nc.sbuf_tensor("B_woT_sb", [128, NCJ * D], BF16))
            for j in range(NCJ):
                sc.op("pool", lambda g, j=j: g.dma_start(
                    out=woT_sb[:, j * D:(j + 1) * D],
                    in_=wo2_d[j * 128:(j + 1) * 128, :]), inc="ld", dma=True)
            ld_w = sc.c("ld")

            bv = sc.op("pool", lambda g: g.dma_start(out=bo_sb[:], in_=bo_d), inc="ld", dma=True)
            for oj in range(2):
                mm = sc.op("pe", lambda t, oj=oj: t.matmul(
                    pb[4][:, 0:512], ones1[:], bo_sb[:, oj * 512:(oj + 1) * 512],
                    start=True, stop=True),
                    waits=[("ld", bv), ("pool", init_done)] + ([("act", sc.c("act"))] if oj else []),
                    inc="pe")
                sc.op("act", lambda a, oj=oj: a.copy(
                    out=bo_bc[:, oj * 512:(oj + 1) * 512], in_=pb[4][:, 0:512]),
                    waits=[("pe", mm)], inc="act")
            bo_done = sc.c("act")

            NT1 = TPC // 128
            for tt in range(NT1):
                xv = sc.op("sync", lambda e, tt=tt: e.dma_start(
                    out=xres[:], in_=x_sl[tt * 128:(tt + 1) * 128, :]),
                    waits=([("dve", sc.c("dve"))] if tt else []), inc="ld2", dma=True)
                ccw = [("cc", dram["cc_wait"])] if fused else []
                yv = sc.op("sync", lambda e, tt=tt: e.dma_start(
                    out=yfm_t[:].rearrange("p (j t) -> p j t", j=NCJ),
                    in_=yfm[:, tt * 128:(tt + 1) * 128].rearrange("(j p) t -> p j t", p=128)),
                    waits=([("pe", sc.c("pe"))] if tt else []) + ccw, inc="ld2", dma=True)
                mms = []
                for oj in range(2):
                    for j in range(NCJ):
                        w = []
                        if j == 0:
                            w = [("ld", ld_w), ("ld2", max(xv, yv))]
                            if tt > 0:
                                w.append(("dve", sc.c("dve")))
                        mm = sc.op("pe", lambda t, oj=oj, j=j: t.matmul(
                            pb[oj][:],
                            yfm_t[:, j * 128:(j + 1) * 128],
                            woT_sb[:, j * D + oj * 512: j * D + (oj + 1) * 512],
                            start=(j == 0), stop=(j == NCJ - 1)),
                            waits=w, inc="pe" if j == NCJ - 1 else None)
                    mms.append(mm)
                for oj in range(2):
                    sc.op("dve", lambda v, oj=oj: v.tensor_add(
                        out=pre[:, oj * 512:(oj + 1) * 512], in0=pb[oj][:],
                        in1=bo_bc[:, oj * 512:(oj + 1) * 512]),
                        waits=[("pe", mms[oj]), ("act", bo_done)], inc="dve")
                v1 = sc.op("dve", lambda v: v.tensor_add(out=pre[:], in0=pre[:], in1=xres[:]),
                           waits=[("ld2", max(xv, yv))], inc="dve")
                v2 = sc.op("dve", lambda v: v.reduce_sum(out=mean[:, 0:1], in_=pre[:], axis=AXL.X),
                           inc="dve")
                v3 = sc.op("dve", lambda v: v.tensor_scalar_mul(
                    out=nmean[:, 0:1], in0=mean[:, 0:1], scalar1=-1.0 / D),
                    waits=[("dve", v2)], inc="dve")
                a1 = sc.op("act", lambda a: a.activation(
                    out=xout[:], in_=pre[:], func=AF.Square, bias=nmean[:, 0:1], scale=1.0,
                    accum_out=var[:, 0:1]),
                    waits=[("dve", v3), ("st1", sc.c("st1")), ("st", sc.c("st"))], inc="act")
                v4 = sc.op("dve", lambda v: v.tensor_scalar(
                    out=var[:, 0:1], in0=var[:, 0:1], scalar1=1.0 / D, scalar2=1e-5,
                    op0=ALU.mult, op1=ALU.add),
                    waits=[("act", a1)], inc="dve")
                a2 = sc.op("act", lambda a: a.activation(
                    out=rstd[:, 0:1], in_=var[:, 0:1], func=AF.Sqrt),
                    waits=[("dve", v4)], inc="act")
                v5 = sc.op("dve", lambda v: v.reciprocal(out=rstd[:, 0:1], in_=rstd[:, 0:1]),
                           waits=[("act", a2)], inc="dve")
                v6 = sc.op("dve", lambda v: v.tensor_scalar(
                    out=xout[:], in0=pre[:], scalar1=nmean[:, 0:1], scalar2=rstd[:, 0:1],
                    op0=ALU.add, op1=ALU.mult),
                    waits=[("dve", v5)], inc="dve")
                v7 = sc.op("dve", lambda v: v.tensor_copy(out=xout_bf[:], in_=xout[:]),
                           inc="dve")
                for sj in range(8):
                    tv = sc.op("pe", lambda t, sj=sj: t.transpose(
                        ptr[:], xout_bf[:, sj * 128:(sj + 1) * 128], ident[:]),
                        waits=[("dve", v7), ("pool", init_done)] +
                              ([("dve", sc.c("dve"))] if sj else []), inc="pe")
                    sc.op("dve", lambda v, sj=sj: v.tensor_copy(
                        out=x1fm_t[:, sj * 128:(sj + 1) * 128], in_=ptr[:]),
                        waits=[("pe", tv)], inc="dve")
                vlast = sc.c("dve")
                sc.op("sync", lambda e, tt=tt: e.dma_start(
                    out=x1_dram[tt * 128:(tt + 1) * 128, :], in_=xout[:]),
                    waits=[("dve", vlast)], inc="st1", dma=True)
                sc.op("sync", lambda e, tt=tt: e.dma_start(
                    out=x1fm_dram[:, tt * 128:(tt + 1) * 128].rearrange("(j p) t -> p j t", p=128),
                    in_=x1fm_t[:].rearrange("p (j t) -> p j t", j=8)),
                    waits=[("dve", vlast)], inc="st1", dma=True)

            b1_st = sc.c("st1")
            sc.flush(block)
        es2 = ExitStack()
        es2.__enter__()
        wg_sb = es2.enter_context(nc.sbuf_tensor("B_wg_sb", [128, 8 * 3072], BF16))
        wu_sb = es2.enter_context(nc.sbuf_tensor("B_wu_sb", [128, 8 * 3072], BF16))
        wd_sb = es2.enter_context(nc.sbuf_tensor("B_wd_sb", [128, 24 * D], BF16))
        stage_i = es2.enter_context(nc.sbuf_tensor("B_stage_i", [128, 512], I32))
        h_fm = es2.enter_context(nc.sbuf_tensor("B_h_fm", [128, 24 * 512], BF16))
        silu_t = es2.enter_context(nc.sbuf_tensor("B_silu_t", [128, 512], F32))
        x1fm_blk = es2.enter_context(nc.sbuf_tensor("B_x1fm_blk", [128, 8 * 512], BF16))
        for (wsb, wdr, nch, wid) in ((wg_sb, wgT_d, 8, 3072), (wu_sb, wuT_d, 8, 3072),
                                     (wd_sb, wdT_d, 24, D)):
            for j in range(nch):
                for cc2 in range(wid // 512):
                    lv = sc.op("pool", lambda g, wdr=wdr, j=j, cc2=cc2: g.dma_start(
                        out=stage_i[:], in_=wdr[j * 128:(j + 1) * 128, cc2 * 512:(cc2 + 1) * 512]),
                        waits=[("dve", sc.c("dve"))], inc="lst", dma=True)
                    sc.op("dve", lambda v, wsb=wsb, j=j, wid=wid, cc2=cc2: v.tensor_scalar_mul(
                        out=wsb[:, j * wid + cc2 * 512: j * wid + (cc2 + 1) * 512], in0=stage_i[:],
                        scalar1=1.0 / 127.0),
                        waits=[("lst", lv)], inc="dve")
        dq_done = sc.c("dve")

        NB = TPC // 512
        for bb in range(NB):
            xv = sc.op("sync", lambda e, bb=bb: e.dma_start(
                out=x1fm_blk[:].rearrange("p (j t) -> p j t", j=8),
                in_=x1fm_dram[:, bb * 512:(bb + 1) * 512].rearrange("(j p) t -> p j t", p=128)),
                waits=[("st1", b1_st)] + ([("dve", sc.c("dve"))] if bb else []),
                inc="ld2", dma=True)
            for hj in range(24):
                gmm = umm = None
                for j in range(8):
                    w = []
                    if j == 0:
                        w = [("ld2", xv), ("dve", dq_done)]
                        if hj >= 2:
                            w.append(("dve", sc.c("dve")))
                    gmm = sc.op("pe", lambda t, hj=hj, j=j: t.matmul(
                        pb[hj % 2][:],
                        wg_sb[:, j * 3072 + hj * 128: j * 3072 + (hj + 1) * 128],
                        x1fm_blk[:, j * 512:(j + 1) * 512],
                        start=(j == 0), stop=(j == 7)),
                        waits=w, inc="pe" if j == 7 else None)
                for j in range(8):
                    umm = sc.op("pe", lambda t, hj=hj, j=j: t.matmul(
                        pb[2 + hj % 2][:],
                        wu_sb[:, j * 3072 + hj * 128: j * 3072 + (hj + 1) * 128],
                        x1fm_blk[:, j * 512:(j + 1) * 512],
                        start=(j == 0), stop=(j == 7)),
                        inc="pe" if j == 7 else None)
                a1 = sc.op("act", lambda a, hj=hj: a.activation(
                    out=silu_t[:], in_=pb[hj % 2][:], func=AF.Silu),
                    waits=[("pe", gmm)] + ([("dve", sc.c("dve"))] if hj >= 1 else []),
                    inc="act")
                sc.op("dve", lambda v, hj=hj: v.tensor_mul(
                    out=h_fm[:, hj * 512:(hj + 1) * 512], in0=pb[2 + hj % 2][:],
                    in1=silu_t[:]),
                    waits=[("act", a1), ("pe", umm)], inc="dve")
            hdone = sc.c("dve")
            for st_i in range(4):
                tt = bb * 4 + st_i
                xv2 = sc.op("sync", lambda e, tt=tt: e.dma_start(
                    out=xres[:], in_=x1_dram[tt * 128:(tt + 1) * 128, :]),
                    waits=[("st1", b1_st), ("dve", sc.c("dve"))], inc="ld2", dma=True)
                dmm = []
                for oj in range(2):
                    for hj in range(24):
                        w = []
                        if hj == 0:
                            w = [("dve", hdone)]
                            w.append(("dve", sc.c("dve")))
                        mm = sc.op("pe", lambda t, oj=oj, hj=hj, st_i=st_i: t.matmul(
                            pb[4 + oj][:],
                            h_fm[:, hj * 512 + st_i * 128: hj * 512 + (st_i + 1) * 128],
                            wd_sb[:, hj * D + oj * 512: hj * D + (oj + 1) * 512],
                            start=(hj == 0), stop=(hj == 23)),
                            waits=w, inc="pe" if hj == 23 else None)
                    dmm.append(mm)
                for oj in range(2):
                    sc.op("dve", lambda v, oj=oj: v.tensor_add(
                        out=pre[:, oj * 512:(oj + 1) * 512], in0=pb[4 + oj][:],
                        in1=xres[:, oj * 512:(oj + 1) * 512]),
                        waits=[("pe", dmm[oj]), ("ld2", xv2)], inc="dve")
                v2 = sc.op("dve", lambda v: v.reduce_sum(out=mean[:, 0:1], in_=pre[:], axis=AXL.X),
                           inc="dve")
                v3 = sc.op("dve", lambda v: v.tensor_scalar_mul(
                    out=nmean[:, 0:1], in0=mean[:, 0:1], scalar1=-1.0 / D),
                    waits=[("dve", v2)], inc="dve")
                a1 = sc.op("act", lambda a: a.activation(
                    out=xout[:], in_=pre[:], func=AF.Square, bias=nmean[:, 0:1], scale=1.0,
                    accum_out=var[:, 0:1]),
                    waits=[("dve", v3), ("st", sc.c("st")), ("st1", sc.c("st1"))], inc="act")
                v4 = sc.op("dve", lambda v: v.tensor_scalar(
                    out=var[:, 0:1], in0=var[:, 0:1], scalar1=1.0 / D, scalar2=1e-5,
                    op0=ALU.mult, op1=ALU.add),
                    waits=[("act", a1)], inc="dve")
                a2 = sc.op("act", lambda a: a.activation(
                    out=rstd[:, 0:1], in_=var[:, 0:1], func=AF.Sqrt),
                    waits=[("dve", v4)], inc="act")
                v5 = sc.op("dve", lambda v: v.reciprocal(out=rstd[:, 0:1], in_=rstd[:, 0:1]),
                           waits=[("act", a2)], inc="dve")
                v6 = sc.op("dve", lambda v: v.tensor_scalar(
                    out=xout[:], in0=pre[:], scalar1=nmean[:, 0:1], scalar2=rstd[:, 0:1],
                    op0=ALU.add, op1=ALU.mult),
                    waits=[("dve", v5)], inc="dve")
                if not fused:
                    sc.op("sync", lambda e, tt=tt: e.dma_start(
                        out=dram["x2o"][tt * 128:(tt + 1) * 128, :], in_=xout[:]),
                        waits=[("dve", v6)], inc="st", dma=True)
                else:
                    sc.op("sync", lambda e, tt=tt: e.dma_start(
                        out=x2_dram[tt * 128:(tt + 1) * 128, :], in_=xout[:]),
                        waits=[("dve", v6)], inc="st1", dma=True)
                    v7 = sc.op("dve", lambda v: v.tensor_copy(out=xout_bf[:], in_=xout[:]),
                               waits=[("dve", v6)], inc="dve")
                    for sj in range(8):
                        tv = sc.op("pe", lambda t, sj=sj: t.transpose(
                            ptr[:], xout_bf[:, sj * 128:(sj + 1) * 128], ident[:]),
                            waits=[("dve", sc.c("dve"))], inc="pe")
                        sc.op("dve", lambda v, sj=sj: v.tensor_copy(
                            out=x1fm_t[:, sj * 128:(sj + 1) * 128], in_=ptr[:]),
                            waits=[("pe", tv)], inc="dve")
                    vl2 = sc.c("dve")
                    sc.op("sync", lambda e, tt=tt: e.dma_start(
                        out=x2fm_dram[:, tt * 128:(tt + 1) * 128].rearrange("(j p) t -> p j t", p=128),
                        in_=x1fm_t[:].rearrange("p (j t) -> p j t", j=8)),
                        waits=[("dve", vl2)], inc="st1", dma=True)
        if not fused:
            sc.op("sync", lambda e: None, waits=[("st", sc.c("st"))])
        else:
            sc.op("pool", lambda g, ai=dram["ag2_in"], ao=dram["ag2_out"]: g.collective_compute(
                "AllGather", ALU.bypass, replica_groups=[[0, 1, 2, 3], [4, 5, 6, 7]],
                ins=[ai.opt()], outs=[ao.opt()]),
                waits=[("st1", sc.c("st1"))], inc="cc")
        sc.flush(block)
        es2.__exit__(None, None, None)




def build_C(sign=1.0):
    nc = bass.Bass("TRN2", target_bir_lowering=False)
    dram = {}
    dram["x2fm"] = nc.dram_tensor("x2fm", [D, T], F32, kind="ExternalInput").ap()
    for w in ("q", "k", "v"):
        dram[f"w{w}C"] = nc.dram_tensor(f"w{w}C", [D, HPC * HD], F32, kind="ExternalInput").ap()
    dram["bqkvC"] = nc.dram_tensor("bqkvC", [3, HPC * HD], F32, kind="ExternalInput").ap()
    dram["gain"] = nc.dram_tensor("gain", [1, 1], F32, kind="ExternalInput").ap()
    dram["yfmC"] = nc.dram_tensor("yfm", [HPC * HD, T], BF16, kind="ExternalOutput").ap()
    es0 = ExitStack()
    with es0:
        sc = Sched(nc, es0)
        block = es0.enter_context(nc.Block())
        phase_C(nc, sc, block, dram, sign, fused=False)
    return nc


def phase_C(nc, sc, block, dram, sign, fused):
    wC = {w: dram[f"w{w}C"] for w in ("q", "k", "v")}
    bqkv = dram["bqkvC"]
    gain_d = dram["gain"]
    xfm = None if fused else dram["x2fm"]
    ag2_out = dram.get("ag2_out")

    TT = 512
    NTT = T // TT
    NQS = T // 128
    NQT = T // 512
    es = ExitStack()
    with es:
        def sb(name, shape, dt=F32):
            return es.enter_context(nc.sbuf_tensor("C_" + name, shape, dt))

        bias_sb = sb("bias", [128, 6], F32)
        q_fm = sb("q_fm", [128, 2 * T], BF16)
        k_fm = sb("k_fm", [128, 2 * T], BF16)
        vt = sb("vt", [128, HPC * NCHUNK * 65], BF16)
        v_stage = sb("v_stage", [128, TT], F32)
        ident = sb("ident", [128, 128], BF16)
        identf = sb("identf", [128, 128], F32)
        onesc = sb("onesc", [65, 128], BF16)
        onesf = sb("onesf", [1, 128], F32)
        ksum = sb("ksum", [128, 2], F32)
        ksumb = sb("ksumb", [128, 2], BF16)
        negmu = [sb(f"negmu{h}", [128, NQS]) for h in range(HPC)]
        absum = [sb(f"absum{h}", [128, NQS]) for h in range(HPC)]
        epsc = [sb(f"epsc{h}", [128, NQS]) for h in range(HPC)]
        acc8 = [sb(f"acc8_{i}", [128, 8], F32) for i in range(2)]
        uabs = sb("uabs", [128, 1024], BF16)
        uabs_d = [sb(f"uabs_d{i}", [128, 1024], BF16) for i in range(2)]
        gaincol = sb("gaincol", [128, 1], F32)
        g4inv = sb("g4inv", [128, 1], F32)
        scr = sb("scr", [128, 2], F32)
        madt = sb("madt", [128, NQS], F32)
        m2t = sb("m2t", [128, NQS], F32)
        u_t = [sb(f"u_t{i}", [128, 1024], BF16) for i in range(3)]
        r_t = [sb(f"r_t{i}", [128, 1024], BF16) for i in range(3)]
        dmask = [sb(f"dmask{i}", [128, 512], BF16) for i in range(4)]
        negmurow = [sb(f"negmurow{i}", [1, 512], BF16) for i in range(2)]
        epsrow = [sb(f"epsrow{i}", [1, 512], F32) for i in range(2)]
        denrow = [sb(f"denrow{i}", [1, 512], F32) for i in range(2)]
        rden = [sb(f"rdenC{i}", [1, 512], F32) for i in range(2)]
        bcast_sb = [sb(f"bcastC{i}", [64, 512], F32) for i in range(2)]
        y_fm = sb("y_fmC", [128, 2 * T], BF16)

        pa = es.enter_context(nc.psum_tensor("C_pa", [128, 2048], F32))
        pa2 = es.enter_context(nc.psum_tensor("C_pa2", [128, 2048], F32))
        ptrf = pa2[:, 1536:1664]
        pproj = [pa[:, i * 512:(i + 1) * 512] for i in range(4)] + \
                [pa2[:, i * 512:(i + 1) * 512] for i in range(2)]
        psF = {0: [pa[:, 0:1024], pa[:, 1024:2048]],
               1: [pa2[:, 0:1024], pa2[:, 1024:2048]]}
        psc2 = [pa[:, 0:1024], pa[:, 1024:2048]]
        py = {0: pa2[0:65, 0:512], 1: pa2[0:65, 512:1024]}
        prow = {0: pa2[:, 1024:1536], 1: pa2[:, 1536:2048]}
        pmu = prow[0]

        make_ident(sc, ident)
        v = sc.op("pool", lambda g: g.memset(identf[:], 1.0), inc="pool")
        sc.op("pool", lambda g: g.affine_select(
            out=identf[:], in_=identf[:], pattern=[[-1, 128]],
            compare_op=ALU.is_equal, fill=0.0, base=0, channel_multiplier=1),
            waits=[("pool", v)], inc="pool")
        sc.op("pool", lambda g: g.memset(onesc[:], 1.0), inc="pool")
        sc.op("pool", lambda g: g.memset(onesf[:], 1.0), inc="pool")
        sc.op("pool", lambda g: g.memset(vt[:], 1.0), inc="pool")
        for j in range(4):
            v = sc.op("pool", lambda g, j=j: g.memset(dmask[j][:], 1.0), inc="pool")
            v = sc.op("pool", lambda g, j=j: g.affine_select(
                out=dmask[j][:], in_=dmask[j][:], pattern=[[1, 512]],
                compare_op=ALU.is_ge, fill=0.0, base=-128 * j, channel_multiplier=-1),
                waits=[("pool", v)], inc="pool")
        init_done = sc.c("pool")

        for w in ("q", "k", "v"):
            for j in range(8):
                sc.op("pool", lambda g, w=w, j=j: g.dma_start(
                    out=w_sb[w][:, j * 256:(j + 1) * 256],
                    in_=wC[w][j * 128:(j + 1) * 128, :]), inc="ld", dma=True)
        for wi in range(3):
            for p in range(2):
                sc.op("pool", lambda g, wi=wi, p=p: g.dma_start(
                    out=bias_sb[:, wi * 2 + p: wi * 2 + p + 1],
                    in_=bqkv[wi:wi + 1, p * 128:(p + 1) * 128].rearrange("a e -> e a")),
                    inc="ld", dma=True)
        gv = sc.op("pool", lambda g: g.dma_start(out=scr[0:1, 0:1], in_=gain_d),
                   inc="ld", dma=True)
        ld_w = sc.c("ld")

        gmm = sc.op("pe", lambda t: t.matmul(
            pmu[:, 0:1], onesf[:], scr[0:1, 0:1], start=True, stop=True),
            waits=[("ld", gv), ("pool", init_done)], inc="pe")
        a_g = sc.op("act", lambda a: a.copy(out=gaincol[:, 0:1], in_=pmu[:, 0:1]),
                    waits=[("pe", gmm)], inc="act")
        v1 = sc.op("dve", lambda v: v.tensor_mul(out=scr[:, 0:1], in0=gaincol[:, 0:1],
                                                 in1=gaincol[:, 0:1]),
                   waits=[("act", a_g)], inc="dve")
        v2 = sc.op("dve", lambda v: v.tensor_mul(out=scr[:, 1:2], in0=scr[:, 0:1],
                                                 in1=scr[:, 0:1]),
                   waits=[("dve", v1)], inc="dve")
        g4_done = sc.op("dve", lambda v: v.reciprocal(out=g4inv[:, 0:1], in_=scr[:, 1:2]),
                        waits=[("dve", v2)], inc="dve")

        es1 = ExitStack()
        with es1:
            w_sb = {w: es1.enter_context(nc.sbuf_tensor("C_w" + w, [128, 8 * 256], BF16)) for w in ("q", "k", "v")}
            x_t = [es1.enter_context(nc.sbuf_tensor(f"C_x_t{i}", [128, 8 * TT], BF16)) for i in range(2)]
            pe_tile_done = {}
            dve_tile_done = {}
            for ti in range(NTT):
                buf = ti % 2
                waits = [("pe", pe_tile_done[ti - 2])] if ti >= 2 else []
                if fused:
                    rr = ti // 4
                    t0 = (ti % 4) * TT
                    xv = sc.op("pool", lambda g, buf=buf, rr=rr, t0=t0: g.dma_start(
                        out=x_t[buf][:].rearrange("p (j t) -> p j t", j=8),
                        in_=ag2_out[rr * D:(rr + 1) * D, t0:t0 + TT].rearrange("(j p) t -> p j t", p=128)),
                        waits=waits + [("cc", dram["cc_wait"])], inc=f"lx{buf}", dma=True)
                else:
                    xv = sc.op("pool", lambda g, buf=buf, ti=ti: g.dma_start(
                        out=x_t[buf][:].rearrange("p (j t) -> p j t", j=8),
                        in_=xfm[:, ti * TT:(ti + 1) * TT].rearrange("(j p) t -> p j t", p=128)),
                        waits=waits, inc=f"lx{buf}", dma=True)
                for wi, w in enumerate(("q", "k", "v")):
                    for p in range(2):
                        bank = pproj[wi * 2 + p]
                        bcol = bias_sb[:, wi * 2 + p: wi * 2 + p + 1]
                        for j in range(8):
                            wts = []
                            if j == 0:
                                wts = [("ld", ld_w), (f"lx{buf}", xv)]
                                if ti > 0:
                                    wts.append(("dve", dve_tile_done[ti - 1]))
                                if ti == 0 and wi == 1 and p == 0:
                                    pass
                            mm = sc.op("pe", lambda t, bank=bank, w=w, j=j, p=p, buf=buf: t.matmul(
                                bank, w_sb[w][:, j * 256 + p * 128: j * 256 + (p + 1) * 128],
                                x_t[buf][:, j * TT:(j + 1) * TT],
                                start=(j == 0), stop=(j == 7)),
                                waits=wts, inc="pe" if j == 7 else None)
                        if w == "q":
                            dst = q_fm[:, p * T + ti * TT: p * T + (ti + 1) * TT]
                            sc.op("dve", lambda v, bank=bank, bcol=bcol, dst=dst: v.tensor_scalar(
                                out=dst, in0=bank, scalar1=bcol, scalar2=float(sign),
                                op0=ALU.add, op1=ALU.mult),
                                waits=[("pe", mm), ("ld", ld_w)], inc="dve")
                        elif w == "k":
                            dst = k_fm[:, p * T + ti * TT: p * T + (ti + 1) * TT]
                            sc.op("dve", lambda v, bank=bank, bcol=bcol, dst=dst: v.tensor_scalar_add(
                                out=dst, in0=bank, scalar1=bcol),
                                waits=[("pe", mm), ("ld", ld_w)], inc="dve")
                        else:
                            sc.op("dve", lambda v, bank=bank, bcol=bcol: v.tensor_scalar_add(
                                out=v_stage[:], in0=bank, scalar1=bcol),
                                waits=[("pe", mm), ("ld", ld_w)], inc="dve")
                            for sti in range(4):
                                c = ti * 4 + sti
                                tv = sc.op("pe", lambda t, sti=sti: t.transpose(
                                    ptrf, v_stage[:, sti * 128:(sti + 1) * 128], identf[:]),
                                    waits=[("dve", sc.c("dve")), ("pool", init_done)], inc="pe")
                                for hh in range(2):
                                    h = 2 * p + hh
                                    sc.op("dve", lambda v, hh=hh, h=h, c=c: v.tensor_copy(
                                        out=vt[:, (h * NCHUNK + c) * 65:(h * NCHUNK + c) * 65 + 64],
                                        in_=ptrf[:, hh * 64:(hh + 1) * 64]),
                                        waits=[("pe", tv)], inc="dve")
                pe_tile_done[ti] = sc.c("pe")
                dve_tile_done[ti] = sc.c("dve")

            sc.flush(block)
        proj_dve = sc.c("dve")

        kv = {}
        for p in range(2):
            r0 = sc.op("dve", lambda v, p=p: v.reduce_sum(
                out=ksum[:, p:p + 1], in_=k_fm[:, p * T:(p + 1) * T], axis=AXL.X),
                waits=[("dve", proj_dve)], inc="dve")
            kv[p] = sc.op("dve", lambda v, p=p: v.tensor_copy(
                out=ksumb[:, p:p + 1], in_=ksum[:, p:p + 1]),
                waits=[("dve", r0)], inc="dve")

        mu_done = {}
        for h in range(HPC):
            par, pair = h % 2, h // 2
            ro = par * 64
            for qs in range(NQS):
                w = []
                if qs == 0:
                    w = [("dve", kv[pair]), ("act", a_g)]
                    if h > 0:
                        w.append(("dve", mu_done[h - 1]))
                mm = sc.op("pe", lambda t, ro=ro, pair=pair, qs=qs: t.matmul(
                    pmu[:, qs:qs + 1],
                    q_fm[ro:ro + 64, pair * T + qs * 128: pair * T + (qs + 1) * 128],
                    ksumb[ro:ro + 64, pair:pair + 1],
                    start=True, stop=True, tile_position=(ro, 0)),
                    waits=w, inc="pe" if qs == NQS - 1 else None)
            sc.op("dve", lambda v, h=h: v.tensor_scalar_mul(
                out=negmu[h][:], in0=pmu[:, 0:NQS], scalar1=-1.0 / T),
                waits=[("pe", mm)], inc="dve")
            mu_done[h] = sc.c("dve")
        mu_all = sc.c("dve")

        DVE_KQ = (2, 5)
        eps_done = {}
        bank_last = {}
        for pair in range(2):
            hA, hB = 2 * pair, 2 * pair + 1
            prev_reduce = None
            for qs in range(NQS):
                for kq in range(8):
                    half = kq % 2
                    for a_i, (h, ro) in enumerate(((hA, 0), (hB, 64))):
                        for kk in range(2):
                            kt0 = (kq * 2 + kk) * 512
                            w = []
                            if kk == 0:
                                bk = (a_i, half)
                                if bk in bank_last:
                                    w.append(bank_last[bk])
                                if pair == 0 and qs == 0 and kq < 2:
                                    w.append(("dve", mu_all))
                            mm = sc.op("pe", lambda t, ro=ro, pair=pair, qs=qs, kt0=kt0,
                                       a_i=a_i, half=half, kk=kk: t.matmul(
                                psF[a_i][half][:, kk * 512:(kk + 1) * 512],
                                q_fm[ro:ro + 64, pair * T + qs * 128: pair * T + (qs + 1) * 128],
                                k_fm[ro:ro + 64, pair * T + kt0: pair * T + kt0 + 512],
                                start=True, stop=True, tile_position=(ro, 0)),
                                waits=w, inc="pe" if kk == 1 else None)
                        h_ = hA if a_i == 0 else hB
                        if kq in DVE_KQ:
                            dv1 = sc.op("dve", lambda v, a_i=a_i, half=half, h_=h_, qs=qs: v.tensor_scalar(
                                out=uabs_d[a_i][:], in0=psF[a_i][half],
                                scalar1=negmu[h_][:, qs:qs + 1], scalar2=None,
                                op0=ALU.add, op1=ALU.bypass),
                                waits=[("pe", mm)], inc="dve")
                            bank_last[(a_i, half)] = ("dve", dv1)
                            sc.op("dve", lambda v, a_i=a_i, kq=kq: v.reduce_sum(
                                out=acc8[a_i][:, kq:kq + 1], in_=uabs_d[a_i][:],
                                axis=AXL.X, apply_absolute_value=True),
                                inc="dve")
                        else:
                            aw = [("pe", mm)]
                            if prev_reduce is not None and kq == 0:
                                aw.append(("dve", prev_reduce))
                            bank_last[(a_i, half)] = ("act", sc.op(
                                "act", lambda a, a_i=a_i, half=half, h_=h_, qs=qs, kq=kq: a.activation(
                                    out=uabs[:], in_=psF[a_i][half],
                                    func=AF.Abs, bias=negmu[h_][:, qs:qs + 1], scale=1.0,
                                    accum_out=acc8[a_i][:, kq:kq + 1]),
                                waits=aw, inc="act"))
                rs = None
                for a_i, h_ in ((0, hA), (1, hB)):
                    la = bank_last[(a_i, 0)]
                    lb = bank_last[(a_i, 1)]
                    rs = sc.op("dve", lambda v, a_i=a_i, h_=h_, qs=qs: v.reduce_sum(
                        out=absum[h_][:, qs:qs + 1], in_=acc8[a_i][:], axis=AXL.X),
                        waits=[la, lb], inc="dve")
                prev_reduce = rs
            for h_ in (hA, hB):
                w1 = sc.op("dve", lambda v, h_=h_: v.tensor_scalar(
                    out=madt[:], in0=absum[h_][:], scalar1=1.0 / T, scalar2=1e-6,
                    op0=ALU.mult, op1=ALU.add), waits=[("dve", prev_reduce)], inc="dve")
                w2 = sc.op("dve", lambda v: v.tensor_mul(out=m2t[:], in0=madt[:], in1=madt[:]),
                           inc="dve")
                w3 = sc.op("dve", lambda v: v.tensor_mul(out=madt[:], in0=m2t[:], in1=m2t[:]),
                           inc="dve")
                eps_done[h_] = sc.op("dve", lambda v, h_=h_: v.tensor_scalar(
                    out=epsc[h_][:], in0=madt[:], scalar1=g4inv[:, 0:1], scalar2=1e-6,
                    op0=ALU.mult, op1=ALU.mult),
                    waits=[("dve", g4_done)], inc="dve")
        stats_done = sc.c("dve")
        stats_act = sc.c("act")

        ymult_done = 0
        prev_pair_last = {}
        relu_done = {}
        u_free = {}
        r_free = {}
        bc_free = {}
        prev_avmm = {}
        for pair in range(2):
            hA, hB = 2 * pair, 2 * pair + 1
            for qt in range(NQT):
                nrows = {}
                for a_i, h_ in ((0, hA), (1, hB)):
                    ro = (h_ % 2) * 64
                    for jj in range(4):
                        qs = qt * 4 + jj
                        w = [("dve", stats_done)] if jj == 0 else []
                        key = ("rowbank", a_i)
                        if jj == 0 and key in prev_pair_last:
                            w.append(("act", prev_pair_last[key]))
                        if jj == 0 and a_i in prev_avmm:
                            w.append(("pe", prev_avmm[a_i]))
                        tv = sc.op("pe", lambda t, a_i=a_i, h_=h_, qs=qs, jj=jj: t.transpose(
                            prow[a_i][0:1, jj * 128:(jj + 1) * 128],
                            negmu[h_][:, qs:qs + 1], identf[:]),
                            waits=w, inc="pe" if jj == 3 else None)
                    nrows[a_i] = sc.op("act", lambda a, a_i=a_i: a.copy(
                        out=negmurow[a_i][:], in_=prow[a_i][0:1, 0:512]),
                        waits=[("pe", tv)], inc="act")
                    prev_pair_last[("rowbank", a_i)] = nrows[a_i]
                kcmax = 4 * qt + 4
                avmm = {}
                roA, roB = (hA % 2) * 64, (hB % 2) * 64

                def sched_av(kc, kcmax=kcmax, hA=hA, hB=hB):
                    buf3 = kc % 3
                    for a_i, h_ in ((0, hA), (1, hB)):
                        w = [("dve", r2_done[buf3])]
                        if kc == 0:
                            w.append(("dve", ymult_done))
                        avmm[a_i] = sc.op("pe", lambda t, a_i=a_i, h_=h_, kc=kc, buf3=buf3, kcmax=kcmax: t.matmul(
                            py[a_i],
                            vt[:, (h_ * NCHUNK + kc) * 65:(h_ * NCHUNK + kc) * 65 + 65],
                            r_t[buf3][:, a_i * 512:(a_i + 1) * 512],
                            start=(kc == 0), stop=(kc == kcmax - 1)),
                            waits=w, inc="pe")
                        r_free[buf3] = avmm[a_i]

                r2_done = {}
                for kc in range(kcmax):
                    buf = kc % 2
                    w = []
                    if "relu" in relu_done:
                        pass
                    if (buf,) in relu_done:
                        w.append(("act", relu_done[(buf,)]))
                    else:
                        w.append(("act", nrows[1]))
                    for a_i, (h_, ro) in enumerate(((hA, roA), (hB, roB))):
                        sc.op("pe", lambda t, buf=buf, a_i=a_i, ro=ro, pair=pair, kc=kc, qt=qt: t.matmul(
                            psc2[buf][:, a_i * 512:(a_i + 1) * 512],
                            k_fm[ro:ro + 64, pair * T + kc * 128: pair * T + (kc + 1) * 128],
                            q_fm[ro:ro + 64, pair * T + qt * 512: pair * T + (qt + 1) * 512],
                            start=True, stop=False, tile_position=(ro, 0)),
                            waits=(w if a_i == 0 else []), inc="pe")
                    for a_i in (0, 1):
                        bmm = sc.op("pe", lambda t, buf=buf, a_i=a_i: t.matmul(
                            psc2[buf][:, a_i * 512:(a_i + 1) * 512],
                            onesc[0:1, :], negmurow[a_i][:],
                            start=False, stop=True, tile_position=(0, 0)),
                            inc="pe")
                    if kc >= 2:
                        sched_av(kc - 2)
                    buf3 = kc % 3
                    rv = sc.op("act", lambda a, buf=buf, buf3=buf3: a.activation(
                        out=u_t[buf3][:], in_=psc2[buf][:], func=AF.Relu),
                        waits=[("pe", bmm)] +
                              ([("dve", u_free[buf3])] if buf3 in u_free else []),
                        inc="act")
                    relu_done[(buf,)] = rv
                    d1 = sc.op("dve", lambda v, buf3=buf3: v.tensor_mul(
                        out=u_t[buf3][:], in0=u_t[buf3][:], in1=u_t[buf3][:]),
                        waits=[("act", rv)], inc="dve")
                    d2 = sc.op("dve", lambda v, buf3=buf3: v.tensor_mul(
                        out=r_t[buf3][:], in0=u_t[buf3][:], in1=u_t[buf3][:]),
                        waits=([("pe", r_free[buf3])] if buf3 in r_free else []),
                        inc="dve")
                    u_free[buf3] = d2
                    if kc >= 4 * qt:
                        for hh in range(2):
                            d2 = sc.op("dve", lambda v, buf3=buf3, kc=kc, qt=qt, hh=hh: v.tensor_mul(
                                out=r_t[buf3][:, hh * 512:(hh + 1) * 512],
                                in0=r_t[buf3][:, hh * 512:(hh + 1) * 512],
                                in1=dmask[kc - 4 * qt][:]),
                                inc="dve")
                    r2_done[buf3] = d2
                if kcmax >= 2:
                    sched_av(kcmax - 2)
                sched_av(kcmax - 1)
                for a_i, h_ in ((0, hA), (1, hB)):
                    ro = (h_ % 2) * 64
                    for jj in range(4):
                        qs = qt * 4 + jj
                        w = [("pe", avmm[a_i])] if jj == 0 else []
                        tv = sc.op("pe", lambda t, a_i=a_i, h_=h_, qs=qs, jj=jj: t.transpose(
                            prow[a_i][0:1, jj * 128:(jj + 1) * 128],
                            epsc[h_][:, qs:qs + 1], identf[:]),
                            waits=w, inc="pe" if jj == 3 else None)
                    ev = sc.op("act", lambda a, a_i=a_i: a.copy(
                        out=epsrow[a_i][:], in_=prow[a_i][0:1, 0:512]),
                        waits=[("pe", tv)], inc="act")
                    d1 = sc.op("dve", lambda v, a_i=a_i: v.tensor_add(
                        out=denrow[a_i][:], in0=py[a_i][64:65, 0:512], in1=epsrow[a_i][:]),
                        waits=[("act", ev), ("pe", avmm[a_i])], inc="dve")
                    d2 = sc.op("dve", lambda v, a_i=a_i: v.reciprocal(
                        out=rden[a_i][:], in_=denrow[a_i][:]),
                        waits=[("dve", d1)], inc="dve")
                    bmm = sc.op("pe", lambda t, a_i=a_i: t.matmul(
                        prow[a_i][0:64, 0:512], onesf[0:1, 0:64], rden[a_i][:],
                        start=True, stop=True),
                        waits=[("dve", d2), ("act", ev)], inc="pe")
                    bev = sc.op("act", lambda a, a_i=a_i: a.copy(
                        out=bcast_sb[a_i][:], in_=prow[a_i][0:64, 0:512]),
                        waits=[("pe", bmm)] +
                              ([("dve", bc_free[a_i])] if a_i in bc_free else []),
                        inc="act")
                    prev_pair_last[("rowbank", a_i)] = bev
                    ydst = y_fm[ro:ro + 64, (h_ // 2) * T + qt * 512:(h_ // 2) * T + (qt + 1) * 512]
                    ym = sc.op("dve", lambda v, a_i=a_i, ydst=ydst: v.tensor_mul(
                        out=ydst, in0=py[a_i][0:64, 0:512], in1=bcast_sb[a_i][:]),
                        waits=[("act", bev)], inc="dve")
                    bc_free[a_i] = ym
                    ymult_done = ym
                prev_avmm = dict(avmm)

        yl = sc.c("dve")
        if not fused:
            for p in range(2):
                sc.op("sync", lambda e, p=p: e.dma_start(
                    out=dram["yfmC"][p * 128:(p + 1) * 128, :], in_=y_fm[:, p * T:(p + 1) * T]),
                    waits=[("dve", yl)], inc="st", dma=True)
            sc.op("sync", lambda e: None, waits=[("st", sc.c("st"))])
        else:
            a2a_in, a2a_out = dram["a2a3_in"], dram["a2a3_out"]
            for jj in range(8):
                for p in range(2):
                    sc.op("sync", lambda e, jj=jj, p=p: e.dma_start(
                        out=a2a_in[jj * 256 + p * 128: jj * 256 + (p + 1) * 128, :],
                        in_=y_fm[:, p * T + (jj % 4) * TPC: p * T + (jj % 4) * TPC + TPC]),
                        waits=[("dve", yl)], inc="st1", dma=True)

            def _a2a(g, a2a_in=a2a_in, a2a_out=a2a_out):
                with allow_ici_all_to_all():
                    return g.collective_compute(
                        "AllToAll", ALU.bypass, replica_groups=[list(range(8))],
                        ins=[a2a_in.opt()], outs=[a2a_out.opt()])
            sc.op("pool", _a2a, waits=[("st1", sc.c("st1"))], inc="cc")
        sc.flush(block)




def build_D(stage=4):
    nc = bass.Bass("TRN2", target_bir_lowering=False)
    dram = {}
    dram["y1fm_sl"] = nc.dram_tensor("y1fm_sl", [D, TPC], BF16, kind="ExternalInput").ap()
    dram["x2_sl"] = nc.dram_tensor("x2_sl", [TPC, D], F32, kind="ExternalInput").ap()
    dram["wproj"] = nc.dram_tensor("wproj", [D, D], F32, kind="ExternalInput").ap()
    dram["bproj"] = nc.dram_tensor("bproj", [1, D], F32, kind="ExternalInput").ap()
    dram["wLT"] = nc.dram_tensor("wLT", [D, D], F32, kind="ExternalInput").ap()
    dram["bL"] = nc.dram_tensor("bL", [1, D], F32, kind="ExternalInput").ap()
    dram["ldT"] = nc.dram_tensor("ldT", [100, 4], F32, kind="ExternalInput").ap()
    dram["da"] = nc.dram_tensor("da", [100, D], F32, kind="ExternalInput").ap()
    dram["luT"] = nc.dram_tensor("luT", [4, 50], F32, kind="ExternalInput").ap()
    dram["uaT"] = nc.dram_tensor("uaT", [50, D], F32, kind="ExternalInput").ap()
    dram["x4o"] = nc.dram_tensor("x4o", [TPC, D], F32, kind="ExternalOutput").ap()
    dram["x3_scr"] = nc.dram_tensor("x3_scr", [TPC, D], F32).ap()
    dram["x3fm_scr"] = nc.dram_tensor("x3fm_scr", [D, TPC], BF16).ap()
    es0 = ExitStack()
    with es0:
        sc = Sched(nc, es0)
        block = es0.enter_context(nc.Block())
        phase_D(nc, sc, block, dram, fused=False, stage=stage)
    return nc


def phase_D(nc, sc, block, dram, fused, stage=4):
    bp_d = dram["bproj"]
    wl_d = dram["wLT"]
    bl_d = dram["bL"]
    ldT_d = dram["ldT"]
    da_d = dram["da"]
    luT_d = dram["luT"]
    uaT_d = dram["uaT"]
    x4o = dram["x4o"]
    x3_dram = dram["x3_scr"]
    x3fm_dram = dram["x3fm_scr"]
    if fused:
        yfm = dram["a2a3_out"]
        wp_d = dram["wp2"]
        NCJ = 16
        x_sl = dram["x2_scr"]
    else:
        yfm = dram["y1fm_sl"]
        wp_d = dram["wproj"]
        NCJ = 8
        x_sl = dram["x2_sl"]

    es = ExitStack()
    with es:
        def sb(name, shape, dt=F32):
            return es.enter_context(nc.sbuf_tensor("D_" + name, shape, dt))

        yfm_t = sb("yfm_t", [128, NCJ * 128], BF16)
        wp_sb = sb("wp_sb", [128, NCJ * D], BF16)
        weff = sb("weff", [128, 8 * D], BF16)
        bp_bc = sb("bp_bc", [128, D], F32)
        bl_bc = sb("bl_bc", [128, D], F32)
        bp_sb = sb("bp_sb", [1, D], F32)
        bl_sb = sb("bl_sb", [1, D], F32)
        xres = sb("xres", [128, D], F32)
        pre = sb("pre", [128, D], F32)
        mean = sb("mean", [128, 1], F32)
        nmean = sb("nmean", [128, 1], F32)
        var = sb("var", [128, 1], F32)
        rstd = sb("rstd", [128, 1], F32)
        xout = sb("xout", [128, D], F32)
        xout_bf = sb("xout_bf", [128, D], BF16)
        ident = sb("ident", [128, 128], BF16)
        onesf = sb("onesf", [1, 128], F32)
        x3fm_t = sb("x3fm_t", [128, 8 * 128], BF16)
        ldT_sb = sb("ldT_sb", [100, 4], BF16)
        da_sb = sb("da_sb", [100, D], BF16)
        luT_sb = sb("luT_sb", [4, 50], BF16)
        uaT_sb = sb("uaT_sb", [50, D], BF16)
        a1_sb = sb("a1_sb", [4, D], BF16)
        a2_sb = sb("a2_sb", [50, D], BF16)

        pb = [es.enter_context(nc.psum_tensor(f"D_pbD{i}", [128, 512], F32)) for i in range(4)]
        ptr = es.enter_context(nc.psum_tensor("D_ptrD", [128, 128], BF16))

        make_ident(sc, ident)
        sc.op("pool", lambda g: g.memset(onesf[:], 1.0), inc="pool")
        init_done = sc.c("pool")

        for j in range(NCJ):
            sc.op("pool", lambda g, j=j: g.dma_start(
                out=wp_sb[:, j * D:(j + 1) * D], in_=wp_d[j * 128:(j + 1) * 128, :]),
                inc="ld", dma=True)
        for nm, dst, srcd in (("ldT", ldT_sb, ldT_d), ("da", da_sb, da_d),
                              ("luT", luT_sb, luT_d), ("uaT", uaT_sb, uaT_d)):
            sc.op("pool", lambda g, dst=dst, srcd=srcd: g.dma_start(out=dst[:], in_=srcd),
                  inc="ld", dma=True)
        bv = sc.op("pool", lambda g: g.dma_start(out=bp_sb[:], in_=bp_d), inc="ld", dma=True)
        bv = sc.op("pool", lambda g: g.dma_start(out=bl_sb[:], in_=bl_d), inc="ld", dma=True)
        ld_w = sc.c("ld")

        evs = []
        for (row, bc) in ((bp_sb, bp_bc), (bl_sb, bl_bc)):
            for oj in range(2):
                mm = sc.op("pe", lambda t, row=row, oj=oj: t.matmul(
                    pb[3][:], onesf[:], row[:, oj * 512:(oj + 1) * 512],
                    start=True, stop=True),
                    waits=[("ld", ld_w), ("pool", init_done)] +
                          ([("act", evs[-1])] if evs else []),
                    inc="pe")
                evs.append(sc.op("act", lambda a, bc=bc, oj=oj: a.copy(
                    out=bc[:, oj * 512:(oj + 1) * 512], in_=pb[3][:]),
                    waits=[("pe", mm)], inc="act"))
        bias_done = evs[-1]

        prev = bias_done
        for oj in range(2 if stage >= 0 else 0):
            mm = sc.op("pe", lambda t, oj=oj: t.matmul(
                pb[3][0:4, :], ldT_sb[:], da_sb[:, oj * 512:(oj + 1) * 512],
                start=True, stop=True),
                waits=[("act", prev)], inc="pe")
            prev = sc.op("act", lambda a, oj=oj: a.copy(
                out=a1_sb[:, oj * 512:(oj + 1) * 512], in_=pb[3][0:4, :]),
                waits=[("pe", mm)], inc="act")
        for oj in range(2 if stage >= 1 else 0):
            mm = sc.op("pe", lambda t, oj=oj: t.matmul(
                pb[3][0:50, :], luT_sb[:], a1_sb[:, oj * 512:(oj + 1) * 512],
                start=True, stop=True),
                waits=[("act", prev)], inc="pe")
            prev = sc.op("act", lambda a, oj=oj: a.copy(
                out=a2_sb[:, oj * 512:(oj + 1) * 512], in_=pb[3][0:50, :]),
                waits=[("pe", mm)], inc="act")
        wl_stage = sb("wl_stage", [128, D], BF16)
        wdone = prev
        dv = None
        for j in range(8 if stage >= 2 else 0):
            lv = sc.op("pool", lambda g, j=j: g.dma_start(
                out=wl_stage[:], in_=wl_d[j * 128:(j + 1) * 128, :]),
                waits=[("dve", sc.c("dve"))], inc="lst", dma=True)
            for oj in range(2):
                mm = sc.op("pe", lambda t, j=j, oj=oj: t.matmul(
                    pb[2][:], a2_sb[:, j * 128:(j + 1) * 128],
                    uaT_sb[:, oj * 512:(oj + 1) * 512],
                    start=True, stop=True),
                    waits=[("act", wdone)] + ([("dve", dv)] if dv else []), inc="pe")
                dv = sc.op("dve", lambda v, j=j, oj=oj: v.tensor_add(
                    out=weff[:, j * D + oj * 512: j * D + (oj + 1) * 512],
                    in0=pb[2][:], in1=wl_stage[:, oj * 512:(oj + 1) * 512]),
                    waits=[("pe", mm), ("lst", lv)], inc="dve")
        weff_done = sc.c("dve")

        NT1 = TPC // 128
        def ln_tile(tt, src_pre, out_dram, fm_dram, fm_tile, extra_store=None):
            v2 = sc.op("dve", lambda v: v.reduce_sum(out=mean[:, 0:1], in_=src_pre, axis=AXL.X),
                       inc="dve")
            v3 = sc.op("dve", lambda v: v.tensor_scalar_mul(
                out=nmean[:, 0:1], in0=mean[:, 0:1], scalar1=-1.0 / D),
                waits=[("dve", v2)], inc="dve")
            a1 = sc.op("act", lambda a: a.activation(
                out=xout[:], in_=src_pre, func=AF.Square, bias=nmean[:, 0:1], scale=1.0,
                accum_out=var[:, 0:1]),
                waits=[("dve", v3), ("st", sc.c("st")), ("st1", sc.c("st1"))], inc="act")
            v4 = sc.op("dve", lambda v: v.tensor_scalar(
                out=var[:, 0:1], in0=var[:, 0:1], scalar1=1.0 / D, scalar2=1e-5,
                op0=ALU.mult, op1=ALU.add),
                waits=[("act", a1)], inc="dve")
            a2 = sc.op("act", lambda a: a.activation(
                out=rstd[:, 0:1], in_=var[:, 0:1], func=AF.Sqrt),
                waits=[("dve", v4)], inc="act")
            v5 = sc.op("dve", lambda v: v.reciprocal(out=rstd[:, 0:1], in_=rstd[:, 0:1]),
                       waits=[("act", a2)], inc="dve")
            v6 = sc.op("dve", lambda v: v.tensor_scalar(
                out=xout[:], in0=src_pre, scalar1=nmean[:, 0:1], scalar2=rstd[:, 0:1],
                op0=ALU.add, op1=ALU.mult),
                waits=[("dve", v5)], inc="dve")
            stsem = "st1" if fm_dram is not None else "st"
            if fm_dram is not None:
                v7 = sc.op("dve", lambda v: v.tensor_copy(out=xout_bf[:], in_=xout[:]),
                           inc="dve")
                for sj in range(8):
                    tv = sc.op("pe", lambda t, sj=sj: t.transpose(
                        ptr[:], xout_bf[:, sj * 128:(sj + 1) * 128], ident[:]),
                        waits=[("dve", sc.c("dve")), ("pool", init_done)], inc="pe")
                    sc.op("dve", lambda v, sj=sj: v.tensor_copy(
                        out=fm_tile[:, sj * 128:(sj + 1) * 128], in_=ptr[:]),
                        waits=[("pe", tv)], inc="dve")
                vlast = sc.c("dve")
                sc.op("sync", lambda e, tt=tt: e.dma_start(
                    out=fm_dram[:, tt * 128:(tt + 1) * 128].rearrange("(j p) t -> p j t", p=128),
                    in_=fm_tile[:].rearrange("p (j t) -> p j t", j=8)),
                    waits=[("dve", vlast)], inc=stsem, dma=True)
            else:
                vlast = v6
            sc.op("sync", lambda e, tt=tt: e.dma_start(
                out=out_dram[tt * 128:(tt + 1) * 128, :], in_=xout[:]),
                waits=[("dve", vlast)], inc=stsem, dma=True)

        for tt in range(NT1 if stage >= 3 else 0):
            xv = sc.op("sync", lambda e, tt=tt: e.dma_start(
                out=xres[:], in_=x_sl[tt * 128:(tt + 1) * 128, :]),
                waits=([("dve", sc.c("dve"))] if tt else []), inc="ld2", dma=True)
            ccw = [("cc", dram["cc_wait"])] if fused else []
            yv = sc.op("sync", lambda e, tt=tt: e.dma_start(
                out=yfm_t[:].rearrange("p (j t) -> p j t", j=NCJ),
                in_=yfm[:, tt * 128:(tt + 1) * 128].rearrange("(j p) t -> p j t", p=128)),
                waits=([("pe", sc.c("pe"))] if tt else []) + ccw, inc="ld2", dma=True)
            mms = []
            for oj in range(2):
                for j in range(NCJ):
                    w = []
                    if j == 0:
                        w = [("ld", ld_w), ("ld2", max(xv, yv))]
                        if tt > 0:
                            w.append(("dve", sc.c("dve")))
                    mm = sc.op("pe", lambda t, oj=oj, j=j: t.matmul(
                        pb[oj][:], yfm_t[:, j * 128:(j + 1) * 128],
                        wp_sb[:, j * D + oj * 512: j * D + (oj + 1) * 512],
                        start=(j == 0), stop=(j == NCJ - 1)),
                        waits=w, inc="pe" if j == NCJ - 1 else None)
                mms.append(mm)
            for oj in range(2):
                sc.op("dve", lambda v, oj=oj: v.tensor_add(
                    out=pre[:, oj * 512:(oj + 1) * 512], in0=pb[oj][:],
                    in1=bp_bc[:, oj * 512:(oj + 1) * 512]),
                    waits=[("pe", mms[oj]), ("act", bias_done)], inc="dve")
            sc.op("dve", lambda v: v.tensor_add(out=pre[:], in0=pre[:], in1=xres[:]),
                  waits=[("ld2", max(xv, yv))], inc="dve")
            ln_tile(tt, pre[:], x3_dram, x3fm_dram, x3fm_t)
        d1_st = sc.c("st1")

        for tt in range(NT1 if stage >= 4 else 0):
            xv = sc.op("sync", lambda e, tt=tt: e.dma_start(
                out=xres[:], in_=x3_dram[tt * 128:(tt + 1) * 128, :]),
                waits=[("st1", d1_st), ("dve", sc.c("dve"))], inc="ld2", dma=True)
            yv = sc.op("sync", lambda e, tt=tt: e.dma_start(
                out=yfm_t[:, 0:1024].rearrange("p (j t) -> p j t", j=8),
                in_=x3fm_dram[:, tt * 128:(tt + 1) * 128].rearrange("(j p) t -> p j t", p=128)),
                waits=[("st1", d1_st), ("pe", sc.c("pe"))], inc="ld2", dma=True)
            mms = []
            for oj in range(2):
                for j in range(8):
                    w = []
                    if j == 0:
                        w = [("dve", weff_done), ("ld2", max(xv, yv))]
                        if tt > 0:
                            w.append(("dve", sc.c("dve")))
                    mm = sc.op("pe", lambda t, oj=oj, j=j: t.matmul(
                        pb[oj][:], yfm_t[:, j * 128:(j + 1) * 128],
                        weff[:, j * D + oj * 512: j * D + (oj + 1) * 512],
                        start=(j == 0), stop=(j == 7)),
                        waits=w, inc="pe" if j == 7 else None)
                mms.append(mm)
            for oj in range(2):
                sc.op("dve", lambda v, oj=oj: v.tensor_add(
                    out=pre[:, oj * 512:(oj + 1) * 512], in0=pb[oj][:],
                    in1=bl_bc[:, oj * 512:(oj + 1) * 512]),
                    waits=[("pe", mms[oj]), ("act", bias_done)], inc="dve")
            sc.op("dve", lambda v: v.tensor_add(out=pre[:], in0=pre[:], in1=xres[:]),
                  waits=[("ld2", max(xv, yv))], inc="dve")
            ln_tile(tt, pre[:], x4o, None, None)
        sc.op("sync", lambda e: None, waits=[("st", sc.c("st"))])
        sc.flush(block)


HEAD_OF_CORE = [(c // 4, 4 * (c % 4)) for c in range(8)]


def kernel_unfused(**inp):
    x = np.asarray(inp["L0_attn_x"])
    Wq = np.asarray(inp["L0_attn_Wq"])
    Wk = np.asarray(inp["L0_attn_Wk"])
    Wv = np.asarray(inp["L0_attn_Wv"])
    bq = np.asarray(inp["L0_attn_bq"])
    bk = np.asarray(inp["L0_attn_bk"])
    bv = np.asarray(inp["L0_attn_bv"])

    nc = build_A()
    in_maps = []
    for c in range(8):
        b, h0 = HEAD_OF_CORE[c]
        sl = slice(h0 * HD, (h0 + HPC) * HD)
        in_maps.append({
            "xfm": np.ascontiguousarray(x[b].T),
            "wqT": np.ascontiguousarray(Wq[sl, :].T),
            "wkT": np.ascontiguousarray(Wk[sl, :].T),
            "wvT": np.ascontiguousarray(Wv[sl, :].T),
            "bqkv": np.ascontiguousarray(np.stack([bq[sl], bk[sl], bv[sl]])),
        })
    res = run_bass_kernel_spmd(nc, in_maps, core_ids=list(range(8)))
    y_fm = [np.concatenate([np.asarray(res.results[4 * b_ + i]["yfm"]) for i in range(4)], axis=0)
            for b_ in range(B)]

    nc = build_B(debug=False)
    in_maps = []
    for c in range(8):
        b, q = c // 4, c % 4
        ts = slice(q * TPC, (q + 1) * TPC)
        in_maps.append({
            "yfm_sl": np.ascontiguousarray(y_fm[b][:, ts]),
            "x_sl": np.ascontiguousarray(x[b][ts, :]),
            "woT": np.ascontiguousarray(np.asarray(inp["L0_attn_Wo"]).T),
            "bo": np.ascontiguousarray(np.asarray(inp["L0_attn_bo"])[None, :]),
            "wgT": np.ascontiguousarray(np.asarray(inp["L0_ffn_w_gate"]).T),
            "wuT": np.ascontiguousarray(np.asarray(inp["L0_ffn_w_up"]).T),
            "wdT": np.ascontiguousarray(np.asarray(inp["L0_ffn_w_down"]).T),
        })
    res = run_bass_kernel_spmd(nc, in_maps, core_ids=list(range(8)))
    x2 = np.zeros((B, T, D), np.float32)
    for c in range(8):
        b, q = c // 4, c % 4
        ts = slice(q * TPC, (q + 1) * TPC)
        x2[b, ts] = np.asarray(res.results[c]["x2o"])

    W_attn = np.asarray(inp["L1_attn_W_attn"])
    b_attn = np.asarray(inp["L1_attn_b_attn"])
    gain = np.asarray(inp["L1_attn_score_gain"]).reshape(1, 1)
    sign = 1.0 if float(gain[0, 0]) >= 0 else -1.0
    nc = build_C(sign=sign)
    in_maps = []
    for c in range(8):
        b, h0 = HEAD_OF_CORE[c]
        cs = slice(h0 * HD, (h0 + HPC) * HD)
        in_maps.append({
            "x2fm": np.ascontiguousarray(x2[b].T),
            "wqC": np.ascontiguousarray(W_attn[:, cs]),
            "wkC": np.ascontiguousarray(W_attn[:, 1024 + h0 * HD: 1024 + (h0 + HPC) * HD]),
            "wvC": np.ascontiguousarray(W_attn[:, 2048 + h0 * HD: 2048 + (h0 + HPC) * HD]),
            "bqkvC": np.ascontiguousarray(np.stack([
                b_attn[cs], b_attn[1024 + h0 * HD: 1024 + (h0 + HPC) * HD],
                b_attn[2048 + h0 * HD: 2048 + (h0 + HPC) * HD]])),
            "gain": np.ascontiguousarray(gain.astype(np.float32)),
        })
    res = run_bass_kernel_spmd(nc, in_maps, core_ids=list(range(8)))
    y1_fm = [np.concatenate([np.asarray(res.results[4 * b_ + i]["yfm"]) for i in range(4)], axis=0)
             for b_ in range(B)]

    nc = build_D()
    in_maps = []
    for c in range(8):
        b, q = c // 4, c % 4
        ts = slice(q * TPC, (q + 1) * TPC)
        in_maps.append({
            "y1fm_sl": np.ascontiguousarray(y1_fm[b][:, ts]),
            "x2_sl": np.ascontiguousarray(x2[b][ts, :]),
            "wproj": np.ascontiguousarray(np.asarray(inp["L1_attn_W_proj"])),
            "bproj": np.ascontiguousarray(np.asarray(inp["L1_attn_b_proj"])[None, :]),
            "wLT": np.ascontiguousarray(np.asarray(inp["L1_ffn_W"]).T),
            "bL": np.ascontiguousarray(np.asarray(inp["L1_ffn_b"])[None, :]),
            "ldT": np.ascontiguousarray(np.asarray(inp["L1_ffn_lora_down"]).T),
            "da": np.ascontiguousarray(np.asarray(inp["L1_ffn_down_aux"])),
            "luT": np.ascontiguousarray(np.asarray(inp["L1_ffn_lora_up"]).T),
            "uaT": np.ascontiguousarray(np.asarray(inp["L1_ffn_up_aux"]).T),
        })
    res = run_bass_kernel_spmd(nc, in_maps, core_ids=list(range(8)))
    out = np.zeros((B, T, D), np.float32)
    for c in range(8):
        b, q = c // 4, c % 4
        ts = slice(q * TPC, (q + 1) * TPC)
        out[b, ts] = np.asarray(res.results[c]["x4o"])
    return out


def build_full(sign=1.0):
    nc = bass.Bass("TRN2", target_bir_lowering=False)
    dram = {}
    dram["xfm"] = nc.dram_tensor("xfm", [D, T], F32, kind="ExternalInput").ap()
    for w in ("q", "k", "v"):
        dram[f"w{w}T"] = nc.dram_tensor(f"w{w}T", [D, HPC * HD], F32, kind="ExternalInput").ap()
    dram["bqkv"] = nc.dram_tensor("bqkv", [3, HPC * HD], F32, kind="ExternalInput").ap()
    dram["x_sl"] = nc.dram_tensor("x_sl", [TPC, D], F32, kind="ExternalInput").ap()
    dram["wo2T"] = nc.dram_tensor("wo2T", [2 * D, D], F32, kind="ExternalInput").ap()
    dram["bo"] = nc.dram_tensor("bo", [1, D], F32, kind="ExternalInput").ap()
    dram["wgT"] = nc.dram_tensor("wgT", [D, 3072], I32, kind="ExternalInput").ap()
    dram["wuT"] = nc.dram_tensor("wuT", [D, 3072], I32, kind="ExternalInput").ap()
    dram["wdT"] = nc.dram_tensor("wdT", [3072, D], I32, kind="ExternalInput").ap()
    for w in ("q", "k", "v"):
        dram[f"w{w}C"] = nc.dram_tensor(f"w{w}C", [D, HPC * HD], F32, kind="ExternalInput").ap()
    dram["bqkvC"] = nc.dram_tensor("bqkvC", [3, HPC * HD], F32, kind="ExternalInput").ap()
    dram["gain"] = nc.dram_tensor("gain", [1, 1], F32, kind="ExternalInput").ap()
    dram["wp2"] = nc.dram_tensor("wp2", [2 * D, D], F32, kind="ExternalInput").ap()
    dram["bproj"] = nc.dram_tensor("bproj", [1, D], F32, kind="ExternalInput").ap()
    dram["wLT"] = nc.dram_tensor("wLT", [D, D], F32, kind="ExternalInput").ap()
    dram["bL"] = nc.dram_tensor("bL", [1, D], F32, kind="ExternalInput").ap()
    dram["ldT"] = nc.dram_tensor("ldT", [100, 4], F32, kind="ExternalInput").ap()
    dram["da"] = nc.dram_tensor("da", [100, D], F32, kind="ExternalInput").ap()
    dram["luT"] = nc.dram_tensor("luT", [4, 50], F32, kind="ExternalInput").ap()
    dram["uaT"] = nc.dram_tensor("uaT", [50, D], F32, kind="ExternalInput").ap()
    dram["x4o"] = nc.dram_tensor("x4o", [TPC, D], F32, kind="ExternalOutput").ap()
    for n, sh, dt in (("x1_scr", [TPC, D], F32), ("x1fm_scr", [D, TPC], BF16),
                      ("x2_scr", [TPC, D], F32), ("x3_scr", [TPC, D], F32),
                      ("x3fm_scr", [D, TPC], BF16),
                      ("a2a1_in", [2048, TPC], BF16), ("a2a1_out", [2048, TPC], BF16),
                      ("ag2_in", [D, TPC], BF16), ("ag2_out", [4 * D, TPC], BF16),
                      ("a2a3_in", [2048, TPC], BF16), ("a2a3_out", [2048, TPC], BF16)):
        dram[n] = nc.dram_tensor(n, sh, dt).ap()
    es0 = ExitStack()
    with es0:
        sc = Sched(nc, es0)
        block = es0.enter_context(nc.Block())
        phase_A(nc, sc, block, dram, fused=True)
        sc.barrier()
        dram["cc_wait"] = 1
        phase_B(nc, sc, block, dram, fused=True)
        sc.barrier()
        dram["cc_wait"] = 2
        phase_C(nc, sc, block, dram, sign, fused=True)
        sc.barrier()
        dram["cc_wait"] = 3
        phase_D(nc, sc, block, dram, fused=True)
    return nc


def kernel(**inp):
    inp = {k: np.asarray(v) for k, v in inp.items()}
    x = inp["L0_attn_x"]
    gain = inp["L1_attn_score_gain"].reshape(1, 1).astype(np.float32)
    sign = 1.0 if float(gain[0, 0]) >= 0 else -1.0
    W_attn = inp["L1_attn_W_attn"]
    b_attn = inp["L1_attn_b_attn"]
    woT = np.ascontiguousarray(inp["L0_attn_Wo"].T)
    wproj = np.ascontiguousarray(inp["L1_attn_W_proj"])
    zeros = np.zeros((D, D), np.float32)
    nc = build_full(sign=sign)
    in_maps = []
    for c in range(8):
        b, h0 = HEAD_OF_CORE[c]
        q = c % 4
        sl = slice(h0 * HD, (h0 + HPC) * HD)
        ts = slice(q * TPC, (q + 1) * TPC)
        wo2 = np.concatenate([woT, zeros], 0) if b == 0 else np.concatenate([zeros, woT], 0)
        wp2 = np.concatenate([wproj, zeros], 0) if b == 0 else np.concatenate([zeros, wproj], 0)
        in_maps.append({
            "xfm": np.ascontiguousarray(x[b].T),
            "wqT": np.ascontiguousarray(inp["L0_attn_Wq"][sl, :].T),
            "wkT": np.ascontiguousarray(inp["L0_attn_Wk"][sl, :].T),
            "wvT": np.ascontiguousarray(inp["L0_attn_Wv"][sl, :].T),
            "bqkv": np.ascontiguousarray(np.stack([
                inp["L0_attn_bq"][sl], inp["L0_attn_bk"][sl], inp["L0_attn_bv"][sl]])),
            "x_sl": np.ascontiguousarray(x[b][ts, :]),
            "wo2T": np.ascontiguousarray(wo2),
            "bo": np.ascontiguousarray(inp["L0_attn_bo"][None, :]),
            "wgT": np.ascontiguousarray(inp["L0_ffn_w_gate"].T),
            "wuT": np.ascontiguousarray(inp["L0_ffn_w_up"].T),
            "wdT": np.ascontiguousarray(inp["L0_ffn_w_down"].T),
            "wqC": np.ascontiguousarray(W_attn[:, sl]),
            "wkC": np.ascontiguousarray(W_attn[:, 1024 + h0 * HD: 1024 + (h0 + HPC) * HD]),
            "wvC": np.ascontiguousarray(W_attn[:, 2048 + h0 * HD: 2048 + (h0 + HPC) * HD]),
            "bqkvC": np.ascontiguousarray(np.stack([
                b_attn[sl], b_attn[1024 + h0 * HD: 1024 + (h0 + HPC) * HD],
                b_attn[2048 + h0 * HD: 2048 + (h0 + HPC) * HD]])),
            "gain": np.ascontiguousarray(gain),
            "wp2": np.ascontiguousarray(wp2),
            "bproj": np.ascontiguousarray(inp["L1_attn_b_proj"][None, :]),
            "wLT": np.ascontiguousarray(inp["L1_ffn_W"].T),
            "bL": np.ascontiguousarray(inp["L1_ffn_b"][None, :]),
            "ldT": np.ascontiguousarray(inp["L1_ffn_lora_down"].T),
            "da": np.ascontiguousarray(inp["L1_ffn_down_aux"]),
            "luT": np.ascontiguousarray(inp["L1_ffn_lora_up"].T),
            "uaT": np.ascontiguousarray(inp["L1_ffn_up_aux"].T),
        })
    res = run_bass_kernel_spmd(nc, in_maps, core_ids=list(range(8)))
    out = np.zeros((B, T, D), np.float32)
    for c in range(8):
        b, _ = HEAD_OF_CORE[c]
        q = c % 4
        out[b, q * TPC:(q + 1) * TPC] = np.asarray(res.results[c]["x4o"])
    return out
